# version 03
# baseline (speedup 1.0000x reference)
import numpy as np
from contextlib import ExitStack

import concourse.bass as bass
import concourse.mybir as mybir

F32 = mybir.dt.float32
BF16 = mybir.dt.bfloat16
AF = mybir.ActivationFunctionType
ALU = mybir.AluOpType
AX = mybir.AxisListType

ENGINES = ("sync", "vector", "scalar", "tensor", "gpsimd")

D = 1024
NDC = D // 128
TC = 512


class Prog:

    def __init__(self, nc, stack, T=4096):
        self.nc = nc
        self.stack = stack
        self.T = T
        self.NTC = T // TC
        self.ops = {e: [] for e in ENGINES}
        self._sems = {}
        self.counts = {}
        self.meta = {}

    def sem(self, name):
        if name not in self._sems:
            self._sems[name] = self.stack.enter_context(self.nc.semaphore(name))
            self.counts[name] = 0
        return name

    def emit(self, eng, fn, waits=(), inc=None, incn=1):
        at = None
        if inc is not None:
            self.sem(inc)
            self.counts[inc] += incn
            at = self.counts[inc]
        ws = []
        for s, v in waits:
            self.sem(s)
            if v and v > 0:
                ws.append((s, v))
        self.ops[eng].append([fn, tuple(ws), inc, incn, at])
        return at

    def dma(self, eng, out, in_, waits=(), inc=None, **kw):
        return self.emit(eng, lambda e: e.dma_start(out=out, in_=in_, **kw),
                         waits=waits, inc=inc, incn=16)

    def now(self, sem):
        self.sem(sem)
        return self.counts[sem]

    def barrier_counts(self):
        return [(s, c) for s, c in self.counts.items() if c > 0]

    def barrier(self):
        counts = self.barrier_counts()
        for eng in ENGINES:
            self.emit(eng, lambda e: e.nop(), waits=counts)

    def finalize(self, block, self_edges=False, edge_engines=None):
        if edge_engines is None:
            edge_engines = ENGINES if self_edges else ()
        edge_engines = tuple(set(edge_engines) | {"vector"})
        for e in edge_engines:
            self.sem(f"_se_{e}")
        sems = self._sems
        for eng_name in ENGINES:
            ops = self.ops[eng_name]

            def body(engine, ops=ops, eng_name=eng_name):
                se = eng_name in edge_engines
                last = None
                secount = 0
                for fn, waits, inc, incn, at in ops:
                    if se and last is not None:
                        engine.wait_ge(sems[last[0]], last[1])
                    for s, v in waits:
                        engine.wait_ge(sems[s], v)
                    ins = fn(engine)
                    if inc is not None:
                        ins.then_inc(sems[inc], incn)
                        last = (inc, at)
                    elif se:
                        secount += 1
                        ins.then_inc(sems[f"_se_{eng_name}"], 1)
                        last = (f"_se_{eng_name}", secount)
            getattr(block, eng_name)(body)


class Bufs:
    def __init__(self, nc, stack):
        en = stack.enter_context
        sb = lambda name, shape, dt: en(nc.sbuf_tensor(name, shape, dt))
        ps = lambda name, shape, dt: en(nc.psum_tensor(name, shape, dt))
        self.BIGA = sb("BIGA", [128, 32768], BF16)
        self.BIGB = sb("BIGB", [128, 33280], BF16)
        self.XC = sb("XC", [128, 4096], F32)
        self.XB = sb("XB", [128, 4096], BF16)
        self.STB = sb("STB", [128, 4096], BF16)
        self.FM = sb("FM", [128, 2048], F32)
        self.WS = sb("WS", [128, 4096], BF16)
        self.S1 = sb("S1", [128, 1024], F32)
        self.S2 = sb("S2", [128, 1024], F32)
        self.WQK = sb("WQK", [128, 8192], BF16)
        self.CT = self.WS[:, 512:1024]
        self.ST = self.WS[:, 1024:1536]
        self.COLS = sb("COLS", [128, 256], F32)
        self.ROWS = sb("ROWS", [1, 1024], F32)
        self.ONEB = sb("ONEB", [128, 8], BF16)
        self.ONEF = sb("ONEF", [128, 8], F32)
        self.EPSC = sb("EPSC", [128, 4], F32)
        self.IDN = sb("IDN", [128, 128], BF16)
        self.IDF = sb("IDF", [128, 128], F32)
        self.TRIL = sb("TRIL", [128, 128], BF16)
        self.pb = [ps(f"pb{i}", [128, 512], F32) for i in range(5)]
        self.pbc = [ps(f"pbc{i}", [128, 512], F32) for i in range(2)]
        self.PTR = ps("PTR", [128, 1024], BF16)
        self.ptr = [self.PTR[:, i * 128:(i + 1) * 128] for i in range(8)]


def emit_consts(p: Prog, b: Bufs, aps):
    p.emit("vector", lambda e: e.memset(b.ONEB[:], 1.0))
    p.emit("vector", lambda e: e.memset(b.ONEF[:], 1.0))
    p.emit("vector", lambda e: e.memset(b.EPSC[:, 0:1], 1e-5))
    p.emit("vector", lambda e: e.memset(b.EPSC[:, 1:2], 1e-6))
    p.emit("vector", lambda e: e.memset(b.EPSC[:, 2:3], 1e-9))
    p.emit("vector", lambda e: e.memset(b.EPSC[:, 3:4], 1e-8), inc="sV")
    p.dma("sync", b.IDN[:], aps["ident"], inc="sLC")
    p.dma("sync", b.IDF[:], aps["identf"], inc="sLC")
    p.dma("sync", b.TRIL[:], aps["tril"], inc="sLC")
    p.meta["consts"] = [("sV", p.now("sV")), ("sLC", p.now("sLC"))]


def emit_ln(p: Prog, b: Bufs, eps_slice, gate=(), dst=None, lo_dst=None):
    pbcw = p.meta.get("pbc_free", [])
    for k in range(NDC):
        sl = slice(k * TC, (k + 1) * TC)
        vW = p.emit("vector", lambda e, sl=sl: e.tensor_copy(b.XB[:, sl], b.XC[:, sl]),
                    waits=gate if k == 0 else (), inc="sV" if k == NDC - 1 else None)
        aW = p.emit("scalar", lambda e, sl=sl: e.activation(b.STB[:, sl], b.XC[:, sl], AF.Square),
                    waits=gate if k == 0 else (), inc="sA" if k == NDC - 1 else None)
    for k in range(NDC):
        sl = slice(k * TC, (k + 1) * TC)
        w_ = ([("sV", vW)] + pbcw) if k == 0 else ()
        p.emit("tensor", lambda e, sl=sl, k=k: e.matmul(
            b.pbc[0][0:1, :], b.ONEB[:, 0:1], b.XB[:, sl], start=(k == 0), stop=(k == NDC - 1)),
            waits=w_)
    for k in range(NDC):
        sl = slice(k * TC, (k + 1) * TC)
        w_ = [("sA", aW)] if k == 0 else ()
        mmW = p.emit("tensor", lambda e, sl=sl, k=k: e.matmul(
            b.pbc[1][0:1, :], b.ONEB[:, 0:1], b.STB[:, sl], start=(k == 0), stop=(k == NDC - 1)),
            waits=w_, inc="sP" if k == NDC - 1 else None)
    R = b.ROWS
    Sx = b.S1[0:1, 0:512]
    p.emit("vector", lambda e: e.tensor_scalar_mul(R[:, 0:512], b.pbc[0][0:1, :], 1.0 / D),
           waits=[("sP", mmW)])
    p.emit("vector", lambda e: e.tensor_scalar_mul(R[:, 512:1024], b.pbc[1][0:1, :], 1.0 / D))
    p.emit("vector", lambda e: e.tensor_mul(Sx, R[:, 0:512], R[:, 0:512]))
    rv = p.emit("vector", lambda e: e.tensor_sub(Sx, R[:, 512:1024], Sx), inc="sV")
    sq = p.emit("scalar", lambda e: e.activation(R[:, 512:1024], Sx, AF.Sqrt,
                                                 bias=eps_slice), waits=[("sV", rv)], inc="sA")
    rc = p.emit("vector", lambda e: e.reciprocal(Sx, R[:, 512:1024]),
                waits=[("sA", sq)], inc="sV")
    p.emit("tensor", lambda e: e.matmul(b.pbc[0][:], b.ONEF[0:1, 0:1].broadcast_to([1, 128]),
                                        R[:, 0:512], start=True, stop=True),
           waits=[("sV", rc)])
    bcW = p.emit("tensor", lambda e: e.matmul(b.pbc[1][:], b.ONEF[0:1, 0:1].broadcast_to([1, 128]),
                                              b.S1[0:1, 0:512], start=True, stop=True), inc="sP")
    for k in range(NDC):
        sl = slice(k * TC, (k + 1) * TC)
        out_ap = b.XB[:, sl] if dst is None else dst[k]
        p.emit("vector", lambda e, sl=sl: e.tensor_sub(b.S1[:, 0:512], b.XC[:, sl], b.pbc[0][:]),
               waits=[("sP", bcW)] if k == 0 else ())
        w = p.emit("vector", lambda e, out_ap=out_ap: e.tensor_mul(out_ap, b.S1[:, 0:512], b.pbc[1][:]),
                   inc="sV")
        if lo_dst is not None:
            p.emit("vector", lambda e: e.tensor_mul(b.S1[:, 512:1024], b.S1[:, 0:512], b.pbc[1][:]))
            w = p.emit("vector", lambda e, out_ap=out_ap, k=k: e.tensor_sub(
                lo_dst[k], b.S1[:, 512:1024], out_ap), inc="sV")
    p.meta["ln_done"] = [("sV", w)]
    p.meta["pbc_free"] = [("sV", w)]
    return p.meta["ln_done"]


def load_xc(p: Prog, b: Bufs, xin, tc, gate=()):
    for k in range(NDC):
        ldx = p.dma("sync", b.XC[:, k * TC:(k + 1) * TC],
                    xin[k * 128:(k + 1) * 128, tc * TC:(tc + 1) * TC],
                    waits=gate if k == 0 else (), inc="sLX")
    return [("sLX", ldx)]


def emit_binarize(p: Prog, b: Bufs, dst, dst_cols_per_k, wT_ap, w_ap, K, M, alpha_col0, gate=()):
    nK = K // 128
    nM = M // 128
    colchunk = 2048
    ncc = (M + colchunk - 1) // colchunk
    slot_ready = [None, None]
    si = 0
    for k in range(nK):
        for c in range(ncc):
            cs = slice(c * colchunk, min(M, (c + 1) * colchunk))
            ncols = cs.stop - cs.start
            xsl = slice(si * 2048, si * 2048 + ncols)
            waits = list(gate) if (k == 0 and c == 0) else []
            if slot_ready[si] is not None:
                waits.append(("sA", slot_ready[si]))
            ldw = p.dma("sync", b.XC[:, xsl], wT_ap[k * 128:(k + 1) * 128, cs],
                        waits=waits, inc=f"bsg{si}")
            dsl = slice(k * dst_cols_per_k + cs.start, k * dst_cols_per_k + cs.stop)
            slot_ready[si] = p.emit(
                "scalar", lambda e, dsl=dsl, xsl=xsl: e.activation(dst[:, dsl], b.XC[:, xsl], AF.Sign),
                waits=[(f"bsg{si}", ldw)], inc="sA")
            si ^= 1
    fslot = [None, None]
    fi = 0
    nc2 = max(1, K // 1024)
    for m in range(nM):
        for c2 in range(nc2):
            fsl = slice(fi * 1024, (fi + 1) * 1024)
            waits = list(gate) if (m == 0 and c2 == 0) else []
            if fslot[fi] is not None:
                waits.append(("sV", fslot[fi]))
            ldn = p.dma("sync", b.FM[:, fsl], w_ap[m * 128:(m + 1) * 128, c2 * 1024:(c2 + 1) * 1024],
                        waits=waits, inc=f"bna{fi}")
            col = slice(alpha_col0 + m, alpha_col0 + m + 1)
            if c2 == 0:
                fslot[fi] = p.emit("vector", lambda e, fsl=fsl, col=col: e.tensor_reduce(
                    b.COLS[:, col], b.FM[:, fsl], AX.X, ALU.add, apply_absolute_value=True),
                    waits=[(f"bna{fi}", ldn)], inc="sV")
            else:
                p.emit("vector", lambda e, fsl=fsl: e.tensor_reduce(
                    b.S2[:, 0:1], b.FM[:, fsl], AX.X, ALU.add, apply_absolute_value=True),
                    waits=[(f"bna{fi}", ldn)])
                fslot[fi] = p.emit("vector", lambda e, col=col: e.tensor_add(
                    b.COLS[:, col], b.COLS[:, col], b.S2[:, 0:1]), inc="sV")
            fi ^= 1
    w = p.emit("vector", lambda e: e.tensor_scalar_mul(
        b.COLS[:, alpha_col0:alpha_col0 + nM], b.COLS[:, alpha_col0:alpha_col0 + nM], 1.0 / K),
        inc="sV")
    return [("sV", w), ("sA", p.now("sA"))]


def emit_ffn_bin(p: Prog, b: Bufs, aps, gate=()):
    NTC = p.NTC
    g1 = emit_binarize(p, b, b.BIGA, 4096, aps["w1T"], aps["w1"], 1024, 4096, 0, gate=gate)
    g2 = emit_binarize(p, b, b.BIGB, 1024, aps["w2T"], aps["w2"], 4096, 1024, 64,
                       gate=[("sA", p.now("sA")), ("sV", p.now("sV"))])
    ld1 = p.dma("sync", b.COLS[:, 96:128], aps["b1c"], inc="sLB")
    ld2 = p.dma("sync", b.COLS[:, 88:96], aps["b2c"], inc="sLB")
    wgt = g1 + g2 + [("sLB", ld2)]

    bank_free = [None, None]
    stb_free = [None] * 8
    last_mm = None
    for tc in range(NTC):
        xg = load_xc(p, b, aps["xin"], tc,
                     gate=(wgt if tc == 0 else [("sV", p.now("sV")), ("sA", p.now("sA"))]))
        lng = xg + ([("sP", last_mm)] if last_mm is not None else [])
        lng += [(f"sHo{s}", p.now(f"sHo{s}")) for s in range(8)]
        xw = emit_ln(p, b, b.EPSC[0:1, 0:1], gate=lng)
        for m in range(32):
            bank = b.pb[m % 2]
            for k in range(NDC):
                w_ = []
                if k == 0:
                    w_ = list(xw)
                    if bank_free[m % 2] is not None:
                        w_.append(("sA", bank_free[m % 2]))
                last_mm = p.emit("tensor", lambda e, m=m, k=k, bank=bank: e.matmul(
                    bank[:], b.BIGA[:, k * 4096 + m * 128:k * 4096 + (m + 1) * 128],
                    b.XB[:, k * TC:(k + 1) * TC], start=(k == 0), stop=(k == NDC - 1)),
                    waits=w_, inc="sP" if k == NDC - 1 else None)
            sl = slice((m % 8) * TC, (m % 8 + 1) * TC)
            ew = [("sP", last_mm)]
            if stb_free[m % 8] is not None:
                ew.append((f"sHo{m % 8}", stb_free[m % 8]))
            ev = p.emit("scalar", lambda e, sl=sl, m=m, bank=bank: e.activation(
                b.STB[:, sl], bank[:], AF.Gelu,
                bias=b.COLS[:, 96 + m:97 + m], scale=b.COLS[:, m:m + 1]),
                waits=ew, inc="sA")
            bank_free[m % 2] = ev
            st = p.dma("gpsimd", aps["hd"][m * 128:(m + 1) * 128, tc * TC:(tc + 1) * TC],
                       b.STB[:, sl], waits=[("sA", ev)], inc=f"sHo{m % 8}")
            stb_free[m % 8] = st

    all_h = [(f"sHo{s}", p.now(f"sHo{s}")) for s in range(8)]
    fm_free = [None] * 4
    stbD_free = [None] * 8
    first = True
    for tc in range(NTC):
        xg = load_xc(p, b, aps["xin"], tc,
                     gate=[("sV", p.now("sV")), ("sA", p.now("sA"))])
        for grp in range(2):
            kcW = None
            for kc in range(32):
                sl = slice((kc % 8) * TC, (kc % 8 + 1) * TC)
                w_ = list(all_h) if (tc == 0 and grp == 0 and kc < 8) else []
                first = False
                if stbD_free[kc % 8] is not None:
                    w_.append(("sP", stbD_free[kc % 8]))
                ldh = p.dma("sync", b.STB[:, sl],
                            aps["hd"][kc * 128:(kc + 1) * 128, tc * TC:(tc + 1) * TC],
                            waits=w_, inc=f"sLH{kc % 8}")
                for j in range(4):
                    dm = grp * 4 + j
                    ww = []
                    if j == 0:
                        ww.append((f"sLH{kc % 8}", ldh))
                    if kc == 0 and j == 0:
                        if fm_free[0] is not None:
                            ww.append(("sV", p.meta.get("ffnD_evac", 0)))
                        else:
                            ww.append(("sA", p.now("sA")))
                    kcW = p.emit("tensor", lambda e, kc=kc, dm=dm, j=j, sl=sl: e.matmul(
                        b.pb[j][:], b.BIGB[:, kc * 1024 + dm * 128:kc * 1024 + (dm + 1) * 128],
                        b.STB[:, sl], start=(kc == 0), stop=(kc == 31)),
                        waits=ww, inc="sP" if j == 3 else None)
                stbD_free[kc % 8] = kcW
            for j in range(4):
                dm = grp * 4 + j
                fs = slice(j * TC, (j + 1) * TC)
                w_ = [("sP", kcW)] + xg
                if fm_free[j] is not None:
                    w_.append((f"sXo{j}", fm_free[j]))
                p.emit("vector", lambda e, dm=dm, j=j, fs=fs: e.tensor_scalar(
                    b.FM[:, fs], b.pb[j][:], b.COLS[:, 64 + dm:65 + dm],
                    b.COLS[:, 88 + dm:89 + dm], ALU.mult, ALU.add), waits=w_)
                ev = p.emit("vector", lambda e, dm=dm, fs=fs: e.tensor_add(
                    b.FM[:, fs], b.FM[:, fs], b.XC[:, dm * TC:(dm + 1) * TC]), inc="sV")
                p.meta["ffnD_evac"] = ev
                st = p.dma("gpsimd", aps["xout"][dm * 128:(dm + 1) * 128, tc * TC:(tc + 1) * TC],
                           b.FM[:, fs], waits=[("sV", ev)], inc=f"sXo{j}")
                fm_free[j] = st
    return [(f"sXo{j}", p.now(f"sXo{j}")) for j in range(4)]


def emit_moe(p: Prog, b: Bufs, aps, gate=()):
    NTC = p.NTC
    T = p.T
    ntsub = T // 128

    for tc in range(NTC):
        xg = load_xc(p, b, aps["xin"], tc,
                     gate=(list(gate) + p.meta.get("pbc_free", []) if tc == 0
                           else [("sV", p.now("sV")), ("sA", p.now("sA"))]))
        dst = [b.BIGA[:, k * T + tc * TC:k * T + (tc + 1) * TC] for k in range(NDC)]
        xw = emit_ln(p, b, b.EPSC[0:1, 0:1], gate=xg, dst=dst)

    ldg = p.dma("gpsimd", b.WS[:, 0:64].rearrange("p (k c) -> p k c", k=8),
                aps["gate_w"].rearrange("(k p) c -> p k c", p=128), inc="sLG")
    for ts in range(ntsub):
        w_ = (list(xw) + [("sLG", ldg)]) if ts == 0 else [("sV", p.meta.get("moe_top2", 0))]
        for k in range(NDC):
            mm = p.emit("tensor", lambda e, k=k, ts=ts: e.matmul(
                b.pb[4][:, 0:8], b.BIGA[:, k * T + ts * 128:k * T + (ts + 1) * 128],
                b.WS[:, k * 8:(k + 1) * 8], start=(k == 0), stop=(k == NDC - 1)),
                waits=w_ if k == 0 else (), inc="sP" if k == NDC - 1 else None)
        S = b.S1
        p.emit("vector", lambda e: e.reduce_max(S[:, 0:1], b.pb[4][:, 0:8], AX.X),
               waits=[("sP", mm)])
        p.emit("vector", lambda e: e.tensor_single_scalar(S[:, 8:16], b.pb[4][:, 0:8], S[:, 0:1], ALU.is_ge))
        p.emit("vector", lambda e: e.scalar_tensor_tensor(S[:, 16:24], S[:, 8:16], -1e30, b.pb[4][:, 0:8], ALU.mult, ALU.add))
        p.emit("vector", lambda e: e.reduce_max(S[:, 1:2], S[:, 16:24], AX.X))
        sv = p.emit("vector", lambda e: e.tensor_sub(S[:, 2:3], S[:, 0:1], S[:, 1:2]), inc="sV")
        sg = p.emit("scalar", lambda e: e.activation(S[:, 3:4], S[:, 2:3], AF.Sigmoid),
                    waits=[("sV", sv)], inc="sA")
        p.emit("vector", lambda e: e.tensor_scalar(S[:, 4:5], S[:, 3:4], -1.0, 1.0, ALU.mult, ALU.add),
               waits=[("sA", sg)])
        p.emit("vector", lambda e: e.tensor_single_scalar(S[:, 24:32], S[:, 16:24], S[:, 1:2], ALU.is_ge))
        p.emit("vector", lambda e: e.tensor_single_scalar(S[:, 8:16], S[:, 8:16], S[:, 3:4], ALU.mult))
        p.emit("vector", lambda e: e.tensor_single_scalar(S[:, 24:32], S[:, 24:32], S[:, 4:5], ALU.mult))
        tv = p.emit("vector", lambda e, ts=ts: e.tensor_add(b.S2[:, ts * 8:(ts + 1) * 8], S[:, 8:16], S[:, 24:32]),
                    inc="sV")
        p.meta["moe_top2"] = tv

    W1C = 16384
    fm_free = [None] * 4
    for e in range(8):
        wld = [("sP", p.now("sP"))] if e > 0 else list(gate)
        for k in range(NDC):
            p.dma("gpsimd", b.BIGB[:, k * 2048:(k + 1) * 2048],
                  aps["w1"][e, k * 128:(k + 1) * 128, :], waits=wld if k == 0 else (),
                  inc="sLE")
        for hm in range(16):
            l2 = p.dma("gpsimd", b.BIGB[:, W1C + hm * 1024:W1C + (hm + 1) * 1024],
                       aps["w2"][e, hm * 128:(hm + 1) * 128, :], inc="sLE")
        ewg = [("sLE", l2)]
        prev_acc = [(f"sAcc{j}", p.now(f"sAcc{j}")) for j in range(4)]
        first_store = True
        for tc in range(NTC):
            for hm in range(16):
                bank = b.pb[hm % 2]
                for k in range(NDC):
                    w_ = []
                    if k == 0:
                        if tc == 0 and hm == 0:
                            w_ += list(ewg)
                            if e == 0:
                                w_ += list(xw) + [("sV", p.meta["moe_top2"])]
                        if p.meta.get(f"moe_bank{hm % 2}") is not None:
                            w_.append(("sA", p.meta[f"moe_bank{hm % 2}"]))
                    mm = p.emit("tensor", lambda e_, hm=hm, k=k, tc=tc, bank=bank: e_.matmul(
                        bank[:], b.BIGB[:, k * 2048 + hm * 128:k * 2048 + (hm + 1) * 128],
                        b.BIGA[:, k * T + tc * TC:k * T + (tc + 1) * TC],
                        start=(k == 0), stop=(k == NDC - 1)),
                        waits=w_, inc="sP" if k == NDC - 1 else None)
                dstap = (b.STB[:, (hm % 8) * TC:(hm % 8 + 1) * TC] if hm < 8 else
                         b.XB[:, (hm - 8) * TC:(hm - 8 + 1) * TC])
                ev = p.emit("scalar", lambda e_, dstap=dstap, bank=bank: e_.activation(
                    dstap, bank[:], AF.Silu), waits=[("sP", mm)], inc="sA")
                p.meta[f"moe_bank{hm % 2}"] = ev
            hready = p.now("sA")
            for tsq in range(4):
                ts_global = tc * 4 + tsq
                for dn in range(2):
                    bank = b.pb[2 + dn]
                    for hm in range(16):
                        w_ = []
                        if hm == 0:
                            w_ = [("sA", hready)]
                            if p.meta.get(f"moe_ybank{dn}") is not None:
                                w_.append(("sV", p.meta[f"moe_ybank{dn}"]))
                        lhs = (b.STB[:, (hm % 8) * TC + tsq * 128:(hm % 8) * TC + tsq * 128 + 128]
                               if hm < 8 else
                               b.XB[:, (hm - 8) * TC + tsq * 128:(hm - 8) * TC + tsq * 128 + 128])
                        ymm = p.emit("tensor", lambda e_, lhs=lhs, hm=hm, dn=dn, bank=bank: e_.matmul(
                            bank[:], lhs,
                            b.BIGB[:, W1C + hm * 1024 + dn * 512:W1C + hm * 1024 + (dn + 1) * 512],
                            start=(hm == 0), stop=(hm == 15)),
                            waits=w_, inc="sP" if hm == 15 else None)
                    j = (tsq * 2 + dn) % 4
                    fs = slice(j * TC, (j + 1) * TC)
                    cidx = ts_global * 8 + e
                    w_ = [("sP", ymm)]
                    if fm_free[j] is not None:
                        w_.append((f"sAcc{j}", fm_free[j]))
                    yev = p.emit("vector", lambda e_, fs=fs, cidx=cidx, bank=bank: e_.tensor_single_scalar(
                        b.FM[:, fs], bank[:], b.S2[:, cidx:cidx + 1], ALU.mult),
                        waits=w_, inc="sV")
                    p.meta[f"moe_ybank{dn}"] = yev
                    stw = [("sV", yev)]
                    if first_store and e > 0:
                        stw += prev_acc
                        first_store = False
                    st = p.dma("gpsimd",
                               aps["accd"][ts_global * 128:(ts_global + 1) * 128, dn * 512:(dn + 1) * 512],
                               b.FM[:, fs], waits=stw,
                               inc=f"sAcc{j}", accum_op=(ALU.bypass if e == 0 else ALU.add))
                    fm_free[j] = st

    allacc = [(f"sAcc{j}", p.now(f"sAcc{j}")) for j in range(4)]
    engines_done = [("sV", p.now("sV")), ("sA", p.now("sA"))]
    s1_free = [None] * 8
    xcf_free = [None] * 8
    fmF_free = [None] * 4
    pbF_free = [None] * 4
    ti = 0
    for dc in range(NDC):
        for tt in range(T // 128):
            s1s = ti % 8
            xcs = ti % 8
            j = ti % 4
            w_ = (list(allacc) + list(engines_done)) if ti == 0 else []
            if s1_free[s1s] is not None:
                w_.append(("sP", s1_free[s1s]))
            lda = p.dma("sync", b.S1[:, s1s * 128:(s1s + 1) * 128],
                        aps["accd"][tt * 128:(tt + 1) * 128, dc * 128:(dc + 1) * 128],
                        waits=w_, inc=f"sFA{s1s}")
            w_ = []
            if xcf_free[xcs] is not None:
                w_.append(("sV", xcf_free[xcs]))
            ldx = p.dma("sync", b.XC[:, xcs * 128:(xcs + 1) * 128],
                        aps["xin"][dc * 128:(dc + 1) * 128, tt * 128:(tt + 1) * 128],
                        waits=w_, inc=f"sFX{xcs}")
            w_ = [(f"sFA{s1s}", lda)]
            if ti < 4:
                w_ += engines_done
            if pbF_free[j] is not None:
                w_.append(("sV", pbF_free[j]))
            tr = p.emit("tensor", lambda e_, s1s=s1s, j=j: e_.transpose(
                b.pb[j][:, 0:128], b.S1[:, s1s * 128:(s1s + 1) * 128], b.IDF[:]),
                waits=w_, inc="sP")
            s1_free[s1s] = tr
            w_ = [("sP", tr), (f"sFX{xcs}", ldx)]
            if fmF_free[j] is not None:
                w_.append((f"sFO{j}", fmF_free[j]))
            ad = p.emit("vector", lambda e_, j=j, xcs=xcs: e_.tensor_add(
                b.FM[:, j * 128:(j + 1) * 128], b.pb[j][:, 0:128],
                b.XC[:, xcs * 128:(xcs + 1) * 128]), waits=w_, inc="sV")
            pbF_free[j] = ad
            xcf_free[xcs] = ad
            st = p.dma("gpsimd", aps["xout"][dc * 128:(dc + 1) * 128, tt * 128:(tt + 1) * 128],
                       b.FM[:, j * 128:(j + 1) * 128], waits=[("sV", ad)], inc=f"sFO{j}")
            fmF_free[j] = st
            ti += 1
    return [(f"sFO{j}", p.now(f"sFO{j}")) for j in range(4)]


def emit_rms(p: Prog, b: Bufs, src, dst, wcol, eps_slice, gate=()):
    pbcw = p.meta.get("pbc_free", [])
    for k in range(NDC):
        sl = slice(k * TC, (k + 1) * TC)
        aW = p.emit("scalar", lambda e, sl=sl, k=k: e.activation(b.STB[:, sl], src(k), AF.Square),
                    waits=gate if k == 0 else (), inc="sA" if k == NDC - 1 else None)
    for k in range(NDC):
        sl = slice(k * TC, (k + 1) * TC)
        w_ = ([("sA", aW)] + pbcw) if k == 0 else ()
        mmW = p.emit("tensor", lambda e, sl=sl, k=k: e.matmul(
            b.pbc[0][0:1, :], b.ONEB[:, 0:1], b.STB[:, sl], start=(k == 0), stop=(k == NDC - 1)),
            waits=w_, inc="sP" if k == NDC - 1 else None)
    R = b.ROWS
    sv = p.emit("vector", lambda e: e.tensor_scalar_mul(R[:, 0:512], b.pbc[0][0:1, :], 1.0 / D),
                waits=[("sP", mmW)], inc="sV")
    sq = p.emit("scalar", lambda e: e.activation(R[:, 512:1024], R[:, 0:512], AF.Sqrt,
                                                 bias=eps_slice), waits=[("sV", sv)], inc="sA")
    rc = p.emit("vector", lambda e: e.reciprocal(R[:, 0:512], R[:, 512:1024]),
                waits=[("sA", sq)], inc="sV")
    bcW = p.emit("tensor", lambda e: e.matmul(b.pbc[0][:], b.ONEF[0:1, 0:1].broadcast_to([1, 128]),
                                              R[:, 0:512], start=True, stop=True),
                 waits=[("sV", rc)], inc="sP")
    for k in range(NDC):
        w = p.emit("vector", lambda e, k=k: e.scalar_tensor_tensor(
            dst(k), src(k), wcol(k), b.pbc[0][:], ALU.mult, ALU.mult),
            waits=[("sP", bcW)] if k == 0 else (), inc="sV")
    p.meta["pbc_free"] = [("sV", w)]
    return [("sV", w)]


def emit_attnb(p: Prog, b: Bufs, aps, gate=()):
    NTC = p.NTC
    T = p.T
    C = b.COLS
    ldc = p.dma("sync", C[:, 0:144], aps["cols"], waits=list(gate), inc="sLB")
    th = p.emit("scalar", lambda e: e.activation(C[:, 0:8], C[:, 0:8], AF.Tanh),
                waits=[("sLB", ldc)], inc="sA")

    for tc in range(NTC):
        xg = load_xc(p, b, aps["xin"], tc,
                     gate=(list(gate) + p.meta.get("pbc_free", []) if tc == 0
                           else [("sV", p.now("sV")), ("sA", p.now("sA"))]))
        dstb = [b.BIGB[:, k * T + tc * TC:k * T + (tc + 1) * TC] for k in range(NDC)]
        xw = emit_ln(p, b, b.EPSC[0:1, 0:1], gate=xg, dst=dstb)
        yw = emit_rms(p, b,
                      src=lambda k, tc=tc: b.BIGB[:, k * T + tc * TC:k * T + (tc + 1) * TC],
                      dst=lambda k, tc=tc: b.BIGA[:, k * T + tc * TC:k * T + (tc + 1) * TC],
                      wcol=lambda k: C[:, 48 + k:49 + k],
                      eps_slice=b.EPSC[0:1, 1:2], gate=xw)

    ugate = [("sA", th)] + yw
    for k in range(NDC):
        ys = lambda lo, hi, k=k: b.BIGA[:, k * T + lo:k * T + hi]
        bw = p.emit("vector", lambda e, k=k, ys=ys: e.tensor_single_scalar(
            b.STB[:, 0:T], ys(0, T), C[:, 8 + k:9 + k], ALU.mult),
            waits=ugate if k == 0 else (), inc="sV")
        sc = p.emit("vector", lambda e, k=k: e.tensor_tensor_scan(
            b.XB[:, 0:T], C[:, 0 + k:1 + k].broadcast_to([128, T]), b.STB[:, 0:T],
            0.0, ALU.mult, ALU.add), inc="sV")
        for tc in range(NTC):
            lo, hi = tc * TC, (tc + 1) * TC
            fs = slice((tc % 2) * TC, (tc % 2) * TC + TC)
            fw = []
            if p.meta.get(f"ab_fm{tc % 2}") is not None:
                fw.append((f"sUo{tc % 2}", p.meta[f"ab_fm{tc % 2}"]))
            if tc == 0:
                p.emit("vector", lambda e, fs=fs: e.memset(b.FM[:, fs.start:fs.start + 1], 0.0),
                       waits=fw)
                p.emit("vector", lambda e, k=k, fs=fs, lo=lo, hi=hi, ys=ys: e.tensor_single_scalar(
                    b.FM[:, fs.start + 1:fs.stop], ys(lo, hi - 1), C[:, 24 + k:25 + k], ALU.mult))
            else:
                p.emit("vector", lambda e, k=k, fs=fs, lo=lo, hi=hi, ys=ys: e.tensor_single_scalar(
                    b.FM[:, fs], ys(lo - 1, hi - 1), C[:, 24 + k:25 + k], ALU.mult),
                    waits=fw)
            p.emit("vector", lambda e, k=k, fs=fs, lo=lo, hi=hi, ys=ys: e.scalar_tensor_tensor(
                b.FM[:, fs], ys(lo, hi), C[:, 32 + k:33 + k], b.FM[:, fs], ALU.mult, ALU.add))
            if tc == NTC - 1:
                p.emit("vector", lambda e, k=k, fs=fs, lo=lo, hi=hi, ys=ys: e.scalar_tensor_tensor(
                    b.FM[:, fs.start:fs.stop - 1], ys(lo + 1, hi), C[:, 40 + k:41 + k],
                    b.FM[:, fs.start:fs.stop - 1], ALU.mult, ALU.add))
            else:
                p.emit("vector", lambda e, k=k, fs=fs, lo=lo, hi=hi, ys=ys: e.scalar_tensor_tensor(
                    b.FM[:, fs], ys(lo + 1, hi + 1), C[:, 40 + k:41 + k], b.FM[:, fs],
                    ALU.mult, ALU.add))
            p.emit("vector", lambda e, k=k, fs=fs: e.tensor_scalar_add(
                b.FM[:, fs], b.FM[:, fs], C[:, 16 + k:17 + k]))
            p.emit("vector", lambda e, k=k, fs=fs, lo=lo, hi=hi: e.tensor_add(
                b.FM[:, fs], b.FM[:, fs], b.BIGB[:, k * T + lo:k * T + hi]))
            p.emit("vector", lambda e, fs=fs, lo=lo, hi=hi: e.tensor_add(
                b.FM[:, fs], b.FM[:, fs], b.XB[:, lo:hi]))
            uv = p.emit("vector", lambda e, k=k, fs=fs, lo=lo, hi=hi: e.tensor_copy(
                b.BIGB[:, k * T + lo:k * T + hi], b.FM[:, fs]), inc="sV")
            st = p.dma("gpsimd", aps["ud"][k * 128:(k + 1) * 128, lo:hi],
                       b.BIGB[:, k * T + lo:k * T + hi], waits=[("sV", uv)], inc=f"sUo{tc % 2}")
            p.meta[f"ab_fm{tc % 2}"] = st
    uall = [(f"sUo{s}", p.now(f"sUo{s}")) for s in range(2)]

    wg0 = [("sV", p.now("sV"))]
    for k in range(NDC):
        lg = p.dma("gpsimd", b.BIGA[:, k * 4096:(k + 1) * 4096],
                   aps["w_gate"][k * 128:(k + 1) * 128, :], waits=wg0 if k == 0 else (),
                   inc="sLWG")
    wu0 = uall
    for k in range(NDC):
        lu = p.dma("gpsimd", b.BIGB[:, k * 4096:(k + 1) * 4096],
                   aps["w_up"][k * 128:(k + 1) * 128, :], waits=wu0 if k == 0 else (),
                   inc="sLWU")
    wboth = [("sLWG", lg), ("sLWU", lu)]
    bank_free = {}
    stb_free = [None] * 8
    last_mm = None
    for tc in range(NTC):
        ulw = [("sV", p.now("sV")), ("sA", p.now("sA"))]
        ulw += [(f"sHo{s}", p.now(f"sHo{s}")) for s in range(8)]
        if tc == 0:
            ulw += uall
        for k in range(NDC):
            ldu = p.dma("sync", b.STB[:, k * TC:(k + 1) * TC],
                        aps["ud"][k * 128:(k + 1) * 128, tc * TC:(tc + 1) * TC],
                        waits=ulw if k == 0 else (), inc="sLU")
        pbcw = p.meta.get("pbc_free", [])
        for k in range(NDC):
            sl = slice(k * TC, (k + 1) * TC)
            aW = p.emit("scalar", lambda e, sl=sl: e.activation(b.XB[:, sl], b.STB[:, sl], AF.Square),
                        waits=([("sLU", ldu), ("sP", p.now("sP"))] if k == 0 else ()),
                        inc="sA" if k == NDC - 1 else None)
        for k in range(NDC):
            sl = slice(k * TC, (k + 1) * TC)
            w_ = ([("sA", aW)] + pbcw) if k == 0 else ()
            mmW = p.emit("tensor", lambda e, sl=sl, k=k: e.matmul(
                b.pbc[0][0:1, :], b.ONEB[:, 0:1], b.XB[:, sl], start=(k == 0), stop=(k == NDC - 1)),
                waits=w_, inc="sP" if k == NDC - 1 else None)
        R = b.ROWS
        sv = p.emit("vector", lambda e: e.tensor_scalar_mul(R[:, 0:512], b.pbc[0][0:1, :], 1.0 / D),
                    waits=[("sP", mmW)], inc="sV")
        sq = p.emit("scalar", lambda e: e.activation(R[:, 512:1024], R[:, 0:512], AF.Sqrt,
                                                     bias=b.EPSC[0:1, 1:2]), waits=[("sV", sv)], inc="sA")
        rc = p.emit("vector", lambda e: e.reciprocal(R[:, 0:512], R[:, 512:1024]),
                    waits=[("sA", sq)], inc="sV")
        bcW = p.emit("tensor", lambda e: e.matmul(b.pbc[0][:], b.ONEF[0:1, 0:1].broadcast_to([1, 128]),
                                                  R[:, 0:512], start=True, stop=True),
                     waits=[("sV", rc)], inc="sP")
        for k in range(NDC):
            sl = slice(k * TC, (k + 1) * TC)
            zw = p.emit("vector", lambda e, sl=sl, k=k: e.scalar_tensor_tensor(
                b.XB[:, sl], b.STB[:, sl], C[:, 56 + k:57 + k], b.pbc[0][:], ALU.mult, ALU.mult),
                waits=[("sP", bcW)] if k == 0 else (), inc="sV")
        p.meta["pbc_free"] = [("sV", zw)]
        for m in range(32):
            bg, bu = b.pb[(m % 2) * 2], b.pb[(m % 2) * 2 + 1]
            for k in range(NDC):
                w_ = []
                if k == 0:
                    w_ = [("sV", zw)] + (wboth if (tc == 0 and m == 0) else [])
                    if bank_free.get((m % 2, "g")) is not None:
                        w_.append(("sA", bank_free[(m % 2, "g")]))
                p.emit("tensor", lambda e, m=m, k=k, bg=bg: e.matmul(
                    bg[:], b.BIGA[:, k * 4096 + m * 128:k * 4096 + (m + 1) * 128],
                    b.XB[:, k * TC:(k + 1) * TC], start=(k == 0), stop=(k == NDC - 1)),
                    waits=w_)
                w_ = []
                if k == 0 and bank_free.get((m % 2, "u")) is not None:
                    w_ = [("sV", bank_free[(m % 2, "u")])]
                last_mm = p.emit("tensor", lambda e, m=m, k=k, bu=bu: e.matmul(
                    bu[:], b.BIGB[:, k * 4096 + m * 128:k * 4096 + (m + 1) * 128],
                    b.XB[:, k * TC:(k + 1) * TC], start=(k == 0), stop=(k == NDC - 1)),
                    waits=w_, inc="sP" if k == NDC - 1 else None)
            w_ = [("sP", last_mm)]
            if p.meta.get("ab_ev2") is not None:
                w_.append(("sV", p.meta["ab_ev2"]))
            ev1 = p.emit("scalar", lambda e, m=m, bg=bg: e.activation(
                b.S1[:, 0:512], bg[:], AF.Silu, bias=C[:, 64 + m:65 + m]),
                waits=w_, inc="sA")
            bank_free[(m % 2, "g")] = ev1
            sl = slice((m % 8) * TC, (m % 8 + 1) * TC)
            w_ = [("sA", ev1)]
            if stb_free[m % 8] is not None:
                w_.append((f"sHo{m % 8}", stb_free[m % 8]))
            ev2 = p.emit("vector", lambda e, m=m, sl=sl, bu=bu: e.scalar_tensor_tensor(
                b.STB[:, sl], bu[:], C[:, 96 + m:97 + m], b.S1[:, 0:512], ALU.add, ALU.mult),
                waits=w_, inc="sV")
            p.meta["ab_ev2"] = ev2
            bank_free[(m % 2, "u")] = ev2
            st = p.dma("gpsimd", aps["hd"][m * 128:(m + 1) * 128, tc * TC:(tc + 1) * TC],
                       b.STB[:, sl], waits=[("sV", ev2)], inc=f"sHo{m % 8}")
            stb_free[m % 8] = st

    wd0 = [("sP", p.now("sP"))]
    for k in range(32):
        ldw = p.dma("gpsimd", b.BIGA[:, k * 1024:(k + 1) * 1024],
                    aps["w_down"][k * 128:(k + 1) * 128, :], waits=wd0 if k == 0 else (),
                    inc="sLWD")
    all_h = [(f"sHo{s}", p.now(f"sHo{s}")) for s in range(8)] + [("sLWD", ldw)]
    fm_free = [None] * 2
    stbD_free = [None] * 8
    uD_free = [None] * 2
    for tc in range(NTC):
        xg = load_xc(p, b, aps["xin"], tc,
                     gate=[("sV", p.now("sV")), ("sA", p.now("sA"))])
        if tc == 0:
            w_ = [("sP", p.now("sP"))]
        else:
            w_ = [("sV", p.meta["abD_evac"])]
        for k in range(NDC):
            ldu = p.dma("sync", b.XB[:, k * TC:(k + 1) * TC],
                        aps["ud"][k * 128:(k + 1) * 128, tc * TC:(tc + 1) * TC],
                        waits=w_ if k == 0 else (), inc="sLU2")
        for grp in range(2):
            kcW = None
            for kc in range(32):
                sl = slice((kc % 8) * TC, (kc % 8 + 1) * TC)
                w_ = list(all_h) if (tc == 0 and grp == 0 and kc < 8) else []
                if stbD_free[kc % 8] is not None:
                    w_.append(("sP", stbD_free[kc % 8]))
                ldh = p.dma("sync", b.STB[:, sl],
                            aps["hd"][kc * 128:(kc + 1) * 128, tc * TC:(tc + 1) * TC],
                            waits=w_, inc=f"sLH{kc % 8}")
                for j in range(4):
                    dm = grp * 4 + j
                    ww = []
                    if j == 0:
                        ww.append((f"sLH{kc % 8}", ldh))
                    if kc == 0 and j == 0:
                        if fm_free[0] is not None:
                            ww.append(("sV", p.meta.get("abD_evac", 0)))
                        else:
                            ww.append(("sA", p.now("sA")))
                            ww.append(("sV", p.now("sV")))
                    kcW = p.emit("tensor", lambda e, kc=kc, dm=dm, j=j, sl=sl: e.matmul(
                        b.pb[j][:], b.BIGA[:, kc * 1024 + dm * 128:kc * 1024 + (dm + 1) * 128],
                        b.STB[:, sl], start=(kc == 0), stop=(kc == 31)),
                        waits=ww, inc="sP" if j == 3 else None)
                stbD_free[kc % 8] = kcW
            for j in range(4):
                dm = grp * 4 + j
                fs = slice((j % 2) * TC, (j % 2) * TC + TC)
                w_ = [("sP", kcW), ("sLX", p.now("sLX")), ("sLU2", ldu)]
                if fm_free[j % 2] is not None:
                    w_.append((f"sXo{j % 2}", fm_free[j % 2]))
                p.emit("vector", lambda e, dm=dm, j=j, fs=fs: e.tensor_scalar(
                    b.FM[:, fs], b.pb[j][:], C[:, 128 + dm:129 + dm], None, ALU.add),
                    waits=w_)
                p.emit("vector", lambda e, dm=dm, fs=fs: e.tensor_add(
                    b.FM[:, fs], b.FM[:, fs], b.XB[:, dm * TC:(dm + 1) * TC]))
                ev = p.emit("vector", lambda e, dm=dm, fs=fs: e.tensor_add(
                    b.FM[:, fs], b.FM[:, fs], b.XC[:, dm * TC:(dm + 1) * TC]), inc="sV")
                p.meta["abD_evac"] = ev
                uD_free[tc % 2] = ev
                st = p.dma("gpsimd", aps["xout"][dm * 128:(dm + 1) * 128, tc * TC:(tc + 1) * TC],
                           b.FM[:, fs], waits=[("sV", ev)], inc=f"sXo{j % 2}")
                fm_free[j % 2] = st
    return [(f"sXo{j}", p.now(f"sXo{j}")) for j in range(2)]


def emit_attn(p: Prog, b: Bufs, aps, rope: bool, gate=()):
    NTC = p.NTC
    T = p.T
    NSC = T // 128
    V65 = 1040

    if rope:
        p.dma("sync", b.COLS[:, 0:8], aps["bqc"], waits=list(gate), inc="sLB")
        p.dma("sync", b.COLS[:, 8:16], aps["bkc"], inc="sLB")
        p.dma("sync", b.COLS[:, 16:24], aps["boc"], inc="sLB")
        p.dma("sync", b.S2[:, 0:1024], aps["bvb"], inc="sLB")
        p.dma("gpsimd", b.WS[:, 0:128], aps["pswap"], waits=list(gate), inc="sLB2")
        bg = [("sLB", p.now("sLB")), ("sLB2", p.now("sLB2"))]
    else:
        bg = list(gate)

    def load_w(ap, wait):
        for kc in range(NDC):
            lw = p.dma("gpsimd", b.WQK[:, kc * 1024:(kc + 1) * 1024],
                       ap[kc * 128:(kc + 1) * 128, :], waits=wait if kc == 0 else (), inc="sLWS")
        return [("sLWS", lw)]

    w1r = load_w(aps["wqT"], list(gate))

    vt3 = b.BIGB[:, 0:NSC * V65].rearrange("p (s h c) -> p s h c", h=16, c=65)
    mo = p.emit("vector", lambda e: e.memset(vt3[:, :, :, 64:65], 1.0),
                waits=[("sV", p.now("sV")), ("sA", p.now("sA")), ("sP", p.now("sP"))] + list(gate),
                inc="sV")

    def rope_tables(tc):
        w_ = [("sV", p.meta["attn_ct_used"])] if p.meta.get("attn_ct_used") else []
        p.dma("gpsimd", b.CT, aps["cosT"][:, tc * TC:(tc + 1) * TC], waits=w_, inc="sLR")
        lst = p.dma("gpsimd", b.ST, aps["sinT"][:, tc * TC:(tc + 1) * TC], inc="sLR")
        return [("sLR", lst)]

    def proj_pass(which, tc, wrdy):
        for hp in range(NDC):
            bank = b.pb[hp % 2]
            for kc in range(NDC):
                w_ = []
                if kc == 0:
                    w_ = list(p.meta["attn_xbrdy"])
                    if tc == 0 and hp < 2:
                        w_ += wrdy
                    bkey = f"attn_bank{hp % 2}"
                    if p.meta.get(bkey) is not None:
                        w_.append(p.meta[bkey])
                mm = p.emit("tensor", lambda e, hp=hp, kc=kc, bank=bank: e.matmul(
                    bank[:], b.WQK[:, kc * 1024 + hp * 128:kc * 1024 + (hp + 1) * 128],
                    b.XB[:, kc * TC:(kc + 1) * TC], start=(kc == 0), stop=(kc == NDC - 1)),
                    waits=w_, inc="sP" if kc == NDC - 1 else None)
            if rope:
                rg = p.meta["attn_rg"]
                bcol = b.COLS[:, (0 if which == "q" else 8) + hp:(0 if which == "q" else 8) + hp + 1]
                sl = slice((hp % 4) * TC, (hp % 4 + 1) * TC)
                w_ = [("sP", mm)]
                if p.meta.get(f"attn_stb{hp % 4}") is not None:
                    w_.append(("sP", p.meta[f"attn_stb{hp % 4}"]))
                ev = p.emit("scalar", lambda e, sl=sl, bank=bank, bcol=bcol: e.activation(
                    b.STB[:, sl], bank[:], AF.Identity, bias=bcol),
                    waits=w_, inc="sA")
                p.meta[f"attn_bank{hp % 2}"] = ("sA", ev)
                sw = p.emit("tensor", lambda e, sl=sl: e.matmul(
                    b.pb[4][:], b.WS[:, 0:128], b.STB[:, sl], start=True, stop=True),
                    waits=[("sA", ev), ("sV", p.meta.get("attn_sw_used", 0))] + rg,
                    inc="sP")
                p.meta[f"attn_stb{hp % 4}"] = sw
                p.emit("vector", lambda e, sl=sl: e.tensor_mul(b.FM[:, 0:512], b.STB[:, sl], b.CT),
                       waits=[("sP", sw)])
                sm = p.emit("vector", lambda e: e.tensor_mul(b.S1[:, 0:512], b.pb[4][:], b.ST),
                            inc="sV")
                p.meta["attn_sw_used"] = sm
                if which == "q":
                    qs = slice((hp % 4) * TC + 2048, (hp % 4 + 1) * TC + 2048)
                    w_ = []
                    if p.meta.get(f"attn_qst{hp % 4}") is not None:
                        w_.append((f"sQo{hp % 4}", p.meta[f"attn_qst{hp % 4}"]))
                    rv = p.emit("vector", lambda e, qs=qs: e.tensor_add(
                        b.STB[:, qs], b.FM[:, 0:512], b.S1[:, 0:512]), waits=w_, inc="sV")
                    st = p.dma("gpsimd", aps["qd"][hp * 128:(hp + 1) * 128, tc * TC:(tc + 1) * TC],
                               b.STB[:, qs], waits=[("sV", rv)], inc=f"sQo{hp % 4}")
                    p.meta[f"attn_qst{hp % 4}"] = st
                else:
                    rv = p.emit("vector", lambda e, hp=hp, tc=tc: e.tensor_add(
                        b.BIGA[:, hp * T + tc * TC:hp * T + (tc + 1) * TC],
                        b.FM[:, 0:512], b.S1[:, 0:512]), inc="sV")
                p.meta["attn_ct_used"] = rv
            else:
                if which == "q":
                    sl = slice((hp % 4) * TC, (hp % 4 + 1) * TC)
                    w_ = [("sP", mm)]
                    if p.meta.get(f"attn_qst{hp % 4}") is not None:
                        w_.append((f"sQo{hp % 4}", p.meta[f"attn_qst{hp % 4}"]))
                    ev = p.emit("scalar", lambda e, sl=sl, bank=bank: e.activation(
                        b.STB[:, sl], bank[:], AF.Identity), waits=w_, inc="sA")
                    p.meta[f"attn_bank{hp % 2}"] = ("sA", ev)
                    st = p.dma("gpsimd", aps["qd"][hp * 128:(hp + 1) * 128, tc * TC:(tc + 1) * TC],
                               b.STB[:, sl], waits=[("sA", ev)], inc=f"sQo{hp % 4}")
                    p.meta[f"attn_qst{hp % 4}"] = st
                else:
                    ev = p.emit("scalar", lambda e, hp=hp, tc=tc, bank=bank: e.activation(
                        b.BIGA[:, hp * T + tc * TC:hp * T + (tc + 1) * TC], bank[:], AF.Identity),
                        waits=[("sP", mm)], inc="sA")
                    p.meta[f"attn_bank{hp % 2}"] = ("sA", ev)

    for tc in range(NTC):
        xg = load_xc(p, b, aps["xin"], tc,
                     gate=([("sV", p.now("sV")), ("sA", p.now("sA"))] +
                           (p.meta.get("pbc_free", []) + bg if tc == 0 else [])))
        lng = xg + ([("sP", p.now("sP"))] if tc > 0 else [])
        lng += [(f"sQo{j}", p.now(f"sQo{j}")) for j in range(4)]
        lng += [("sXh", p.now("sXh"))]
        xw = emit_ln(p, b, b.EPSC[0:1, 0:1], gate=lng)
        for k in range(NDC):
            p.dma("gpsimd", aps["xhd"][k * 128:(k + 1) * 128, tc * TC:(tc + 1) * TC],
                  b.XB[:, k * TC:(k + 1) * TC], waits=list(xw) if k == 0 else (), inc="sXh")
        if rope:
            p.meta["attn_rg"] = rope_tables(tc)
        p.meta["attn_xbrdy"] = xw
        proj_pass("q", tc, w1r)
    allxh = [("sXh", p.now("sXh"))]
    wkr = load_w(aps["wkT"], [("sP", p.now("sP"))])
    p.meta["attn_ct_used"] = None
    for tc in range(NTC):
        w_ = ([("sP", p.now("sP")), ("sA", p.now("sA")), ("sV", p.now("sV"))] + allxh) if tc == 0 \
            else [("sP", p.now("sP"))]
        for k in range(NDC):
            ldxh = p.dma("sync", b.XB[:, k * TC:(k + 1) * TC],
                         aps["xhd"][k * 128:(k + 1) * 128, tc * TC:(tc + 1) * TC],
                         waits=w_ if k == 0 else (), inc="sXh2")
        if rope:
            p.meta["attn_rg"] = rope_tables(tc)
        p.meta["attn_xbrdy"] = [("sXh2", ldxh)]
        proj_pass("k", tc, wkr)

    wvr = load_w(aps["wvT"], [("sP", p.now("sP"))])
    for tc in range(NTC):
        w_ = [("sP", p.now("sP"))] if tc > 0 else [("sP", p.now("sP")), ("sA", p.now("sA"))]
        for k in range(NDC):
            ldxh = p.dma("sync", b.XB[:, k * TC:(k + 1) * TC],
                         aps["xhd"][k * 128:(k + 1) * 128, tc * TC:(tc + 1) * TC],
                         waits=w_ if k == 0 else (), inc="sXh2")
        xh2 = [("sXh2", ldxh)]
        for tsq in range(4):
            sc = tc * 4 + tsq
            for dn in range(2):
                bank = b.pb[2 + dn]
                for kc in range(NDC):
                    w_ = []
                    if kc == 0:
                        w_ = [("sV", mo)] + xh2
                        if tc == 0 and tsq == 0:
                            w_ += wvr
                        bkey = f"attn_vbank{dn}"
                        if p.meta.get(bkey) is not None:
                            w_.append(("sV", p.meta[bkey]))
                    mm = p.emit("tensor", lambda e, tsq=tsq, dn=dn, kc=kc, bank=bank: e.matmul(
                        bank[:], b.XB[:, kc * TC + tsq * 128:kc * TC + tsq * 128 + 128],
                        b.WQK[:, kc * 1024 + dn * 512:kc * 1024 + (dn + 1) * 512],
                        start=(kc == 0), stop=(kc == NDC - 1)),
                        waits=w_, inc="sP" if kc == NDC - 1 else None)
                out3 = b.BIGB[:, sc * V65 + dn * 8 * 65:sc * V65 + (dn * 8 + 8) * 65].rearrange(
                    "p (h c) -> p h c", c=65)[:, :, 0:64]
                in3 = bank[:].rearrange("p (h c) -> p h c", c=64)
                if rope:
                    vv = p.emit("vector", lambda e, out3=out3, in3=in3, dn=dn: e.tensor_add(
                        out3, in3, b.S2[:, dn * 512:(dn + 1) * 512].rearrange("p (h c) -> p h c", c=64)),
                        waits=[("sP", mm)], inc="sV")
                else:
                    vv = p.emit("vector", lambda e, out3=out3, in3=in3: e.tensor_copy(out3, in3),
                                waits=[("sP", mm)], inc="sV")
                p.meta[f"attn_vbank{dn}"] = vv
    wor = load_w(aps["woT"], [("sP", p.now("sP"))])
    kvready = [("sV", p.now("sV")), ("sA", p.now("sA")), ("sP", p.now("sP"))] + wor
    allq = [(f"sQo{j}", p.now(f"sQo{j}")) for j in range(4)]

    scale = 0.125
    stb_free = [None] * 8
    sti = 0
    for tq in range(NTC):
        w_ = (kvready + allq) if tq == 0 else [("sP", p.meta["attn_lastwo"])]
        for hp in range(NDC):
            ldq = p.dma("sync", b.WS[:, hp * TC:(hp + 1) * TC],
                        aps["qd"][hp * 128:(hp + 1) * 128, tq * TC:(tq + 1) * TC],
                        waits=w_ if hp == 0 else (), inc="sLQ")
        xg = load_xc(p, b, aps["xin"], tq, gate=[("sV", p.now("sV")), ("sA", p.now("sA"))])
        qrdy = [("sLQ", ldq)]
        for hp in range(8):
            nsc = tq * 4 + 4
            for sc in range(nsc):
                delta = max(0, sc * 128 - tq * TC)
                sbank = [b.pb[0], b.pb[1]] if (sc % 2 == 0) else [b.pbc[0], b.pbc[1]]
                for h2 in range(2):
                    lhs = b.BIGA[h2 * 64:h2 * 64 + 64, hp * T + sc * 128:hp * T + (sc + 1) * 128]
                    rhs = b.WS[h2 * 64:h2 * 64 + 64, hp * TC + delta:(hp + 1) * TC]
                    w_ = [("sA", p.meta.get(f"attn_sexp{sc % 2}_{h2}", 0))]
                    if sc < 2:
                        w_ += qrdy
                        if tq == 0 and hp == 0:
                            w_ += kvready
                    smm = p.emit("tensor", lambda e, lhs=lhs, rhs=rhs, delta=delta, bank=sbank[h2]: e.matmul(
                        bank[:, delta:512], lhs, rhs, start=True, stop=True),
                        waits=w_, inc="sP")
                    ssl = slice((sti % 8) * TC + delta, (sti % 8) * TC + TC)
                    w_ = [("sP", smm)]
                    if stb_free[sti % 8] is not None:
                        w_.append(("sP", stb_free[sti % 8]))
                    elif sti < 8:
                        w_ += allq
                    ex = p.emit("scalar", lambda e, ssl=ssl, delta=delta, bank=sbank[h2]: e.activation(
                        b.STB[:, ssl], bank[:, delta:512], AF.Exp, scale=scale),
                        waits=w_, inc="sA")
                    p.meta[f"attn_sexp{sc % 2}_{h2}"] = ex
                    pvw = [("sA", ex)]
                    if delta > 0 or sc * 128 == tq * TC:
                        dsl = slice((sti % 8) * TC + delta, (sti % 8) * TC + delta + 128)
                        ex2 = p.emit("vector", lambda e, dsl=dsl: e.tensor_mul(
                            b.STB[:, dsl], b.STB[:, dsl], b.TRIL[:]),
                            waits=[("sA", ex)], inc="sV")
                        pvw = [("sV", ex2)]
                    if delta > 0:
                        zsl = slice((sti % 8) * TC, (sti % 8) * TC + delta)
                        mz = p.emit("vector", lambda e, zsl=zsl: e.memset(b.STB[:, zsl], 0.0),
                                    waits=list(w_), inc="sV")
                        pvw.append(("sV", mz))
                    hglob = hp * 2 + h2
                    if sc == 0:
                        pvw = pvw + [("sV", p.meta.get(f"attn_opsum{h2}", 0))]
                    pvm = p.emit("tensor", lambda e, sc=sc, hglob=hglob, sti=sti, h2=h2, nsc=nsc: e.matmul(
                        b.pb[2 + h2][0:65, 0:512],
                        b.BIGB[:, sc * V65 + hglob * 65:sc * V65 + (hglob + 1) * 65],
                        b.STB[:, (sti % 8) * TC:(sti % 8) * TC + TC],
                        start=(sc == 0), stop=(sc == nsc - 1)),
                        waits=pvw, inc="sP")
                    stb_free[sti % 8] = pvm
                    sti += 1
            for h2 in range(2):
                R = b.ROWS
                w_ = [("sP", p.now("sP"))]
                if rope:
                    p.emit("vector", lambda e, h2=h2: e.tensor_scalar_add(
                        R[:, 0:512], b.pb[2 + h2][64:65, :], b.EPSC[0:1, 3:4]), waits=w_)
                else:
                    p.emit("vector", lambda e, h2=h2: e.tensor_copy(R[:, 0:512], b.pb[2 + h2][64:65, :]),
                           waits=w_)
                rc = p.emit("vector", lambda e: e.reciprocal(R[:, 512:1024], R[:, 0:512]), inc="sV")
                bc = p.emit("tensor", lambda e: e.matmul(
                    b.pb[4][:], b.ONEF[0:1, 0:1].broadcast_to([1, 128]), R[:, 512:1024],
                    start=True, stop=True), waits=[("sV", rc)], inc="sP")
                cp = p.emit("vector", lambda e: e.tensor_copy(b.S1[:, 0:512], b.pb[4][:, 0:512]),
                            waits=[("sP", bc)], inc="sV")
                ev = p.emit("vector", lambda e, hp=hp, h2=h2: e.tensor_mul(
                    b.XB[h2 * 64:h2 * 64 + 64, hp * TC:(hp + 1) * TC],
                    b.pb[2 + h2][0:64, :], b.S1[0:64, 0:512]), inc="sV")
                p.meta["attn_ctx"] = ev
                p.meta[f"attn_opsum{h2}"] = ev
        for dm in range(NDC):
            for kc in range(NDC):
                w_ = []
                if kc == 0:
                    w_ = [("sV", p.meta["attn_ctx"])]
                    if p.meta.get("attn_wob") is not None:
                        w_.append(("sV", p.meta["attn_wob"]))
                wm = p.emit("tensor", lambda e, dm=dm, kc=kc: e.matmul(
                    b.pb[4][:], b.WQK[:, kc * 1024 + dm * 128:kc * 1024 + (dm + 1) * 128],
                    b.XB[:, kc * TC:(kc + 1) * TC], start=(kc == 0), stop=(kc == NDC - 1)),
                    waits=w_, inc="sP" if kc == NDC - 1 else None)
            fs = slice((dm % 2) * TC, (dm % 2) * TC + TC)
            w_ = [("sP", wm)] + (xg if dm == 0 else [])
            if p.meta.get(f"attn_xo{dm % 2}") is not None:
                w_.append((f"sXo{dm % 2}", p.meta[f"attn_xo{dm % 2}"]))
            if rope:
                p.emit("vector", lambda e, dm=dm, fs=fs: e.tensor_scalar_add(
                    b.FM[:, fs], b.pb[4][:], b.COLS[:, 16 + dm:17 + dm]), waits=w_)
                ev = p.emit("vector", lambda e, dm=dm, fs=fs: e.tensor_add(
                    b.FM[:, fs], b.FM[:, fs], b.XC[:, dm * TC:(dm + 1) * TC]), inc="sV")
            else:
                ev = p.emit("vector", lambda e, dm=dm, fs=fs: e.tensor_add(
                    b.FM[:, fs], b.pb[4][:], b.XC[:, dm * TC:(dm + 1) * TC]), waits=w_, inc="sV")
            p.meta["attn_wob"] = ev
            st = p.dma("gpsimd", aps["xout"][dm * 128:(dm + 1) * 128, tq * TC:(tq + 1) * TC],
                       b.FM[:, fs], waits=[("sV", ev)], inc=f"sXo{dm % 2}")
            p.meta[f"attn_xo{dm % 2}"] = st
        p.meta["attn_lastwo"] = p.now("sP")
    return [(f"sXo{j}", p.now(f"sXo{j}")) for j in range(2)]


def emit_gla(p: Prog, b: Bufs, aps, gate=()):
    NTC = p.NTC
    T = p.T
    C = b.COLS
    WQ, WK, WRLO, WR = 0, 8192, 16384, 24576
    QH, KH, GNW, XLO = 12288, 16384, 20480, 21504
    S0B = 25600

    g0 = list(gate)
    for i, nm in enumerate(("Wq", "Wk")):
        for kc in range(NDC):
            p.dma("gpsimd", b.BIGA[:, i * 8192 + kc * 1024:i * 8192 + (kc + 1) * 1024],
                  aps[nm][kc * 128:(kc + 1) * 128, :], waits=g0 if (i == 0 and kc == 0) else (),
                  inc="sGW")
    for kc in range(NDC):
        w_ = list(g0) if kc == 0 else []
        if kc >= 4:
            w_.append(("sV", p.meta.get("gla_wrs", 0)))
        lwr = p.dma("sync", b.XC[:, (kc % 4) * 1024:(kc % 4 + 1) * 1024],
                    aps["Wr"][kc * 128:(kc + 1) * 128, :], waits=w_, inc=f"sWRS{kc % 4}")
        hi = p.emit("scalar", lambda e, kc=kc: e.activation(
            b.BIGA[:, WR + kc * 1024:WR + (kc + 1) * 1024],
            b.XC[:, (kc % 4) * 1024:(kc % 4 + 1) * 1024], AF.Identity),
            waits=[(f"sWRS{kc % 4}", lwr)], inc="sA")
        lo = p.emit("vector", lambda e, kc=kc: e.tensor_sub(
            b.BIGA[:, WRLO + kc * 1024:WRLO + (kc + 1) * 1024],
            b.XC[:, (kc % 4) * 1024:(kc % 4 + 1) * 1024],
            b.BIGA[:, WR + kc * 1024:WR + (kc + 1) * 1024]),
            waits=[("sA", hi)], inc="sV")
        p.meta["gla_wrs"] = lo
    p.dma("gpsimd", b.BIGB[:, GNW:GNW + 1024], aps["gnw_b"], waits=g0, inc="sGW")
    lw = p.dma("gpsimd", b.WS[:, 0:64].rearrange("p (k c) -> p k c", k=8),
               aps["Wa"].rearrange("(k p) c -> p k c", p=128), inc="sGW")
    lc = p.dma("sync", C[:, 0:1], aps["qnw_c"], waits=g0, inc="sGC")
    lc = p.dma("sync", C[:, 1:2], aps["knw_c"], inc="sGC")
    lc = p.dma("sync", b.ROWS[0:1, 0:8], aps["alog_r"], inc="sGC")
    lc = p.dma("sync", b.ROWS[0:1, 8:16], aps["dtb_r"], inc="sGC")
    wrdy = [("sGW", p.now("sGW")), ("sGC", lc)]
    bc1 = p.emit("tensor", lambda e: e.matmul(b.pb[4][:, 0:8],
                 b.ONEF[0:1, 0:1].broadcast_to([1, 128]), b.ROWS[0:1, 0:8],
                 start=True, stop=True), waits=wrdy, inc="sP")
    e1 = p.emit("scalar", lambda e: e.activation(C[:, 16:24], b.pb[4][:, 0:8], AF.Exp),
                waits=[("sP", bc1)], inc="sA")
    p.emit("vector", lambda e: e.tensor_scalar_mul(C[:, 16:24], C[:, 16:24], -1.0),
           waits=[("sA", e1)])
    bc2 = p.emit("tensor", lambda e: e.matmul(b.pb[4][:, 8:16],
                 b.ONEF[0:1, 0:1].broadcast_to([1, 128]), b.ROWS[0:1, 8:16],
                 start=True, stop=True), waits=[("sA", e1)], inc="sP")
    p.emit("vector", lambda e: e.tensor_copy(C[:, 24:32], b.pb[4][:, 8:16]),
           waits=[("sP", bc2)])
    p.emit("vector", lambda e: e.memset(b.S2[:, 0:1024], 0.0))
    p.emit("vector", lambda e: e.memset(b.BIGB[:, S0B:S0B + 1024], 0.0), inc="sV")
    prdy = [("sV", p.now("sV"))]

    sti = [0]

    def slot():
        s = 1536 + (sti[0] % 20) * 128
        sti[0] += 1
        return slice(s, s + 128)

    for tc in range(NTC):
        xg = load_xc(p, b, aps["xin"], tc,
                     gate=([("sV", p.now("sV")), ("sA", p.now("sA"))] +
                           (p.meta.get("pbc_free", []) + prdy if tc == 0 else [])))
        xw = emit_ln(p, b, b.EPSC[0:1, 0:1],
                     gate=xg + ([("sP", p.now("sP"))] if tc > 0 else []),
                     lo_dst=[b.BIGB[:, XLO + k * TC:XLO + (k + 1) * TC] for k in range(NDC)])
        for kc in range(NDC):
            w_ = ([("sP", p.now("sP"))] + (wrdy if tc == 0 else [])) if kc == 0 else []
            lwv = p.dma("gpsimd", b.WQK[:, kc * 1024:(kc + 1) * 1024],
                        aps["Wv"][kc * 128:(kc + 1) * 128, :], waits=w_, inc="sWV")
        wvrdy = [("sWV", lwv)]
        for which, wbase, dsth, ncol in (("q", WQ, QH, 0), ("k", WK, KH, 1)):
            for h in range(8):
                bank = b.pb[h % 2]
                for kc in range(NDC):
                    w_ = []
                    if kc == 0:
                        w_ = list(xw)
                        if which == "q" and h == 0:
                            w_ += [(f"sXo{j}", p.now(f"sXo{j}")) for j in range(2)]
                        if p.meta.get(f"gla_pj{h % 2}") is not None:
                            w_.append(p.meta[f"gla_pj{h % 2}"])
                        if p.meta.get("gla_fmfree") is not None:
                            w_.append(("sV", p.meta["gla_fmfree"]))
                    mm = p.emit("tensor", lambda e, wbase=wbase, h=h, kc=kc, bank=bank: e.matmul(
                        bank[:, 0:512],
                        b.BIGA[:, wbase + kc * 1024 + h * 128:wbase + kc * 1024 + (h + 1) * 128],
                        b.XB[:, kc * TC:(kc + 1) * TC], start=(kc == 0), stop=(kc == NDC - 1)),
                        waits=w_, inc="sP" if kc == NDC - 1 else None)
                sil = p.emit("scalar", lambda e, bank=bank: e.activation(
                    b.FM[:, 0:512], bank[:, 0:512], AF.Silu), waits=[("sP", mm)], inc="sA")
                p.meta[f"gla_pj{h % 2}"] = ("sA", sil)
                sq = p.emit("scalar", lambda e: e.activation(
                    b.STB[:, 1024:1536], b.FM[:, 0:512], AF.Square), inc="sA")
                w_ = [("sA", sq)]
                if p.meta.get("gla_pb4row") is not None:
                    w_.append(("sV", p.meta["gla_pb4row"]))
                smm = p.emit("tensor", lambda e: e.matmul(
                    b.pb[4][0:1, 0:512], b.ONEB[:, 0:1], b.STB[:, 1024:1536],
                    start=True, stop=True), waits=w_, inc="sP")
                sv1 = p.emit("vector", lambda e: e.tensor_scalar(
                    b.ROWS[0:1, 512:1024], b.pb[4][0:1, 0:512], 1.0 / 128.0, None, ALU.mult),
                    waits=[("sP", smm)], inc="sV")
                p.meta["gla_pb4row"] = sv1
                sq2 = p.emit("scalar", lambda e: e.activation(
                    b.ROWS[0:1, 512:1024], b.ROWS[0:1, 512:1024], AF.Sqrt, bias=b.EPSC[0:1, 0:1]),
                    waits=[("sV", sv1)], inc="sA")
                rc = p.emit("vector", lambda e: e.reciprocal(b.ROWS[0:1, 512:1024], b.ROWS[0:1, 512:1024]),
                            waits=[("sA", sq2)], inc="sV")
                bcm = p.emit("tensor", lambda e: e.matmul(
                    b.pbc[0][:], b.ONEF[0:1, 0:1].broadcast_to([1, 128]), b.ROWS[0:1, 512:1024],
                    start=True, stop=True), waits=[("sV", rc)] + p.meta.get("pbc_free", []), inc="sP")
                ev = p.emit("vector", lambda e, dsth=dsth, h=h, ncol=ncol, tc=tc: e.scalar_tensor_tensor(
                    b.BIGB[:, dsth + h * 512:dsth + (h + 1) * 512], b.FM[:, 0:512],
                    C[:, ncol:ncol + 1], b.pbc[0][:], ALU.mult, ALU.mult),
                    waits=[("sP", bcm)], inc="sV")
                p.meta["pbc_free"] = [("sV", ev)]
                p.meta["gla_fmfree"] = ev
        for tsq in range(4):
            for dn in range(2):
                bank = b.pb[2 + dn]
                for kc in range(NDC):
                    w_ = []
                    if kc == 0:
                        w_ = list(xw)
                        if p.meta.get(f"gla_vb{dn}") is not None:
                            w_.append(("sA", p.meta[f"gla_vb{dn}"]))
                    if kc == 0:
                        w_ = w_ + wvrdy
                    mm = p.emit("tensor", lambda e, tsq=tsq, dn=dn, kc=kc, bank=bank: e.matmul(
                        bank[:], b.XB[:, kc * TC + tsq * 128:kc * TC + tsq * 128 + 128],
                        b.WQK[:, kc * 1024 + dn * 512:kc * 1024 + (dn + 1) * 512],
                        start=(kc == 0), stop=(kc == NDC - 1)),
                        waits=w_, inc="sP" if kc == NDC - 1 else None)
                vv = p.emit("scalar", lambda e, tsq=tsq, dn=dn, bank=bank: e.activation(
                    b.BIGB[:, tsq * 1024 + dn * 512:tsq * 1024 + (dn + 1) * 512], bank[:], AF.Silu),
                    waits=[("sP", mm)], inc="sA")
                p.meta[f"gla_vb{dn}"] = vv
        vrdy = p.now("sA")
        for kc in range(NDC):
            w_ = [("sP", p.now("sP"))] if kc == 0 else []
            lwo = p.dma("gpsimd", b.WQK[:, kc * 1024:(kc + 1) * 1024],
                        aps["Wo"][kc * 128:(kc + 1) * 128, :], waits=w_, inc="sWO")
        wordy = [("sWO", lwo)]

        for tsq in range(4):
            for dn in range(2):
                bank = b.pb[2 + dn]
                for kc in range(NDC):
                    w_ = []
                    if kc == 0:
                        w_ = [("sA", vrdy), ("sV", p.meta.get("gla_sc_used", 0))]
                        if tsq == 0:
                            w_.append(("sV", p.now("sV")))
                    for term in range(3):
                        lhs_base = WR if term != 1 else WRLO
                        xsrc = (b.XB[:, kc * TC + tsq * 128:kc * TC + tsq * 128 + 128]
                                if term != 2 else
                                b.BIGB[:, XLO + kc * TC + tsq * 128:XLO + kc * TC + tsq * 128 + 128])
                        mm = p.emit("tensor", lambda e, lhs_base=lhs_base, xsrc=xsrc, dn=dn, kc=kc,
                                    term=term, bank=bank: e.matmul(
                            xsrc, b.BIGA[:, lhs_base + kc * 1024 + dn * 512:lhs_base + kc * 1024 + (dn + 1) * 512],
                            bank[:], start=False, stop=False)
                            if False else e.matmul(
                            bank[:], xsrc,
                            b.BIGA[:, lhs_base + kc * 1024 + dn * 512:lhs_base + kc * 1024 + (dn + 1) * 512],
                            start=(kc == 0 and term == 0), stop=(kc == NDC - 1 and term == 2)),
                            waits=w_ if term == 0 else (), inc="sP" if (kc == NDC - 1 and term == 2) else None)
                        w_ = []
                sg = p.emit("scalar", lambda e, dn=dn, bank=bank: e.activation(
                    b.FM[:, dn * 512:(dn + 1) * 512], bank[:], AF.Sigmoid),
                    waits=[("sP", mm)], inc="sA")
            sgr = sg
            for kc in range(NDC):
                w_ = [("sV", p.now("sV"))] if kc == 0 else []
                zm = p.emit("tensor", lambda e, tsq=tsq, kc=kc: e.matmul(
                    b.pb[4][:, 0:8], b.XB[:, kc * TC + tsq * 128:kc * TC + tsq * 128 + 128],
                    b.WS[:, kc * 8:(kc + 1) * 8], start=(kc == 0), stop=(kc == NDC - 1)),
                    waits=w_, inc="sP" if kc == NDC - 1 else None)
            S = b.S1
            p.emit("vector", lambda e: e.tensor_add(S[:, 640:648], b.pb[4][:, 0:8], C[:, 24:32]),
                   waits=[("sP", zm)])
            vz = p.emit("vector", lambda e: e.tensor_scalar_max(S[:, 648:656], S[:, 640:648], 0.0),
                        inc="sV")
            az = p.emit("scalar", lambda e: e.activation(S[:, 656:664], S[:, 640:648], AF.Abs),
                        waits=[("sV", vz)], inc="sA")
            ez = p.emit("scalar", lambda e: e.activation(S[:, 664:672], S[:, 656:664], AF.Exp,
                                                         scale=-1.0), inc="sA")
            lz = p.emit("scalar", lambda e: e.activation(S[:, 656:664], S[:, 664:672], AF.Ln,
                                                         bias=b.ONEF[:, 0:1]), inc="sA")
            p.emit("vector", lambda e: e.tensor_add(S[:, 648:656], S[:, 648:656], S[:, 656:664]),
                   waits=[("sA", lz)])
            fh = p.emit("vector", lambda e: e.tensor_mul(S[:, 640:648], S[:, 648:656], C[:, 16:24]),
                        inc="sV")
            S = b.S1
            for hg in range(2):
                hs = [hg * 4 + j for j in range(4)]
                SL = lambda h, j: slice(1024 + (h % 4) * 768 + j * 128,
                                        1024 + (h % 4) * 768 + (j + 1) * 128)
                FMB = lambda h, hg=hg: slice(hg * 512 + (h % 4) * 128, hg * 512 + (h % 4) * 128 + 128)
                first = True
                for h in hs:
                    fsl = slice(h * 128, (h + 1) * 128)
                    cur = b.FM[:, fsl]
                    for r in range(4):
                        dst = S[:, (r % 2) * 128:(r % 2) * 128 + 128]
                        w_ = []
                        if first:
                            w_ = [("sA", sgr), ("sV", fh), ("sP", p.meta.get("gla_t1p", 0))]
                            first = False
                        p.emit("vector", lambda e, cur=cur: e.max(out=S[:, 256:264], in_=cur), waits=w_)
                        p.emit("vector", lambda e, cur=cur, dst=dst: e.match_replace(
                            out=dst, in_to_replace=S[:, 256:264], in_values=cur, imm_value=0.0))
                        cur = dst
                    p.emit("vector", lambda e, fsl=fsl, cur=cur: e.tensor_sub(S[:, 384:512], b.FM[:, fsl], cur))
                    p.emit("vector", lambda e: e.reduce_sum(S[:, 264:265], S[:, 384:512], AX.X))
                    p.emit("vector", lambda e: e.tensor_scalar_add(S[:, 264:265], S[:, 264:265],
                                                                   b.EPSC[:, 2:3]))
                    p.emit("vector", lambda e: e.reciprocal(S[:, 265:266], S[:, 264:265]))
                    p.emit("vector", lambda e, h=h: e.tensor_mul(S[:, 266:267], S[:, 265:266],
                                                                 S[:, 640 + h:641 + h]))
                    tv = p.emit("vector", lambda e, h=h, SL=SL: e.tensor_single_scalar(
                        b.STB[:, SL(h, 0)], S[:, 384:512], S[:, 266:267], ALU.mult), inc="sV")
                w_ = [("sV", tv), ("sV", p.meta.get("ptr_reads", 0))]
                for h in hs:
                    tr = p.emit("tensor", lambda e, h=h, SL=SL: e.transpose(
                        b.ptr[h % 4], b.STB[:, SL(h, 0)], b.IDN[:]),
                        waits=w_ if h == hs[0] else (), inc="sP" if h == hs[-1] else None)
                for h in hs:
                    p.emit("vector", lambda e, h=h, SL=SL: e.tensor_copy(b.STB[:, SL(h, 0)], b.ptr[h % 4]),
                           waits=[("sP", tr)] if h == hs[0] else ())
                for h in hs:
                    sc_ = p.emit("vector", lambda e, h=h, SL=SL, FMB=FMB: e.tensor_tensor_scan(
                        b.FM[:, FMB(h)], b.ONEF[:, 0:1].broadcast_to([128, 128]), b.STB[:, SL(h, 0)],
                        0.0, ALU.mult, ALU.add), inc="sV" if h == hs[-1] else None)
                p.meta["ptr_reads"] = p.now("sV")
                for h in hs:
                    w_ = [("sV", sc_)] if h == hs[0] else []
                    p.emit("scalar", lambda e, h=h, SL=SL, FMB=FMB: e.activation(
                        b.STB[:, SL(h, 1)], b.FM[:, FMB(h)], AF.Exp), waits=w_)
                    p.emit("scalar", lambda e, h=h, SL=SL, FMB=FMB: e.activation(
                        b.STB[:, SL(h, 2)], b.FM[:, FMB(h)], AF.Exp, scale=-1.0))
                    p.emit("scalar", lambda e, h=h, FMB=FMB: e.activation(
                        S[:, 896 + h:897 + h], b.FM[:, FMB(h).stop - 1:FMB(h).stop], AF.Exp))
                    ea = p.emit("scalar", lambda e, h=h, SL=SL, FMB=FMB: e.activation(
                        b.FM[:, FMB(h)], b.STB[:, SL(h, 0)], AF.Exp),
                        inc="sA" if h == hs[-1] else None)
                for h in hs:
                    w_ = [("sA", ea)] if h == hs[0] else []
                    p.emit("vector", lambda e, h=h, FMB=FMB: e.tensor_scalar(
                        b.FM[:, FMB(h)], b.FM[:, FMB(h)], -1.0, 1.0, ALU.mult, ALU.add), waits=w_)
                    p.emit("vector", lambda e, h=h, SL=SL, tsq=tsq: e.tensor_mul(
                        b.STB[:, SL(h, 3)],
                        b.BIGB[:, QH + h * 512 + tsq * 128:QH + h * 512 + (tsq + 1) * 128],
                        b.STB[:, SL(h, 1)]))
                    p.emit("vector", lambda e, h=h, SL=SL, FMB=FMB, tsq=tsq: e.tensor_mul(
                        b.STB[:, SL(h, 4)],
                        b.BIGB[:, KH + h * 512 + tsq * 128:KH + h * 512 + (tsq + 1) * 128],
                        b.FM[:, FMB(h)]))
                    p.emit("vector", lambda e, h=h, SL=SL: e.tensor_mul(
                        b.STB[:, SL(h, 4)], b.STB[:, SL(h, 4)], b.STB[:, SL(h, 2)]))
                    t5 = p.emit("vector", lambda e, h=h, SL=SL: e.tensor_single_scalar(
                        b.STB[:, SL(h, 5)], b.STB[:, SL(h, 4)], S[:, 896 + h:897 + h], ALU.mult),
                        inc="sV" if h == hs[-1] else None)
                for h in hs:
                    w_ = []
                    if h == hs[0]:
                        w_ = [("sV", t5), ("sV", p.meta.get("gla_maskph", 0))]
                        if tsq == 0 and hg == 0:
                            w_.append(("sA", vrdy))
                    am = p.emit("tensor", lambda e, h=h, SL=SL: e.matmul(
                        b.pbc[1][:, (h % 4) * 128:(h % 4) * 128 + 128],
                        b.STB[:, SL(h, 4)], b.STB[:, SL(h, 3)], start=True, stop=True,
                        skip_group_check=True),
                        waits=w_, inc="sP" if h == hs[-1] else None)
                for h in hs:
                    t7 = p.emit("vector", lambda e, h=h, SL=SL: e.tensor_mul(
                        b.STB[:, SL(h, 4)], b.pbc[1][:, (h % 4) * 128:(h % 4) * 128 + 128], b.TRIL[:]),
                        waits=[("sP", am)] if h == hs[0] else (), inc="sV" if h == hs[-1] else None)
                p.meta["gla_maskph"] = t7
                for h in hs:
                    w_ = []
                    if h == hs[0]:
                        w_ = [("sV", t7), ("sV", p.meta.get("gla_oevph", 0))]
                        if tsq == 0 and hg == 0:
                            w_.append(("sA", vrdy))
                    p.emit("tensor", lambda e, h=h, SL=SL: e.matmul(
                        b.pb[0][:, (h % 4) * 128:(h % 4) * 128 + 128], b.STB[:, SL(h, 3)],
                        b.BIGB[:, S0B + h * 128:S0B + (h + 1) * 128],
                        start=True, stop=False, skip_group_check=True), waits=w_)
                    p.emit("tensor", lambda e, h=h, SL=SL, tsq=tsq: e.matmul(
                        b.pb[0][:, (h % 4) * 128:(h % 4) * 128 + 128], b.STB[:, SL(h, 4)],
                        b.BIGB[:, tsq * 1024 + h * 128:tsq * 1024 + (h + 1) * 128],
                        start=False, stop=True, skip_group_check=True))
                for h in hs:
                    w_ = [("sV", p.meta.get("ptr_reads", 0))] if h == hs[0] else []
                    tr2 = p.emit("tensor", lambda e, h=h, SL=SL: e.transpose(
                        b.ptr[h % 4], b.STB[:, SL(h, 5)], b.IDN[:]),
                        waits=w_, inc="sP" if h == hs[-1] else None)
                for h in hs:
                    t9 = p.emit("vector", lambda e, h=h, SL=SL: e.tensor_copy(
                        b.STB[:, SL(h, 5)], b.ptr[h % 4]),
                        waits=[("sP", tr2)] if h == hs[0] else (), inc="sV" if h == hs[-1] else None)
                p.meta["ptr_reads"] = t9
                for h in hs:
                    w_ = []
                    if h == hs[0]:
                        w_ = [("sV", t9), ("sV", p.meta.get("gla_stph", 0))]
                        if tsq == 0 and hg == 0:
                            w_.append(("sA", vrdy))
                    sm = p.emit("tensor", lambda e, h=h, SL=SL, tsq=tsq: e.matmul(
                        b.pb[1][:, (h % 4) * 128:(h % 4) * 128 + 128], b.STB[:, SL(h, 5)],
                        b.BIGB[:, tsq * 1024 + h * 128:tsq * 1024 + (h + 1) * 128],
                        start=True, stop=True, skip_group_check=True),
                        waits=w_, inc="sP" if h == hs[-1] else None)
                for h in hs:
                    p.emit("vector", lambda e, h=h: e.scalar_tensor_tensor(
                        b.S2[:, h * 128:(h + 1) * 128], b.S2[:, h * 128:(h + 1) * 128],
                        S[:, 896 + h:897 + h], b.pb[1][:, (h % 4) * 128:(h % 4) * 128 + 128],
                        ALU.mult, ALU.add),
                        waits=[("sP", sm)] if h == hs[0] else ())
                    p.emit("vector", lambda e, h=h: e.tensor_copy(
                        b.BIGB[:, S0B + h * 128:S0B + (h + 1) * 128], b.S2[:, h * 128:(h + 1) * 128]))
                    t11 = p.emit("vector", lambda e, h=h: e.tensor_copy(
                        b.FM[:, 1024 + h * 128:1024 + (h + 1) * 128],
                        b.pb[0][:, (h % 4) * 128:(h % 4) * 128 + 128]),
                        inc="sV" if h == hs[-1] else None)
                p.meta["gla_stph"] = t11
                p.meta["gla_oevph"] = t11
                p.meta["gla_t1p"] = p.now("sP")
            p.meta["gla_sc_used"] = p.now("sV")
            o = b.FM[:, 1024:2048]
            S = b.S1
            if aps.get("dbg") is not None and tc == 0 and tsq == 0:
                p.dma("gpsimd", aps["dbg2"], o, waits=[("sV", p.now("sV"))], inc="sDB")
            p.emit("vector", lambda e: e.tensor_mul(S[:, 0:1024], o, o),
                   waits=([("sDB", p.now("sDB"))] if aps.get("dbg") is not None else ()))
            p.emit("vector", lambda e: e.reduce_sum(S[:, 264:265], S[:, 0:1024], AX.X))
            gv = p.emit("vector", lambda e: e.tensor_scalar_mul(S[:, 264:265], S[:, 264:265],
                                                                1.0 / 1024.0), inc="sV")
            gs = p.emit("scalar", lambda e: e.activation(S[:, 265:266], S[:, 264:265], AF.Sqrt,
                                                         bias=b.EPSC[:, 0:1]),
                        waits=[("sV", gv)], inc="sA")
            gr = p.emit("vector", lambda e: e.reciprocal(S[:, 264:265], S[:, 265:266]),
                        waits=[("sA", gs)], inc="sV")
            go = p.emit("vector", lambda e, tsq=tsq: e.scalar_tensor_tensor(
                b.BIGB[:, 4096 + tsq * 1024:4096 + (tsq + 1) * 1024], o, S[:, 264:265],
                b.BIGB[:, GNW:GNW + 1024], ALU.mult, ALU.mult), waits=[("sV", gr)], inc="sV")
            for oc in range(8):
                pslot = b.ptr[4 + oc % 4]
                w_ = [("sV", go)]
                if p.meta.get("ptr_reads") is not None:
                    w_.append(("sV", p.meta["ptr_reads"]))
                tr3 = p.emit("tensor", lambda e, tsq=tsq, oc=oc, pslot=pslot: e.transpose(
                    pslot, b.BIGB[:, 4096 + tsq * 1024 + oc * 128:4096 + tsq * 1024 + (oc + 1) * 128],
                    b.IDN[:]), waits=w_, inc="sP")
                of = p.emit("vector", lambda e, tsq=tsq, oc=oc, pslot=pslot: e.tensor_copy(
                    b.BIGB[:, 8192 + oc * 512 + tsq * 128:8192 + oc * 512 + (tsq + 1) * 128], pslot),
                    waits=[("sP", tr3)], inc="sV")
                p.meta["ptr_reads"] = of
        ofrdy = p.now("sV")

        for dm in range(NDC):
            for kc in range(NDC):
                w_ = []
                if kc == 0:
                    w_ = [("sV", ofrdy)]
                    if p.meta.get("gla_wo") is not None:
                        w_.append(("sV", p.meta["gla_wo"]))
                wm = p.emit("tensor", lambda e, dm=dm, kc=kc: e.matmul(
                    b.pb[4][:], b.WQK[:, kc * 1024 + dm * 128:kc * 1024 + (dm + 1) * 128],
                    b.BIGB[:, 8192 + kc * 512:8192 + (kc + 1) * 512],
                    start=(kc == 0), stop=(kc == NDC - 1)),
                    waits=(w_ + (wordy if (dm == 0 and kc == 0) else [])),
                    inc="sP" if kc == NDC - 1 else None)
            w_ = [("sP", wm), ("sLX", p.now("sLX"))]
            if p.meta.get(f"gla_xo{dm % 2}") is not None:
                w_.append((f"sXo{dm % 2}", p.meta[f"gla_xo{dm % 2}"]))
            ev = p.emit("vector", lambda e, dm=dm: e.tensor_add(
                b.FM[:, dm % 2 * 512:dm % 2 * 512 + 512], b.pb[4][:],
                b.XC[:, dm * TC:(dm + 1) * TC]), waits=w_, inc="sV")
            p.meta["gla_wo"] = ev
            st = p.dma("gpsimd", aps["xout"][dm * 128:(dm + 1) * 128, tc * TC:(tc + 1) * TC],
                       b.FM[:, dm % 2 * 512:dm % 2 * 512 + 512], waits=[("sV", ev)],
                       inc=f"sXo{dm % 2}")
            p.meta[f"gla_xo{dm % 2}"] = st
        p.meta["pbc_free"] = [("sV", p.now("sV"))]
    return [(f"sXo{j}", p.now(f"sXo{j}")) for j in range(2)]


import ml_dtypes
from concourse.bass_utils import run_bass_kernel_spmd

BFNP = ml_dtypes.bfloat16
T_FULL = 4096


def pack_cols(v):
    return np.ascontiguousarray(v.reshape(-1, 128).T)


def make_rope_tables(T):
    dk = 64
    pos = np.arange(T, dtype=np.float64)
    kidx = np.arange(dk // 2, dtype=np.float64)
    ang = pos[:, None] / 10000.0 ** (2.0 * kidx / dk)
    cosT = np.zeros((128, T), np.float32)
    sinT = np.zeros((128, T), np.float32)
    for r in range(128):
        i = (r % 64) // 2
        cosT[r] = np.cos(ang[:, i])
        sinT[r] = (-1 if r % 2 == 0 else 1) * np.sin(ang[:, i])
    return cosT, sinT


def make_pswap():
    P = np.zeros((128, 128), np.float32)
    for i in range(64):
        P[2 * i, 2 * i + 1] = 1.0
        P[2 * i + 1, 2 * i] = 1.0
    return P.astype(BFNP)


def host_consts():
    return {
        "ident": np.eye(128, dtype=BFNP),
        "identf": np.eye(128, dtype=np.float32),
        "tril": np.tril(np.ones((128, 128), np.float32)).T.astype(BFNP),
    }


def host_gla(inp):
    g = lambda n: inp["L0_attn_a_" + n]
    return {
        "Wq": np.ascontiguousarray(g("Wq")), "Wk": np.ascontiguousarray(g("Wk")),
        "Wv": np.ascontiguousarray(g("Wv")), "Wr": np.ascontiguousarray(g("Wr")),
        "Wa": np.ascontiguousarray(g("Wa")), "Wo": np.ascontiguousarray(g("Wo")),
        "alog_r": np.ascontiguousarray(g("A_log").reshape(1, 8)),
        "dtb_r": np.ascontiguousarray(g("dt_bias").reshape(1, 8)),
        "qnw_c": np.ascontiguousarray(g("q_norm_w").reshape(128, 1)),
        "knw_c": np.ascontiguousarray(g("k_norm_w").reshape(128, 1)),
        "gnw_b": np.ascontiguousarray(np.broadcast_to(g("g_norm_w"), (128, 1024))),
    }


def host_ffn(inp):
    w1 = inp["L0_ffn_w1"]
    w2 = inp["L0_ffn_w2"]
    return {
        "w1T": np.ascontiguousarray(w1.T), "w1": np.ascontiguousarray(w1),
        "w2T": np.ascontiguousarray(w2.T), "w2": np.ascontiguousarray(w2),
        "b1c": pack_cols(inp["L0_ffn_b1"]), "b2c": pack_cols(inp["L0_ffn_b2"]),
    }


def host_attnb(inp):
    cols = np.zeros((128, 144), np.float32)
    cols[:, 0:8] = pack_cols(inp["L0_attn_b_a"])
    cols[:, 8:16] = pack_cols(inp["L0_attn_b_b"])
    cols[:, 16:24] = pack_cols(inp["L0_attn_b_conv_b"])
    cw = inp["L0_attn_b_conv_w"].reshape(1024, 3)
    for j in range(3):
        cols[:, 24 + j * 8:24 + (j + 1) * 8] = pack_cols(np.ascontiguousarray(cw[:, j]))
    cols[:, 48:56] = pack_cols(inp["L0_attn_b_norm_w"])
    cols[:, 56:64] = pack_cols(inp["L0_attn_b_mlp_norm_w"])
    cols[:, 64:96] = pack_cols(inp["L0_attn_b_b_gate"])
    cols[:, 96:128] = pack_cols(inp["L0_attn_b_b_up"])
    cols[:, 128:136] = pack_cols(inp["L0_attn_b_b_down"])
    return {
        "cols": cols,
        "w_gate": np.ascontiguousarray(inp["L0_attn_b_w_gate"]),
        "w_up": np.ascontiguousarray(inp["L0_attn_b_w_up"]),
        "w_down": np.ascontiguousarray(inp["L0_attn_b_w_down"]),
    }


def host_attna(inp):
    cosT, sinT = make_rope_tables(T_FULL)
    return {
        "wqT": np.ascontiguousarray(inp["L1_attn_a_Wq"].T),
        "wkT": np.ascontiguousarray(inp["L1_attn_a_Wk"].T),
        "wvT": np.ascontiguousarray(inp["L1_attn_a_Wv"].T),
        "woT": np.ascontiguousarray(inp["L1_attn_a_Wo"].T),
        "bqc": pack_cols(inp["L1_attn_a_bq"]),
        "bkc": pack_cols(inp["L1_attn_a_bk"]),
        "boc": pack_cols(inp["L1_attn_a_bo"]),
        "bvb": np.ascontiguousarray(np.broadcast_to(inp["L1_attn_a_bv"], (128, 1024))),
        "cosT": cosT, "sinT": sinT, "pswap": make_pswap(),
    }


def host_attnb2(inp):
    return {
        "wqkvT": np.ascontiguousarray(inp["L1_attn_b_w_qkv"].T),
        "woT": np.ascontiguousarray(inp["L1_attn_b_w_proj"].T),
    }


def host_moe(inp):
    return {
        "gate_w": np.ascontiguousarray(inp["L1_ffn_gate_w"]),
        "w1": np.ascontiguousarray(inp["L1_ffn_w1"]),
        "w2": np.ascontiguousarray(inp["L1_ffn_w2"]),
    }



def _declare(nc, host, T):
    aps = {}
    for name, arr in host.items():
        dt = BF16 if arr.dtype == BFNP else F32
        aps[name] = nc.dram_tensor(name, list(arr.shape), dt, kind="ExternalInput").ap()
    aps["xin"] = nc.dram_tensor("xin", [1024, T], F32, kind="ExternalInput").ap()
    aps["xout"] = nc.dram_tensor("xout", [1024, T], F32, kind="ExternalOutput").ap()
    return aps


def build_sub(kind, host, T):
    nc = bass.Bass("TRN2", target_bir_lowering=False)
    stack = ExitStack()
    stack.__enter__()
    p = Prog(nc, stack, T=T)
    b = Bufs(nc, stack)
    aps = _declare(nc, host, T)
    if kind == "gla":
        pass
    if kind == "ffn":
        aps["hd"] = nc.dram_tensor("hd", [4096, T], BF16, kind="Internal").ap()
    if kind == "attnb":
        aps["ud"] = nc.dram_tensor("ud", [1024, T], BF16, kind="Internal").ap()
        aps["hd"] = nc.dram_tensor("hd", [4096, T], BF16, kind="Internal").ap()
    if kind in ("attna", "attnb2"):
        aps["qd"] = nc.dram_tensor("qd", [1024, T], BF16, kind="Internal").ap()
        aps["xhd"] = nc.dram_tensor("xhd", [1024, T], BF16, kind="Internal").ap()
        if kind == "attnb2":
            aps["wqT"] = aps["wqkvT"][:, 0:1024]
            aps["wkT"] = aps["wqkvT"][:, 1024:2048]
            aps["wvT"] = aps["wqkvT"][:, 2048:3072]
    if kind == "moe":
        aps["accd"] = nc.dram_tensor("accd", [T, 1024], F32, kind="Internal").ap()
    emit_consts(p, b, aps)
    gate = p.meta["consts"]
    if kind == "gla":
        done = emit_gla(p, b, aps, gate=gate)
    elif kind == "ffn":
        done = emit_ffn_bin(p, b, aps, gate=gate)
    elif kind == "attnb":
        done = emit_attnb(p, b, aps, gate=gate)
    elif kind == "attna":
        done = emit_attn(p, b, aps, True, gate=gate)
    elif kind == "attnb2":
        done = emit_attn(p, b, aps, False, gate=gate)
    elif kind == "moe":
        done = emit_moe(p, b, aps, gate=gate)
    p.emit("sync", lambda e: e.nop(), waits=done)
    block = stack.enter_context(nc.Block())
    p.finalize(block)
    stack.__exit__(None, None, None)
    return nc



CHAIN = [
    ("gla", host_gla), ("ffn", host_ffn), ("attnb", host_attnb), ("ffn", host_ffn),
    ("attna", host_attna), ("moe", host_moe), ("attnb2", host_attnb2), ("moe", host_moe),
]


def build_fused(hosts, T, self_edges=False):
    nc = bass.Bass("TRN2", target_bir_lowering=False)
    stack = ExitStack()
    stack.__enter__()
    p = Prog(nc, stack, T=T)
    b = Bufs(nc, stack)
    aps_all = {}
    for kind, h in hosts.items():
        for name, arr in h.items():
            key = f"{kind}_{name}"
            dt = BF16 if arr.dtype == BFNP else F32
            aps_all[key] = nc.dram_tensor(key, list(arr.shape), dt, kind="ExternalInput").ap()
    cn = {"ident": np.eye(128), "identf": 0, "tril": 0}
    capz = {
        "ident": nc.dram_tensor("ident", [128, 128], BF16, kind="ExternalInput").ap(),
        "identf": nc.dram_tensor("identf", [128, 128], F32, kind="ExternalInput").ap(),
        "tril": nc.dram_tensor("tril", [128, 128], BF16, kind="ExternalInput").ap(),
    }
    xin = nc.dram_tensor("xin", [1024, T], F32, kind="ExternalInput").ap()
    xout = nc.dram_tensor("xout", [1024, T], F32, kind="ExternalOutput").ap()
    xd = [nc.dram_tensor(f"xd{i}", [1024, T], F32, kind="Internal").ap() for i in range(2)]
    scr = {
        "hd": nc.dram_tensor("hd", [4096, T], BF16, kind="Internal").ap(),
        "ud": nc.dram_tensor("ud", [1024, T], BF16, kind="Internal").ap(),
        "qd": nc.dram_tensor("qd", [1024, T], BF16, kind="Internal").ap(),
        "xhd": nc.dram_tensor("xhd", [1024, T], BF16, kind="Internal").ap(),
        "accd": nc.dram_tensor("accd", [T, 1024], F32, kind="Internal").ap(),
    }
    emit_consts(p, b, capz)
    done = p.meta["consts"]
    nsub = len(CHAIN)
    for i, (kind, _) in enumerate(CHAIN):
        aps = {name: aps_all[f"{kind}_{name}"] for name in hosts[kind]}
        aps.update(scr)
        if kind == "attnb2":
            aps["wqT"] = aps["wqkvT"][:, 0:1024]
            aps["wkT"] = aps["wqkvT"][:, 1024:2048]
            aps["wvT"] = aps["wqkvT"][:, 2048:3072]
        aps["xin"] = xin if i == 0 else xd[(i + 1) % 2]
        aps["xout"] = xout if i == nsub - 1 else xd[i % 2]
        p.barrier()
        if kind == "gla":
            done = emit_gla(p, b, aps)
        elif kind == "ffn":
            done = emit_ffn_bin(p, b, aps)
        elif kind == "attnb":
            done = emit_attnb(p, b, aps)
        elif kind == "attna":
            done = emit_attn(p, b, aps, True)
        elif kind == "attnb2":
            done = emit_attn(p, b, aps, False)
        elif kind == "moe":
            done = emit_moe(p, b, aps)
    p.emit("sync", lambda e: e.nop(), waits=done)
    block = stack.enter_context(nc.Block())
    p.finalize(block, self_edges=self_edges)
    stack.__exit__(None, None, None)
    return nc


def kernel(**inputs):
    T = T_FULL
    x = np.ascontiguousarray(
        np.asarray(inputs["L0_attn_a_hidden_states"], dtype=np.float32).transpose(0, 2, 1))
    consts = host_consts()
    hosts = {}
    for kind, hostf in CHAIN:
        if kind not in hosts:
            hosts[kind] = dict(hostf(inputs))
    nc = build_fused(hosts, T)
    flat = dict(consts)
    for kind, h in hosts.items():
        for name, arr in h.items():
            flat[f"{kind}_{name}"] = arr
    in_maps = [dict(flat, xin=np.ascontiguousarray(x[c])) for c in range(8)]
    res = run_bass_kernel_spmd(nc, in_maps, core_ids=list(range(8)))
    out = np.stack([np.asarray(res.results[c]["xout"], dtype=np.float32) for c in range(8)])
    return np.ascontiguousarray(out.transpose(0, 2, 1))


# version 04
# speedup vs baseline: 1.0472x; 1.0472x over previous
import numpy as np
from contextlib import ExitStack

import concourse.bass as bass
import concourse.mybir as mybir

F32 = mybir.dt.float32
BF16 = mybir.dt.bfloat16
AF = mybir.ActivationFunctionType
ALU = mybir.AluOpType
AX = mybir.AxisListType

ENGINES = ("sync", "vector", "scalar", "tensor", "gpsimd")

D = 1024
NDC = D // 128
TC = 512
GLA_TRUNC = False
GLA_TSTOP = None
GLA_ONEGROUP = False
GLA_G1STOP = None


class Prog:

    def __init__(self, nc, stack, T=4096):
        self.nc = nc
        self.stack = stack
        self.T = T
        self.NTC = T // TC
        self.ops = {e: [] for e in ENGINES}
        self._sems = {}
        self.counts = {}
        self.meta = {}

    def sem(self, name):
        if name not in self._sems:
            self._sems[name] = self.stack.enter_context(self.nc.semaphore(name))
            self.counts[name] = 0
        return name

    def emit(self, eng, fn, waits=(), inc=None, incn=1):
        at = None
        if inc is not None:
            self.sem(inc)
            self.counts[inc] += incn
            at = self.counts[inc]
        ws = []
        for s, v in waits:
            self.sem(s)
            if v and v > 0:
                ws.append((s, v))
        self.ops[eng].append([fn, tuple(ws), inc, incn, at])
        return at

    def dma(self, eng, out, in_, waits=(), inc=None, **kw):
        return self.emit(eng, lambda e: e.dma_start(out=out, in_=in_, **kw),
                         waits=waits, inc=inc, incn=16)

    def now(self, sem):
        self.sem(sem)
        return self.counts[sem]

    def barrier_counts(self):
        return [(s, c) for s, c in self.counts.items() if c > 0]

    def barrier(self):
        counts = self.barrier_counts()
        for eng in ENGINES:
            self.emit(eng, lambda e: e.nop(), waits=counts)

    def finalize(self, block, self_edges=False, edge_engines=None):
        if edge_engines is None:
            edge_engines = ENGINES if self_edges else ()
        edge_engines = tuple(set(edge_engines) | {"vector"})
        for e in edge_engines:
            self.sem(f"_se_{e}")
        sems = self._sems
        for eng_name in ENGINES:
            ops = self.ops[eng_name]

            def body(engine, ops=ops, eng_name=eng_name):
                se = eng_name in edge_engines
                last = None
                secount = 0
                for fn, waits, inc, incn, at in ops:
                    if se and last is not None:
                        engine.wait_ge(sems[last[0]], last[1])
                    for s, v in waits:
                        engine.wait_ge(sems[s], v)
                    ins = fn(engine)
                    if inc is not None:
                        ins.then_inc(sems[inc], incn)
                        last = (inc, at)
                    elif se:
                        secount += 1
                        ins.then_inc(sems[f"_se_{eng_name}"], 1)
                        last = (f"_se_{eng_name}", secount)
            getattr(block, eng_name)(body)


class Bufs:
    def __init__(self, nc, stack):
        en = stack.enter_context
        sb = lambda name, shape, dt: en(nc.sbuf_tensor(name, shape, dt))
        ps = lambda name, shape, dt: en(nc.psum_tensor(name, shape, dt))
        self.BIGA = sb("BIGA", [128, 32768], BF16)
        self.BIGB = sb("BIGB", [128, 33280], BF16)
        self.XC = sb("XC", [128, 4096], F32)
        self.XB = sb("XB", [128, 4096], BF16)
        self.STB = sb("STB", [128, 4096], BF16)
        self.FM = sb("FM", [128, 2048], F32)
        self.WS = sb("WS", [128, 4096], BF16)
        self.S1 = sb("S1", [128, 1024], F32)
        self.S2 = sb("S2", [128, 1024], F32)
        self.WQK = sb("WQK", [128, 8192], BF16)
        self.CT = self.WS[:, 512:1024]
        self.ST = self.WS[:, 1024:1536]
        self.COLS = sb("COLS", [128, 256], F32)
        self.ROWS = sb("ROWS", [1, 1024], F32)
        self.ONEB = sb("ONEB", [128, 8], BF16)
        self.ONEF = sb("ONEF", [128, 8], F32)
        self.EPSC = sb("EPSC", [128, 4], F32)
        self.IDN = sb("IDN", [128, 128], BF16)
        self.IDF = sb("IDF", [128, 128], F32)
        self.TRIL = sb("TRIL", [128, 128], BF16)
        self.pb = [ps(f"pb{i}", [128, 512], F32) for i in range(5)]
        self.pbc = [ps(f"pbc{i}", [128, 512], F32) for i in range(2)]
        self.PTR = ps("PTR", [128, 1024], BF16)
        self.ptr = [self.PTR[:, i * 128:(i + 1) * 128] for i in range(8)]


def emit_consts(p: Prog, b: Bufs, aps):
    p.emit("vector", lambda e: e.memset(b.ONEB[:], 1.0))
    p.emit("vector", lambda e: e.memset(b.ONEF[:], 1.0))
    p.emit("vector", lambda e: e.memset(b.EPSC[:, 0:1], 1e-5))
    p.emit("vector", lambda e: e.memset(b.EPSC[:, 1:2], 1e-6))
    p.emit("vector", lambda e: e.memset(b.EPSC[:, 2:3], 1e-9))
    p.emit("vector", lambda e: e.memset(b.EPSC[:, 3:4], 1e-8), inc="sV")
    p.dma("sync", b.IDN[:], aps["ident"], inc="sLC")
    p.dma("sync", b.IDF[:], aps["identf"], inc="sLC")
    p.dma("sync", b.TRIL[:], aps["tril"], inc="sLC")
    p.meta["consts"] = [("sV", p.now("sV")), ("sLC", p.now("sLC"))]


def emit_ln(p: Prog, b: Bufs, eps_slice, gate=(), dst=None, lo_dst=None):
    pbcw = p.meta.get("pbc_free", [])
    for k in range(NDC):
        sl = slice(k * TC, (k + 1) * TC)
        vW = p.emit("vector", lambda e, sl=sl: e.tensor_copy(b.XB[:, sl], b.XC[:, sl]),
                    waits=gate if k == 0 else (), inc="sV" if k == NDC - 1 else None)
        aW = p.emit("scalar", lambda e, sl=sl: e.activation(b.STB[:, sl], b.XC[:, sl], AF.Square),
                    waits=gate if k == 0 else (), inc="sA" if k == NDC - 1 else None)
    for k in range(NDC):
        sl = slice(k * TC, (k + 1) * TC)
        w_ = ([("sV", vW)] + pbcw) if k == 0 else ()
        p.emit("tensor", lambda e, sl=sl, k=k: e.matmul(
            b.pbc[0][0:1, :], b.ONEB[:, 0:1], b.XB[:, sl], start=(k == 0), stop=(k == NDC - 1)),
            waits=w_)
    for k in range(NDC):
        sl = slice(k * TC, (k + 1) * TC)
        w_ = [("sA", aW)] if k == 0 else ()
        mmW = p.emit("tensor", lambda e, sl=sl, k=k: e.matmul(
            b.pbc[1][0:1, :], b.ONEB[:, 0:1], b.STB[:, sl], start=(k == 0), stop=(k == NDC - 1)),
            waits=w_, inc="sP" if k == NDC - 1 else None)
    R = b.ROWS
    Sx = b.S1[0:1, 0:512]
    p.emit("vector", lambda e: e.tensor_scalar_mul(R[:, 0:512], b.pbc[0][0:1, :], 1.0 / D),
           waits=[("sP", mmW)])
    p.emit("vector", lambda e: e.tensor_scalar_mul(R[:, 512:1024], b.pbc[1][0:1, :], 1.0 / D))
    p.emit("vector", lambda e: e.tensor_mul(Sx, R[:, 0:512], R[:, 0:512]))
    rv = p.emit("vector", lambda e: e.tensor_sub(Sx, R[:, 512:1024], Sx), inc="sV")
    sq = p.emit("scalar", lambda e: e.activation(R[:, 512:1024], Sx, AF.Sqrt,
                                                 bias=eps_slice), waits=[("sV", rv)], inc="sA")
    rc = p.emit("vector", lambda e: e.reciprocal(Sx, R[:, 512:1024]),
                waits=[("sA", sq)], inc="sV")
    p.emit("tensor", lambda e: e.matmul(b.pbc[0][:], b.ONEF[0:1, 0:1].broadcast_to([1, 128]),
                                        R[:, 0:512], start=True, stop=True),
           waits=[("sV", rc)])
    bcW = p.emit("tensor", lambda e: e.matmul(b.pbc[1][:], b.ONEF[0:1, 0:1].broadcast_to([1, 128]),
                                              b.S1[0:1, 0:512], start=True, stop=True), inc="sP")
    for k in range(NDC):
        sl = slice(k * TC, (k + 1) * TC)
        out_ap = b.XB[:, sl] if dst is None else dst[k]
        p.emit("vector", lambda e, sl=sl: e.tensor_sub(b.S1[:, 0:512], b.XC[:, sl], b.pbc[0][:]),
               waits=[("sP", bcW)] if k == 0 else ())
        w = p.emit("vector", lambda e, out_ap=out_ap: e.tensor_mul(out_ap, b.S1[:, 0:512], b.pbc[1][:]),
                   inc="sV")
        if lo_dst is not None:
            p.emit("vector", lambda e: e.tensor_mul(b.S1[:, 512:1024], b.S1[:, 0:512], b.pbc[1][:]))
            w = p.emit("vector", lambda e, out_ap=out_ap, k=k: e.tensor_sub(
                lo_dst[k], b.S1[:, 512:1024], out_ap), inc="sV")
    p.meta["ln_done"] = [("sV", w)]
    p.meta["pbc_free"] = [("sV", w)]
    return p.meta["ln_done"]


def load_xc(p: Prog, b: Bufs, xin, tc, gate=()):
    for k in range(NDC):
        ldx = p.dma("sync", b.XC[:, k * TC:(k + 1) * TC],
                    xin[k * 128:(k + 1) * 128, tc * TC:(tc + 1) * TC],
                    waits=gate if k == 0 else (), inc="sLX")
    return [("sLX", ldx)]


def emit_binarize(p: Prog, b: Bufs, dst, dst_cols_per_k, wT_ap, w_ap, K, M, alpha_col0, gate=()):
    nK = K // 128
    nM = M // 128
    colchunk = 2048
    ncc = (M + colchunk - 1) // colchunk
    slot_ready = [None, None]
    si = 0
    for k in range(nK):
        for c in range(ncc):
            cs = slice(c * colchunk, min(M, (c + 1) * colchunk))
            ncols = cs.stop - cs.start
            xsl = slice(si * 2048, si * 2048 + ncols)
            waits = list(gate) if (k == 0 and c == 0) else []
            if slot_ready[si] is not None:
                waits.append(("sA", slot_ready[si]))
            ldw = p.dma("sync", b.XC[:, xsl], wT_ap[k * 128:(k + 1) * 128, cs],
                        waits=waits, inc=f"bsg{si}")
            dsl = slice(k * dst_cols_per_k + cs.start, k * dst_cols_per_k + cs.stop)
            slot_ready[si] = p.emit(
                "scalar", lambda e, dsl=dsl, xsl=xsl: e.activation(dst[:, dsl], b.XC[:, xsl], AF.Sign),
                waits=[(f"bsg{si}", ldw)], inc="sA")
            si ^= 1
    fslot = [None, None]
    fi = 0
    nc2 = max(1, K // 1024)
    for m in range(nM):
        for c2 in range(nc2):
            fsl = slice(fi * 1024, (fi + 1) * 1024)
            waits = list(gate) if (m == 0 and c2 == 0) else []
            if fslot[fi] is not None:
                waits.append(("sV", fslot[fi]))
            ldn = p.dma("sync", b.FM[:, fsl], w_ap[m * 128:(m + 1) * 128, c2 * 1024:(c2 + 1) * 1024],
                        waits=waits, inc=f"bna{fi}")
            col = slice(alpha_col0 + m, alpha_col0 + m + 1)
            if c2 == 0:
                fslot[fi] = p.emit("vector", lambda e, fsl=fsl, col=col: e.tensor_reduce(
                    b.COLS[:, col], b.FM[:, fsl], AX.X, ALU.add, apply_absolute_value=True),
                    waits=[(f"bna{fi}", ldn)], inc="sV")
            else:
                p.emit("vector", lambda e, fsl=fsl: e.tensor_reduce(
                    b.S2[:, 0:1], b.FM[:, fsl], AX.X, ALU.add, apply_absolute_value=True),
                    waits=[(f"bna{fi}", ldn)])
                fslot[fi] = p.emit("vector", lambda e, col=col: e.tensor_add(
                    b.COLS[:, col], b.COLS[:, col], b.S2[:, 0:1]), inc="sV")
            fi ^= 1
    w = p.emit("vector", lambda e: e.tensor_scalar_mul(
        b.COLS[:, alpha_col0:alpha_col0 + nM], b.COLS[:, alpha_col0:alpha_col0 + nM], 1.0 / K),
        inc="sV")
    return [("sV", w), ("sA", p.now("sA"))]


def emit_ffn_bin(p: Prog, b: Bufs, aps, gate=()):
    NTC = p.NTC
    g1 = emit_binarize(p, b, b.BIGA, 4096, aps["w1T"], aps["w1"], 1024, 4096, 0, gate=gate)
    g2 = emit_binarize(p, b, b.BIGB, 1024, aps["w2T"], aps["w2"], 4096, 1024, 64,
                       gate=[("sA", p.now("sA")), ("sV", p.now("sV"))])
    ld1 = p.dma("sync", b.COLS[:, 96:128], aps["b1c"], inc="sLB")
    ld2 = p.dma("sync", b.COLS[:, 88:96], aps["b2c"], inc="sLB")
    wgt = g1 + g2 + [("sLB", ld2)]

    bank_free = [None, None]
    stb_free = [None] * 8
    last_mm = None
    for tc in range(NTC):
        xg = load_xc(p, b, aps["xin"], tc,
                     gate=(wgt if tc == 0 else [("sV", p.now("sV")), ("sA", p.now("sA"))]))
        lng = xg + ([("sP", last_mm)] if last_mm is not None else [])
        lng += [(f"sHo{s}", p.now(f"sHo{s}")) for s in range(8)]
        xw = emit_ln(p, b, b.EPSC[0:1, 0:1], gate=lng)
        for m in range(32):
            bank = b.pb[m % 2]
            for k in range(NDC):
                w_ = []
                if k == 0:
                    w_ = list(xw)
                    if bank_free[m % 2] is not None:
                        w_.append(("sA", bank_free[m % 2]))
                last_mm = p.emit("tensor", lambda e, m=m, k=k, bank=bank: e.matmul(
                    bank[:], b.BIGA[:, k * 4096 + m * 128:k * 4096 + (m + 1) * 128],
                    b.XB[:, k * TC:(k + 1) * TC], start=(k == 0), stop=(k == NDC - 1)),
                    waits=w_, inc="sP" if k == NDC - 1 else None)
            sl = slice((m % 8) * TC, (m % 8 + 1) * TC)
            ew = [("sP", last_mm)]
            if stb_free[m % 8] is not None:
                ew.append((f"sHo{m % 8}", stb_free[m % 8]))
            ev = p.emit("scalar", lambda e, sl=sl, m=m, bank=bank: e.activation(
                b.STB[:, sl], bank[:], AF.Gelu,
                bias=b.COLS[:, 96 + m:97 + m], scale=b.COLS[:, m:m + 1]),
                waits=ew, inc="sA")
            bank_free[m % 2] = ev
            st = p.dma("gpsimd", aps["hd"][m * 128:(m + 1) * 128, tc * TC:(tc + 1) * TC],
                       b.STB[:, sl], waits=[("sA", ev)], inc=f"sHo{m % 8}")
            stb_free[m % 8] = st

    all_h = [(f"sHo{s}", p.now(f"sHo{s}")) for s in range(8)]
    fm_free = [None] * 4
    stbD_free = [None] * 8
    first = True
    for tc in range(NTC):
        xg = load_xc(p, b, aps["xin"], tc,
                     gate=[("sV", p.now("sV")), ("sA", p.now("sA"))])
        for grp in range(2):
            kcW = None
            for kc in range(32):
                sl = slice((kc % 8) * TC, (kc % 8 + 1) * TC)
                w_ = list(all_h) if (tc == 0 and grp == 0 and kc < 8) else []
                first = False
                if stbD_free[kc % 8] is not None:
                    w_.append(("sP", stbD_free[kc % 8]))
                ldh = p.dma("sync", b.STB[:, sl],
                            aps["hd"][kc * 128:(kc + 1) * 128, tc * TC:(tc + 1) * TC],
                            waits=w_, inc=f"sLH{kc % 8}")
                for j in range(4):
                    dm = grp * 4 + j
                    ww = []
                    if j == 0:
                        ww.append((f"sLH{kc % 8}", ldh))
                    if kc == 0 and j == 0:
                        if fm_free[0] is not None:
                            ww.append(("sV", p.meta.get("ffnD_evac", 0)))
                        else:
                            ww.append(("sA", p.now("sA")))
                    kcW = p.emit("tensor", lambda e, kc=kc, dm=dm, j=j, sl=sl: e.matmul(
                        b.pb[j][:], b.BIGB[:, kc * 1024 + dm * 128:kc * 1024 + (dm + 1) * 128],
                        b.STB[:, sl], start=(kc == 0), stop=(kc == 31)),
                        waits=ww, inc="sP" if j == 3 else None)
                stbD_free[kc % 8] = kcW
            for j in range(4):
                dm = grp * 4 + j
                fs = slice(j * TC, (j + 1) * TC)
                w_ = [("sP", kcW)] + xg
                if fm_free[j] is not None:
                    w_.append((f"sXo{j}", fm_free[j]))
                p.emit("vector", lambda e, dm=dm, j=j, fs=fs: e.tensor_scalar(
                    b.FM[:, fs], b.pb[j][:], b.COLS[:, 64 + dm:65 + dm],
                    b.COLS[:, 88 + dm:89 + dm], ALU.mult, ALU.add), waits=w_)
                ev = p.emit("vector", lambda e, dm=dm, fs=fs: e.tensor_add(
                    b.FM[:, fs], b.FM[:, fs], b.XC[:, dm * TC:(dm + 1) * TC]), inc="sV")
                p.meta["ffnD_evac"] = ev
                st = p.dma("gpsimd", aps["xout"][dm * 128:(dm + 1) * 128, tc * TC:(tc + 1) * TC],
                           b.FM[:, fs], waits=[("sV", ev)], inc=f"sXo{j}")
                fm_free[j] = st
    return [(f"sXo{j}", p.now(f"sXo{j}")) for j in range(4)]


def emit_moe(p: Prog, b: Bufs, aps, gate=()):
    NTC = p.NTC
    T = p.T
    ntsub = T // 128

    for tc in range(NTC):
        xg = load_xc(p, b, aps["xin"], tc,
                     gate=(list(gate) + p.meta.get("pbc_free", []) if tc == 0
                           else [("sV", p.now("sV")), ("sA", p.now("sA"))]))
        dst = [b.BIGA[:, k * T + tc * TC:k * T + (tc + 1) * TC] for k in range(NDC)]
        xw = emit_ln(p, b, b.EPSC[0:1, 0:1], gate=xg, dst=dst)

    ldg = p.dma("gpsimd", b.WS[:, 0:64].rearrange("p (k c) -> p k c", k=8),
                aps["gate_w"].rearrange("(k p) c -> p k c", p=128), inc="sLG")
    for ts in range(ntsub):
        w_ = (list(xw) + [("sLG", ldg)]) if ts == 0 else [("sV", p.meta.get("moe_top2", 0))]
        for k in range(NDC):
            mm = p.emit("tensor", lambda e, k=k, ts=ts: e.matmul(
                b.pb[4][:, 0:8], b.BIGA[:, k * T + ts * 128:k * T + (ts + 1) * 128],
                b.WS[:, k * 8:(k + 1) * 8], start=(k == 0), stop=(k == NDC - 1)),
                waits=w_ if k == 0 else (), inc="sP" if k == NDC - 1 else None)
        S = b.S1
        p.emit("vector", lambda e: e.reduce_max(S[:, 0:1], b.pb[4][:, 0:8], AX.X),
               waits=[("sP", mm)])
        p.emit("vector", lambda e: e.tensor_single_scalar(S[:, 8:16], b.pb[4][:, 0:8], S[:, 0:1], ALU.is_ge))
        p.emit("vector", lambda e: e.scalar_tensor_tensor(S[:, 16:24], S[:, 8:16], -1e30, b.pb[4][:, 0:8], ALU.mult, ALU.add))
        p.emit("vector", lambda e: e.reduce_max(S[:, 1:2], S[:, 16:24], AX.X))
        sv = p.emit("vector", lambda e: e.tensor_sub(S[:, 2:3], S[:, 0:1], S[:, 1:2]), inc="sV")
        sg = p.emit("scalar", lambda e: e.activation(S[:, 3:4], S[:, 2:3], AF.Sigmoid),
                    waits=[("sV", sv)], inc="sA")
        p.emit("vector", lambda e: e.tensor_scalar(S[:, 4:5], S[:, 3:4], -1.0, 1.0, ALU.mult, ALU.add),
               waits=[("sA", sg)])
        p.emit("vector", lambda e: e.tensor_single_scalar(S[:, 24:32], S[:, 16:24], S[:, 1:2], ALU.is_ge))
        p.emit("vector", lambda e: e.tensor_single_scalar(S[:, 8:16], S[:, 8:16], S[:, 3:4], ALU.mult))
        p.emit("vector", lambda e: e.tensor_single_scalar(S[:, 24:32], S[:, 24:32], S[:, 4:5], ALU.mult))
        tv = p.emit("vector", lambda e, ts=ts: e.tensor_add(b.S2[:, ts * 8:(ts + 1) * 8], S[:, 8:16], S[:, 24:32]),
                    inc="sV")
        p.meta["moe_top2"] = tv

    W1C = 16384
    fm_free = [None] * 4
    for e in range(8):
        wld = [("sP", p.now("sP"))] if e > 0 else list(gate)
        for k in range(NDC):
            p.dma("gpsimd", b.BIGB[:, k * 2048:(k + 1) * 2048],
                  aps["w1"][e, k * 128:(k + 1) * 128, :], waits=wld if k == 0 else (),
                  inc="sLE")
        for hm in range(16):
            l2 = p.dma("gpsimd", b.BIGB[:, W1C + hm * 1024:W1C + (hm + 1) * 1024],
                       aps["w2"][e, hm * 128:(hm + 1) * 128, :], inc="sLE")
        ewg = [("sLE", l2)]
        prev_acc = [(f"sAcc{j}", p.now(f"sAcc{j}")) for j in range(4)]
        first_store = True
        for tc in range(NTC):
            for hm in range(16):
                bank = b.pb[hm % 2]
                for k in range(NDC):
                    w_ = []
                    if k == 0:
                        if tc == 0 and hm == 0:
                            w_ += list(ewg)
                            if e == 0:
                                w_ += list(xw) + [("sV", p.meta["moe_top2"])]
                        if p.meta.get(f"moe_bank{hm % 2}") is not None:
                            w_.append(("sA", p.meta[f"moe_bank{hm % 2}"]))
                    mm = p.emit("tensor", lambda e_, hm=hm, k=k, tc=tc, bank=bank: e_.matmul(
                        bank[:], b.BIGB[:, k * 2048 + hm * 128:k * 2048 + (hm + 1) * 128],
                        b.BIGA[:, k * T + tc * TC:k * T + (tc + 1) * TC],
                        start=(k == 0), stop=(k == NDC - 1)),
                        waits=w_, inc="sP" if k == NDC - 1 else None)
                dstap = (b.STB[:, (hm % 8) * TC:(hm % 8 + 1) * TC] if hm < 8 else
                         b.XB[:, (hm - 8) * TC:(hm - 8 + 1) * TC])
                ev = p.emit("scalar", lambda e_, dstap=dstap, bank=bank: e_.activation(
                    dstap, bank[:], AF.Silu), waits=[("sP", mm)], inc="sA")
                p.meta[f"moe_bank{hm % 2}"] = ev
            hready = p.now("sA")
            for tsq in range(4):
                ts_global = tc * 4 + tsq
                for dn in range(2):
                    bank = b.pb[2 + dn]
                    for hm in range(16):
                        w_ = []
                        if hm == 0:
                            w_ = [("sA", hready)]
                            if p.meta.get(f"moe_ybank{dn}") is not None:
                                w_.append(("sV", p.meta[f"moe_ybank{dn}"]))
                        lhs = (b.STB[:, (hm % 8) * TC + tsq * 128:(hm % 8) * TC + tsq * 128 + 128]
                               if hm < 8 else
                               b.XB[:, (hm - 8) * TC + tsq * 128:(hm - 8) * TC + tsq * 128 + 128])
                        ymm = p.emit("tensor", lambda e_, lhs=lhs, hm=hm, dn=dn, bank=bank: e_.matmul(
                            bank[:], lhs,
                            b.BIGB[:, W1C + hm * 1024 + dn * 512:W1C + hm * 1024 + (dn + 1) * 512],
                            start=(hm == 0), stop=(hm == 15)),
                            waits=w_, inc="sP" if hm == 15 else None)
                    j = (tsq * 2 + dn) % 4
                    fs = slice(j * TC, (j + 1) * TC)
                    cidx = ts_global * 8 + e
                    w_ = [("sP", ymm)]
                    if fm_free[j] is not None:
                        w_.append((f"sAcc{j}", fm_free[j]))
                    yev = p.emit("vector", lambda e_, fs=fs, cidx=cidx, bank=bank: e_.tensor_single_scalar(
                        b.FM[:, fs], bank[:], b.S2[:, cidx:cidx + 1], ALU.mult),
                        waits=w_, inc="sV")
                    p.meta[f"moe_ybank{dn}"] = yev
                    stw = [("sV", yev)]
                    if first_store and e > 0:
                        stw += prev_acc
                        first_store = False
                    st = p.dma("gpsimd",
                               aps["accd"][ts_global * 128:(ts_global + 1) * 128, dn * 512:(dn + 1) * 512],
                               b.FM[:, fs], waits=stw,
                               inc=f"sAcc{j}", accum_op=(ALU.bypass if e == 0 else ALU.add))
                    fm_free[j] = st

    allacc = [(f"sAcc{j}", p.now(f"sAcc{j}")) for j in range(4)]
    engines_done = [("sV", p.now("sV")), ("sA", p.now("sA"))]
    s1_free = [None] * 8
    xcf_free = [None] * 8
    fmF_free = [None] * 4
    pbF_free = [None] * 4
    ti = 0
    for dc in range(NDC):
        for tt in range(T // 128):
            s1s = ti % 8
            xcs = ti % 8
            j = ti % 4
            w_ = (list(allacc) + list(engines_done)) if ti == 0 else []
            if s1_free[s1s] is not None:
                w_.append(("sP", s1_free[s1s]))
            lda = p.dma("sync", b.S1[:, s1s * 128:(s1s + 1) * 128],
                        aps["accd"][tt * 128:(tt + 1) * 128, dc * 128:(dc + 1) * 128],
                        waits=w_, inc=f"sFA{s1s}")
            w_ = []
            if xcf_free[xcs] is not None:
                w_.append(("sV", xcf_free[xcs]))
            ldx = p.dma("sync", b.XC[:, xcs * 128:(xcs + 1) * 128],
                        aps["xin"][dc * 128:(dc + 1) * 128, tt * 128:(tt + 1) * 128],
                        waits=w_, inc=f"sFX{xcs}")
            w_ = [(f"sFA{s1s}", lda)]
            if ti < 4:
                w_ += engines_done
            if pbF_free[j] is not None:
                w_.append(("sV", pbF_free[j]))
            tr = p.emit("tensor", lambda e_, s1s=s1s, j=j: e_.transpose(
                b.pb[j][:, 0:128], b.S1[:, s1s * 128:(s1s + 1) * 128], b.IDF[:]),
                waits=w_, inc="sP")
            s1_free[s1s] = tr
            w_ = [("sP", tr), (f"sFX{xcs}", ldx)]
            if fmF_free[j] is not None:
                w_.append((f"sFO{j}", fmF_free[j]))
            ad = p.emit("vector", lambda e_, j=j, xcs=xcs: e_.tensor_add(
                b.FM[:, j * 128:(j + 1) * 128], b.pb[j][:, 0:128],
                b.XC[:, xcs * 128:(xcs + 1) * 128]), waits=w_, inc="sV")
            pbF_free[j] = ad
            xcf_free[xcs] = ad
            st = p.dma("gpsimd", aps["xout"][dc * 128:(dc + 1) * 128, tt * 128:(tt + 1) * 128],
                       b.FM[:, j * 128:(j + 1) * 128], waits=[("sV", ad)], inc=f"sFO{j}")
            fmF_free[j] = st
            ti += 1
    return [(f"sFO{j}", p.now(f"sFO{j}")) for j in range(4)]


def emit_rms(p: Prog, b: Bufs, src, dst, wcol, eps_slice, gate=()):
    pbcw = p.meta.get("pbc_free", [])
    for k in range(NDC):
        sl = slice(k * TC, (k + 1) * TC)
        aW = p.emit("scalar", lambda e, sl=sl, k=k: e.activation(b.STB[:, sl], src(k), AF.Square),
                    waits=gate if k == 0 else (), inc="sA" if k == NDC - 1 else None)
    for k in range(NDC):
        sl = slice(k * TC, (k + 1) * TC)
        w_ = ([("sA", aW)] + pbcw) if k == 0 else ()
        mmW = p.emit("tensor", lambda e, sl=sl, k=k: e.matmul(
            b.pbc[0][0:1, :], b.ONEB[:, 0:1], b.STB[:, sl], start=(k == 0), stop=(k == NDC - 1)),
            waits=w_, inc="sP" if k == NDC - 1 else None)
    R = b.ROWS
    sv = p.emit("vector", lambda e: e.tensor_scalar_mul(R[:, 0:512], b.pbc[0][0:1, :], 1.0 / D),
                waits=[("sP", mmW)], inc="sV")
    sq = p.emit("scalar", lambda e: e.activation(R[:, 512:1024], R[:, 0:512], AF.Sqrt,
                                                 bias=eps_slice), waits=[("sV", sv)], inc="sA")
    rc = p.emit("vector", lambda e: e.reciprocal(R[:, 0:512], R[:, 512:1024]),
                waits=[("sA", sq)], inc="sV")
    bcW = p.emit("tensor", lambda e: e.matmul(b.pbc[0][:], b.ONEF[0:1, 0:1].broadcast_to([1, 128]),
                                              R[:, 0:512], start=True, stop=True),
                 waits=[("sV", rc)], inc="sP")
    for k in range(NDC):
        w = p.emit("vector", lambda e, k=k: e.scalar_tensor_tensor(
            dst(k), src(k), wcol(k), b.pbc[0][:], ALU.mult, ALU.mult),
            waits=[("sP", bcW)] if k == 0 else (), inc="sV")
    p.meta["pbc_free"] = [("sV", w)]
    return [("sV", w)]


def emit_attnb(p: Prog, b: Bufs, aps, gate=()):
    NTC = p.NTC
    T = p.T
    C = b.COLS
    ldc = p.dma("sync", C[:, 0:144], aps["cols"], waits=list(gate), inc="sLB")
    th = p.emit("scalar", lambda e: e.activation(C[:, 0:8], C[:, 0:8], AF.Tanh),
                waits=[("sLB", ldc)], inc="sA")

    for tc in range(NTC):
        xg = load_xc(p, b, aps["xin"], tc,
                     gate=(list(gate) + p.meta.get("pbc_free", []) if tc == 0
                           else [("sV", p.now("sV")), ("sA", p.now("sA"))]))
        dstb = [b.BIGB[:, k * T + tc * TC:k * T + (tc + 1) * TC] for k in range(NDC)]
        xw = emit_ln(p, b, b.EPSC[0:1, 0:1], gate=xg, dst=dstb)
        yw = emit_rms(p, b,
                      src=lambda k, tc=tc: b.BIGB[:, k * T + tc * TC:k * T + (tc + 1) * TC],
                      dst=lambda k, tc=tc: b.BIGA[:, k * T + tc * TC:k * T + (tc + 1) * TC],
                      wcol=lambda k: C[:, 48 + k:49 + k],
                      eps_slice=b.EPSC[0:1, 1:2], gate=xw)

    ugate = [("sA", th)] + yw
    for k in range(NDC):
        ys = lambda lo, hi, k=k: b.BIGA[:, k * T + lo:k * T + hi]
        bw = p.emit("vector", lambda e, k=k, ys=ys: e.tensor_single_scalar(
            b.STB[:, 0:T], ys(0, T), C[:, 8 + k:9 + k], ALU.mult),
            waits=ugate if k == 0 else (), inc="sV")
        sc = p.emit("vector", lambda e, k=k: e.tensor_tensor_scan(
            b.XB[:, 0:T], C[:, 0 + k:1 + k].broadcast_to([128, T]), b.STB[:, 0:T],
            0.0, ALU.mult, ALU.add), inc="sV")
        for tc in range(NTC):
            lo, hi = tc * TC, (tc + 1) * TC
            fs = slice((tc % 2) * TC, (tc % 2) * TC + TC)
            fw = []
            if p.meta.get(f"ab_fm{tc % 2}") is not None:
                fw.append((f"sUo{tc % 2}", p.meta[f"ab_fm{tc % 2}"]))
            if tc == 0:
                p.emit("vector", lambda e, fs=fs: e.memset(b.FM[:, fs.start:fs.start + 1], 0.0),
                       waits=fw)
                p.emit("vector", lambda e, k=k, fs=fs, lo=lo, hi=hi, ys=ys: e.tensor_single_scalar(
                    b.FM[:, fs.start + 1:fs.stop], ys(lo, hi - 1), C[:, 24 + k:25 + k], ALU.mult))
            else:
                p.emit("vector", lambda e, k=k, fs=fs, lo=lo, hi=hi, ys=ys: e.tensor_single_scalar(
                    b.FM[:, fs], ys(lo - 1, hi - 1), C[:, 24 + k:25 + k], ALU.mult),
                    waits=fw)
            p.emit("vector", lambda e, k=k, fs=fs, lo=lo, hi=hi, ys=ys: e.scalar_tensor_tensor(
                b.FM[:, fs], ys(lo, hi), C[:, 32 + k:33 + k], b.FM[:, fs], ALU.mult, ALU.add))
            if tc == NTC - 1:
                p.emit("vector", lambda e, k=k, fs=fs, lo=lo, hi=hi, ys=ys: e.scalar_tensor_tensor(
                    b.FM[:, fs.start:fs.stop - 1], ys(lo + 1, hi), C[:, 40 + k:41 + k],
                    b.FM[:, fs.start:fs.stop - 1], ALU.mult, ALU.add))
            else:
                p.emit("vector", lambda e, k=k, fs=fs, lo=lo, hi=hi, ys=ys: e.scalar_tensor_tensor(
                    b.FM[:, fs], ys(lo + 1, hi + 1), C[:, 40 + k:41 + k], b.FM[:, fs],
                    ALU.mult, ALU.add))
            p.emit("vector", lambda e, k=k, fs=fs: e.tensor_scalar_add(
                b.FM[:, fs], b.FM[:, fs], C[:, 16 + k:17 + k]))
            p.emit("vector", lambda e, k=k, fs=fs, lo=lo, hi=hi: e.tensor_add(
                b.FM[:, fs], b.FM[:, fs], b.BIGB[:, k * T + lo:k * T + hi]))
            p.emit("vector", lambda e, fs=fs, lo=lo, hi=hi: e.tensor_add(
                b.FM[:, fs], b.FM[:, fs], b.XB[:, lo:hi]))
            uv = p.emit("vector", lambda e, k=k, fs=fs, lo=lo, hi=hi: e.tensor_copy(
                b.BIGB[:, k * T + lo:k * T + hi], b.FM[:, fs]), inc="sV")
            st = p.dma("gpsimd", aps["ud"][k * 128:(k + 1) * 128, lo:hi],
                       b.BIGB[:, k * T + lo:k * T + hi], waits=[("sV", uv)], inc=f"sUo{tc % 2}")
            p.meta[f"ab_fm{tc % 2}"] = st
    uall = [(f"sUo{s}", p.now(f"sUo{s}")) for s in range(2)]

    wg0 = [("sV", p.now("sV"))]
    for k in range(NDC):
        lg = p.dma("gpsimd", b.BIGA[:, k * 4096:(k + 1) * 4096],
                   aps["w_gate"][k * 128:(k + 1) * 128, :], waits=wg0 if k == 0 else (),
                   inc="sLWG")
    wu0 = uall
    for k in range(NDC):
        lu = p.dma("gpsimd", b.BIGB[:, k * 4096:(k + 1) * 4096],
                   aps["w_up"][k * 128:(k + 1) * 128, :], waits=wu0 if k == 0 else (),
                   inc="sLWU")
    wboth = [("sLWG", lg), ("sLWU", lu)]
    bank_free = {}
    stb_free = [None] * 8
    last_mm = None
    for tc in range(NTC):
        ulw = [("sV", p.now("sV")), ("sA", p.now("sA"))]
        ulw += [(f"sHo{s}", p.now(f"sHo{s}")) for s in range(8)]
        if tc == 0:
            ulw += uall
        for k in range(NDC):
            ldu = p.dma("sync", b.STB[:, k * TC:(k + 1) * TC],
                        aps["ud"][k * 128:(k + 1) * 128, tc * TC:(tc + 1) * TC],
                        waits=ulw if k == 0 else (), inc="sLU")
        pbcw = p.meta.get("pbc_free", [])
        for k in range(NDC):
            sl = slice(k * TC, (k + 1) * TC)
            aW = p.emit("scalar", lambda e, sl=sl: e.activation(b.XB[:, sl], b.STB[:, sl], AF.Square),
                        waits=([("sLU", ldu), ("sP", p.now("sP"))] if k == 0 else ()),
                        inc="sA" if k == NDC - 1 else None)
        for k in range(NDC):
            sl = slice(k * TC, (k + 1) * TC)
            w_ = ([("sA", aW)] + pbcw) if k == 0 else ()
            mmW = p.emit("tensor", lambda e, sl=sl, k=k: e.matmul(
                b.pbc[0][0:1, :], b.ONEB[:, 0:1], b.XB[:, sl], start=(k == 0), stop=(k == NDC - 1)),
                waits=w_, inc="sP" if k == NDC - 1 else None)
        R = b.ROWS
        sv = p.emit("vector", lambda e: e.tensor_scalar_mul(R[:, 0:512], b.pbc[0][0:1, :], 1.0 / D),
                    waits=[("sP", mmW)], inc="sV")
        sq = p.emit("scalar", lambda e: e.activation(R[:, 512:1024], R[:, 0:512], AF.Sqrt,
                                                     bias=b.EPSC[0:1, 1:2]), waits=[("sV", sv)], inc="sA")
        rc = p.emit("vector", lambda e: e.reciprocal(R[:, 0:512], R[:, 512:1024]),
                    waits=[("sA", sq)], inc="sV")
        bcW = p.emit("tensor", lambda e: e.matmul(b.pbc[0][:], b.ONEF[0:1, 0:1].broadcast_to([1, 128]),
                                                  R[:, 0:512], start=True, stop=True),
                     waits=[("sV", rc)], inc="sP")
        for k in range(NDC):
            sl = slice(k * TC, (k + 1) * TC)
            zw = p.emit("vector", lambda e, sl=sl, k=k: e.scalar_tensor_tensor(
                b.XB[:, sl], b.STB[:, sl], C[:, 56 + k:57 + k], b.pbc[0][:], ALU.mult, ALU.mult),
                waits=[("sP", bcW)] if k == 0 else (), inc="sV")
        p.meta["pbc_free"] = [("sV", zw)]
        for m in range(32):
            bg, bu = b.pb[(m % 2) * 2], b.pb[(m % 2) * 2 + 1]
            for k in range(NDC):
                w_ = []
                if k == 0:
                    w_ = [("sV", zw)] + (wboth if (tc == 0 and m == 0) else [])
                    if bank_free.get((m % 2, "g")) is not None:
                        w_.append(("sA", bank_free[(m % 2, "g")]))
                p.emit("tensor", lambda e, m=m, k=k, bg=bg: e.matmul(
                    bg[:], b.BIGA[:, k * 4096 + m * 128:k * 4096 + (m + 1) * 128],
                    b.XB[:, k * TC:(k + 1) * TC], start=(k == 0), stop=(k == NDC - 1)),
                    waits=w_)
                w_ = []
                if k == 0 and bank_free.get((m % 2, "u")) is not None:
                    w_ = [("sV", bank_free[(m % 2, "u")])]
                last_mm = p.emit("tensor", lambda e, m=m, k=k, bu=bu: e.matmul(
                    bu[:], b.BIGB[:, k * 4096 + m * 128:k * 4096 + (m + 1) * 128],
                    b.XB[:, k * TC:(k + 1) * TC], start=(k == 0), stop=(k == NDC - 1)),
                    waits=w_, inc="sP" if k == NDC - 1 else None)
            w_ = [("sP", last_mm)]
            if p.meta.get("ab_ev2") is not None:
                w_.append(("sV", p.meta["ab_ev2"]))
            ev1 = p.emit("scalar", lambda e, m=m, bg=bg: e.activation(
                b.S1[:, 0:512], bg[:], AF.Silu, bias=C[:, 64 + m:65 + m]),
                waits=w_, inc="sA")
            bank_free[(m % 2, "g")] = ev1
            sl = slice((m % 8) * TC, (m % 8 + 1) * TC)
            w_ = [("sA", ev1)]
            if stb_free[m % 8] is not None:
                w_.append((f"sHo{m % 8}", stb_free[m % 8]))
            ev2 = p.emit("vector", lambda e, m=m, sl=sl, bu=bu: e.scalar_tensor_tensor(
                b.STB[:, sl], bu[:], C[:, 96 + m:97 + m], b.S1[:, 0:512], ALU.add, ALU.mult),
                waits=w_, inc="sV")
            p.meta["ab_ev2"] = ev2
            bank_free[(m % 2, "u")] = ev2
            st = p.dma("gpsimd", aps["hd"][m * 128:(m + 1) * 128, tc * TC:(tc + 1) * TC],
                       b.STB[:, sl], waits=[("sV", ev2)], inc=f"sHo{m % 8}")
            stb_free[m % 8] = st

    wd0 = [("sP", p.now("sP"))]
    for k in range(32):
        ldw = p.dma("gpsimd", b.BIGA[:, k * 1024:(k + 1) * 1024],
                    aps["w_down"][k * 128:(k + 1) * 128, :], waits=wd0 if k == 0 else (),
                    inc="sLWD")
    all_h = [(f"sHo{s}", p.now(f"sHo{s}")) for s in range(8)] + [("sLWD", ldw)]
    fm_free = [None] * 2
    stbD_free = [None] * 8
    uD_free = [None] * 2
    for tc in range(NTC):
        xg = load_xc(p, b, aps["xin"], tc,
                     gate=[("sV", p.now("sV")), ("sA", p.now("sA"))])
        if tc == 0:
            w_ = [("sP", p.now("sP"))]
        else:
            w_ = [("sV", p.meta["abD_evac"])]
        for k in range(NDC):
            ldu = p.dma("sync", b.XB[:, k * TC:(k + 1) * TC],
                        aps["ud"][k * 128:(k + 1) * 128, tc * TC:(tc + 1) * TC],
                        waits=w_ if k == 0 else (), inc="sLU2")
        for grp in range(2):
            kcW = None
            for kc in range(32):
                sl = slice((kc % 8) * TC, (kc % 8 + 1) * TC)
                w_ = list(all_h) if (tc == 0 and grp == 0 and kc < 8) else []
                if stbD_free[kc % 8] is not None:
                    w_.append(("sP", stbD_free[kc % 8]))
                ldh = p.dma("sync", b.STB[:, sl],
                            aps["hd"][kc * 128:(kc + 1) * 128, tc * TC:(tc + 1) * TC],
                            waits=w_, inc=f"sLH{kc % 8}")
                for j in range(4):
                    dm = grp * 4 + j
                    ww = []
                    if j == 0:
                        ww.append((f"sLH{kc % 8}", ldh))
                    if kc == 0 and j == 0:
                        if fm_free[0] is not None:
                            ww.append(("sV", p.meta.get("abD_evac", 0)))
                        else:
                            ww.append(("sA", p.now("sA")))
                            ww.append(("sV", p.now("sV")))
                    kcW = p.emit("tensor", lambda e, kc=kc, dm=dm, j=j, sl=sl: e.matmul(
                        b.pb[j][:], b.BIGA[:, kc * 1024 + dm * 128:kc * 1024 + (dm + 1) * 128],
                        b.STB[:, sl], start=(kc == 0), stop=(kc == 31)),
                        waits=ww, inc="sP" if j == 3 else None)
                stbD_free[kc % 8] = kcW
            for j in range(4):
                dm = grp * 4 + j
                fs = slice((j % 2) * TC, (j % 2) * TC + TC)
                w_ = [("sP", kcW), ("sLX", p.now("sLX")), ("sLU2", ldu)]
                if fm_free[j % 2] is not None:
                    w_.append((f"sXo{j % 2}", fm_free[j % 2]))
                p.emit("vector", lambda e, dm=dm, j=j, fs=fs: e.tensor_scalar(
                    b.FM[:, fs], b.pb[j][:], C[:, 128 + dm:129 + dm], None, ALU.add),
                    waits=w_)
                p.emit("vector", lambda e, dm=dm, fs=fs: e.tensor_add(
                    b.FM[:, fs], b.FM[:, fs], b.XB[:, dm * TC:(dm + 1) * TC]))
                ev = p.emit("vector", lambda e, dm=dm, fs=fs: e.tensor_add(
                    b.FM[:, fs], b.FM[:, fs], b.XC[:, dm * TC:(dm + 1) * TC]), inc="sV")
                p.meta["abD_evac"] = ev
                uD_free[tc % 2] = ev
                st = p.dma("gpsimd", aps["xout"][dm * 128:(dm + 1) * 128, tc * TC:(tc + 1) * TC],
                           b.FM[:, fs], waits=[("sV", ev)], inc=f"sXo{j % 2}")
                fm_free[j % 2] = st
    return [(f"sXo{j}", p.now(f"sXo{j}")) for j in range(2)]


def emit_attn(p: Prog, b: Bufs, aps, rope: bool, gate=()):
    NTC = p.NTC
    T = p.T
    NSC = T // 128
    V65 = 1040

    if rope:
        p.dma("sync", b.COLS[:, 0:8], aps["bqc"], waits=list(gate), inc="sLB")
        p.dma("sync", b.COLS[:, 8:16], aps["bkc"], inc="sLB")
        p.dma("sync", b.COLS[:, 16:24], aps["boc"], inc="sLB")
        p.dma("sync", b.S2[:, 0:1024], aps["bvb"], inc="sLB")
        p.dma("gpsimd", b.WS[:, 0:128], aps["pswap"], waits=list(gate), inc="sLB2")
        bg = [("sLB", p.now("sLB")), ("sLB2", p.now("sLB2"))]
    else:
        bg = list(gate)

    def load_w(ap, wait):
        for kc in range(NDC):
            lw = p.dma("gpsimd", b.WQK[:, kc * 1024:(kc + 1) * 1024],
                       ap[kc * 128:(kc + 1) * 128, :], waits=wait if kc == 0 else (), inc="sLWS")
        return [("sLWS", lw)]

    w1r = load_w(aps["wqT"], list(gate))

    vt3 = b.BIGB[:, 0:NSC * V65].rearrange("p (s h c) -> p s h c", h=16, c=65)
    mo = p.emit("vector", lambda e: e.memset(vt3[:, :, :, 64:65], 1.0),
                waits=[("sV", p.now("sV")), ("sA", p.now("sA")), ("sP", p.now("sP"))] + list(gate),
                inc="sV")

    def rope_tables(tc):
        w_ = [("sV", p.meta["attn_ct_used"])] if p.meta.get("attn_ct_used") else []
        p.dma("gpsimd", b.CT, aps["cosT"][:, tc * TC:(tc + 1) * TC], waits=w_, inc="sLR")
        lst = p.dma("gpsimd", b.ST, aps["sinT"][:, tc * TC:(tc + 1) * TC], inc="sLR")
        return [("sLR", lst)]

    def proj_pass(which, tc, wrdy):
        for hp in range(NDC):
            bank = b.pb[hp % 2]
            for kc in range(NDC):
                w_ = []
                if kc == 0:
                    w_ = list(p.meta["attn_xbrdy"])
                    if tc == 0 and hp < 2:
                        w_ += wrdy
                    bkey = f"attn_bank{hp % 2}"
                    if p.meta.get(bkey) is not None:
                        w_.append(p.meta[bkey])
                mm = p.emit("tensor", lambda e, hp=hp, kc=kc, bank=bank: e.matmul(
                    bank[:], b.WQK[:, kc * 1024 + hp * 128:kc * 1024 + (hp + 1) * 128],
                    b.XB[:, kc * TC:(kc + 1) * TC], start=(kc == 0), stop=(kc == NDC - 1)),
                    waits=w_, inc="sP" if kc == NDC - 1 else None)
            if rope:
                rg = p.meta["attn_rg"]
                bcol = b.COLS[:, (0 if which == "q" else 8) + hp:(0 if which == "q" else 8) + hp + 1]
                sl = slice((hp % 4) * TC, (hp % 4 + 1) * TC)
                w_ = [("sP", mm)]
                if p.meta.get(f"attn_stb{hp % 4}") is not None:
                    w_.append(("sP", p.meta[f"attn_stb{hp % 4}"]))
                ev = p.emit("scalar", lambda e, sl=sl, bank=bank, bcol=bcol: e.activation(
                    b.STB[:, sl], bank[:], AF.Identity, bias=bcol),
                    waits=w_, inc="sA")
                p.meta[f"attn_bank{hp % 2}"] = ("sA", ev)
                sw = p.emit("tensor", lambda e, sl=sl: e.matmul(
                    b.pb[4][:], b.WS[:, 0:128], b.STB[:, sl], start=True, stop=True),
                    waits=[("sA", ev), ("sV", p.meta.get("attn_sw_used", 0))] + rg,
                    inc="sP")
                p.meta[f"attn_stb{hp % 4}"] = sw
                p.emit("vector", lambda e, sl=sl: e.tensor_mul(b.FM[:, 0:512], b.STB[:, sl], b.CT),
                       waits=[("sP", sw)])
                sm = p.emit("vector", lambda e: e.tensor_mul(b.S1[:, 0:512], b.pb[4][:], b.ST),
                            inc="sV")
                p.meta["attn_sw_used"] = sm
                if which == "q":
                    qs = slice((hp % 4) * TC + 2048, (hp % 4 + 1) * TC + 2048)
                    w_ = []
                    if p.meta.get(f"attn_qst{hp % 4}") is not None:
                        w_.append((f"sQo{hp % 4}", p.meta[f"attn_qst{hp % 4}"]))
                    rv = p.emit("vector", lambda e, qs=qs: e.tensor_add(
                        b.STB[:, qs], b.FM[:, 0:512], b.S1[:, 0:512]), waits=w_, inc="sV")
                    st = p.dma("gpsimd", aps["qd"][hp * 128:(hp + 1) * 128, tc * TC:(tc + 1) * TC],
                               b.STB[:, qs], waits=[("sV", rv)], inc=f"sQo{hp % 4}")
                    p.meta[f"attn_qst{hp % 4}"] = st
                else:
                    rv = p.emit("vector", lambda e, hp=hp, tc=tc: e.tensor_add(
                        b.BIGA[:, hp * T + tc * TC:hp * T + (tc + 1) * TC],
                        b.FM[:, 0:512], b.S1[:, 0:512]), inc="sV")
                p.meta["attn_ct_used"] = rv
            else:
                if which == "q":
                    sl = slice((hp % 4) * TC, (hp % 4 + 1) * TC)
                    w_ = [("sP", mm)]
                    if p.meta.get(f"attn_qst{hp % 4}") is not None:
                        w_.append((f"sQo{hp % 4}", p.meta[f"attn_qst{hp % 4}"]))
                    ev = p.emit("scalar", lambda e, sl=sl, bank=bank: e.activation(
                        b.STB[:, sl], bank[:], AF.Identity), waits=w_, inc="sA")
                    p.meta[f"attn_bank{hp % 2}"] = ("sA", ev)
                    st = p.dma("gpsimd", aps["qd"][hp * 128:(hp + 1) * 128, tc * TC:(tc + 1) * TC],
                               b.STB[:, sl], waits=[("sA", ev)], inc=f"sQo{hp % 4}")
                    p.meta[f"attn_qst{hp % 4}"] = st
                else:
                    ev = p.emit("scalar", lambda e, hp=hp, tc=tc, bank=bank: e.activation(
                        b.BIGA[:, hp * T + tc * TC:hp * T + (tc + 1) * TC], bank[:], AF.Identity),
                        waits=[("sP", mm)], inc="sA")
                    p.meta[f"attn_bank{hp % 2}"] = ("sA", ev)

    for tc in range(NTC):
        xg = load_xc(p, b, aps["xin"], tc,
                     gate=([("sV", p.now("sV")), ("sA", p.now("sA"))] +
                           (p.meta.get("pbc_free", []) + bg if tc == 0 else [])))
        lng = xg + ([("sP", p.now("sP"))] if tc > 0 else [])
        lng += [(f"sQo{j}", p.now(f"sQo{j}")) for j in range(4)]
        lng += [("sXh", p.now("sXh"))]
        xw = emit_ln(p, b, b.EPSC[0:1, 0:1], gate=lng)
        for k in range(NDC):
            p.dma("gpsimd", aps["xhd"][k * 128:(k + 1) * 128, tc * TC:(tc + 1) * TC],
                  b.XB[:, k * TC:(k + 1) * TC], waits=list(xw) if k == 0 else (), inc="sXh")
        if rope:
            p.meta["attn_rg"] = rope_tables(tc)
        p.meta["attn_xbrdy"] = xw
        proj_pass("q", tc, w1r)
    allxh = [("sXh", p.now("sXh"))]
    wkr = load_w(aps["wkT"], [("sP", p.now("sP"))])
    p.meta["attn_ct_used"] = None
    for tc in range(NTC):
        w_ = ([("sP", p.now("sP")), ("sA", p.now("sA")), ("sV", p.now("sV"))] + allxh) if tc == 0 \
            else [("sP", p.now("sP"))]
        for k in range(NDC):
            ldxh = p.dma("sync", b.XB[:, k * TC:(k + 1) * TC],
                         aps["xhd"][k * 128:(k + 1) * 128, tc * TC:(tc + 1) * TC],
                         waits=w_ if k == 0 else (), inc="sXh2")
        if rope:
            p.meta["attn_rg"] = rope_tables(tc)
        p.meta["attn_xbrdy"] = [("sXh2", ldxh)]
        proj_pass("k", tc, wkr)

    wvr = load_w(aps["wvT"], [("sP", p.now("sP"))])
    for tc in range(NTC):
        w_ = [("sP", p.now("sP"))] if tc > 0 else [("sP", p.now("sP")), ("sA", p.now("sA"))]
        for k in range(NDC):
            ldxh = p.dma("sync", b.XB[:, k * TC:(k + 1) * TC],
                         aps["xhd"][k * 128:(k + 1) * 128, tc * TC:(tc + 1) * TC],
                         waits=w_ if k == 0 else (), inc="sXh2")
        xh2 = [("sXh2", ldxh)]
        for tsq in range(4):
            sc = tc * 4 + tsq
            for dn in range(2):
                bank = b.pb[2 + dn]
                for kc in range(NDC):
                    w_ = []
                    if kc == 0:
                        w_ = [("sV", mo)] + xh2
                        if tc == 0 and tsq == 0:
                            w_ += wvr
                        bkey = f"attn_vbank{dn}"
                        if p.meta.get(bkey) is not None:
                            w_.append(("sV", p.meta[bkey]))
                    mm = p.emit("tensor", lambda e, tsq=tsq, dn=dn, kc=kc, bank=bank: e.matmul(
                        bank[:], b.XB[:, kc * TC + tsq * 128:kc * TC + tsq * 128 + 128],
                        b.WQK[:, kc * 1024 + dn * 512:kc * 1024 + (dn + 1) * 512],
                        start=(kc == 0), stop=(kc == NDC - 1)),
                        waits=w_, inc="sP" if kc == NDC - 1 else None)
                out3 = b.BIGB[:, sc * V65 + dn * 8 * 65:sc * V65 + (dn * 8 + 8) * 65].rearrange(
                    "p (h c) -> p h c", c=65)[:, :, 0:64]
                in3 = bank[:].rearrange("p (h c) -> p h c", c=64)
                if rope:
                    vv = p.emit("vector", lambda e, out3=out3, in3=in3, dn=dn: e.tensor_add(
                        out3, in3, b.S2[:, dn * 512:(dn + 1) * 512].rearrange("p (h c) -> p h c", c=64)),
                        waits=[("sP", mm)], inc="sV")
                else:
                    vv = p.emit("vector", lambda e, out3=out3, in3=in3: e.tensor_copy(out3, in3),
                                waits=[("sP", mm)], inc="sV")
                p.meta[f"attn_vbank{dn}"] = vv
    wor = load_w(aps["woT"], [("sP", p.now("sP"))])
    kvready = [("sV", p.now("sV")), ("sA", p.now("sA")), ("sP", p.now("sP"))] + wor
    allq = [(f"sQo{j}", p.now(f"sQo{j}")) for j in range(4)]

    scale = 0.125
    stb_free = [None] * 8
    sti = 0
    for tq in range(NTC):
        w_ = (kvready + allq) if tq == 0 else [("sP", p.meta["attn_lastwo"])]
        for hp in range(NDC):
            ldq = p.dma("sync", b.WS[:, hp * TC:(hp + 1) * TC],
                        aps["qd"][hp * 128:(hp + 1) * 128, tq * TC:(tq + 1) * TC],
                        waits=w_ if hp == 0 else (), inc="sLQ")
        xg = load_xc(p, b, aps["xin"], tq, gate=[("sV", p.now("sV")), ("sA", p.now("sA"))])
        qrdy = [("sLQ", ldq)]
        for hp in range(8):
            nsc = tq * 4 + 4
            for sc in range(nsc):
                delta = max(0, sc * 128 - tq * TC)
                sbank = [b.pb[0], b.pb[1]] if (sc % 2 == 0) else [b.pbc[0], b.pbc[1]]
                for h2 in range(2):
                    lhs = b.BIGA[h2 * 64:h2 * 64 + 64, hp * T + sc * 128:hp * T + (sc + 1) * 128]
                    rhs = b.WS[h2 * 64:h2 * 64 + 64, hp * TC + delta:(hp + 1) * TC]
                    w_ = [("sA", p.meta.get(f"attn_sexp{sc % 2}_{h2}", 0))]
                    if sc < 2:
                        w_ += qrdy
                        if tq == 0 and hp == 0:
                            w_ += kvready
                    smm = p.emit("tensor", lambda e, lhs=lhs, rhs=rhs, delta=delta, bank=sbank[h2]: e.matmul(
                        bank[:, delta:512], lhs, rhs, start=True, stop=True),
                        waits=w_, inc="sP")
                    ssl = slice((sti % 8) * TC + delta, (sti % 8) * TC + TC)
                    w_ = [("sP", smm)]
                    if stb_free[sti % 8] is not None:
                        w_.append(("sP", stb_free[sti % 8]))
                    elif sti < 8:
                        w_ += allq
                    ex = p.emit("scalar", lambda e, ssl=ssl, delta=delta, bank=sbank[h2]: e.activation(
                        b.STB[:, ssl], bank[:, delta:512], AF.Exp, scale=scale),
                        waits=w_, inc="sA")
                    p.meta[f"attn_sexp{sc % 2}_{h2}"] = ex
                    pvw = [("sA", ex)]
                    if delta > 0 or sc * 128 == tq * TC:
                        dsl = slice((sti % 8) * TC + delta, (sti % 8) * TC + delta + 128)
                        ex2 = p.emit("vector", lambda e, dsl=dsl: e.tensor_mul(
                            b.STB[:, dsl], b.STB[:, dsl], b.TRIL[:]),
                            waits=[("sA", ex)], inc="sV")
                        pvw = [("sV", ex2)]
                    if delta > 0:
                        zsl = slice((sti % 8) * TC, (sti % 8) * TC + delta)
                        mz = p.emit("vector", lambda e, zsl=zsl: e.memset(b.STB[:, zsl], 0.0),
                                    waits=list(w_), inc="sV")
                        pvw.append(("sV", mz))
                    hglob = hp * 2 + h2
                    if sc == 0:
                        pvw = pvw + [("sV", p.meta.get(f"attn_opsum{h2}", 0))]
                    pvm = p.emit("tensor", lambda e, sc=sc, hglob=hglob, sti=sti, h2=h2, nsc=nsc: e.matmul(
                        b.pb[2 + h2][0:65, 0:512],
                        b.BIGB[:, sc * V65 + hglob * 65:sc * V65 + (hglob + 1) * 65],
                        b.STB[:, (sti % 8) * TC:(sti % 8) * TC + TC],
                        start=(sc == 0), stop=(sc == nsc - 1)),
                        waits=pvw, inc="sP")
                    stb_free[sti % 8] = pvm
                    sti += 1
            for h2 in range(2):
                R = b.ROWS
                w_ = [("sP", p.now("sP"))]
                if rope:
                    p.emit("vector", lambda e, h2=h2: e.tensor_scalar_add(
                        R[:, 0:512], b.pb[2 + h2][64:65, :], b.EPSC[0:1, 3:4]), waits=w_)
                else:
                    p.emit("vector", lambda e, h2=h2: e.tensor_copy(R[:, 0:512], b.pb[2 + h2][64:65, :]),
                           waits=w_)
                rc = p.emit("vector", lambda e: e.reciprocal(R[:, 512:1024], R[:, 0:512]), inc="sV")
                bc = p.emit("tensor", lambda e: e.matmul(
                    b.pb[4][:], b.ONEF[0:1, 0:1].broadcast_to([1, 128]), R[:, 512:1024],
                    start=True, stop=True), waits=[("sV", rc)], inc="sP")
                cp = p.emit("vector", lambda e: e.tensor_copy(b.S1[:, 0:512], b.pb[4][:, 0:512]),
                            waits=[("sP", bc)], inc="sV")
                ev = p.emit("vector", lambda e, hp=hp, h2=h2: e.tensor_mul(
                    b.XB[h2 * 64:h2 * 64 + 64, hp * TC:(hp + 1) * TC],
                    b.pb[2 + h2][0:64, :], b.S1[0:64, 0:512]), inc="sV")
                p.meta["attn_ctx"] = ev
                p.meta[f"attn_opsum{h2}"] = ev
        for dm in range(NDC):
            for kc in range(NDC):
                w_ = []
                if kc == 0:
                    w_ = [("sV", p.meta["attn_ctx"])]
                    if p.meta.get("attn_wob") is not None:
                        w_.append(("sV", p.meta["attn_wob"]))
                wm = p.emit("tensor", lambda e, dm=dm, kc=kc: e.matmul(
                    b.pb[4][:], b.WQK[:, kc * 1024 + dm * 128:kc * 1024 + (dm + 1) * 128],
                    b.XB[:, kc * TC:(kc + 1) * TC], start=(kc == 0), stop=(kc == NDC - 1)),
                    waits=w_, inc="sP" if kc == NDC - 1 else None)
            fs = slice((dm % 2) * TC, (dm % 2) * TC + TC)
            w_ = [("sP", wm)] + (xg if dm == 0 else [])
            if p.meta.get(f"attn_xo{dm % 2}") is not None:
                w_.append((f"sXo{dm % 2}", p.meta[f"attn_xo{dm % 2}"]))
            if rope:
                p.emit("vector", lambda e, dm=dm, fs=fs: e.tensor_scalar_add(
                    b.FM[:, fs], b.pb[4][:], b.COLS[:, 16 + dm:17 + dm]), waits=w_)
                ev = p.emit("vector", lambda e, dm=dm, fs=fs: e.tensor_add(
                    b.FM[:, fs], b.FM[:, fs], b.XC[:, dm * TC:(dm + 1) * TC]), inc="sV")
            else:
                ev = p.emit("vector", lambda e, dm=dm, fs=fs: e.tensor_add(
                    b.FM[:, fs], b.pb[4][:], b.XC[:, dm * TC:(dm + 1) * TC]), waits=w_, inc="sV")
            p.meta["attn_wob"] = ev
            st = p.dma("gpsimd", aps["xout"][dm * 128:(dm + 1) * 128, tq * TC:(tq + 1) * TC],
                       b.FM[:, fs], waits=[("sV", ev)], inc=f"sXo{dm % 2}")
            p.meta[f"attn_xo{dm % 2}"] = st
        p.meta["attn_lastwo"] = p.now("sP")
    return [(f"sXo{j}", p.now(f"sXo{j}")) for j in range(2)]


def emit_gla(p: Prog, b: Bufs, aps, gate=()):
    NTC = p.NTC
    T = p.T
    C = b.COLS
    WQ, WK, WRLO, WR = 0, 8192, 16384, 24576
    QH, KH, GNW, XLO = 12288, 16384, 20480, 21504
    S0B = 25600

    g0 = list(gate)
    for i, nm in enumerate(("Wq", "Wk")):
        for kc in range(NDC):
            p.dma("gpsimd", b.BIGA[:, i * 8192 + kc * 1024:i * 8192 + (kc + 1) * 1024],
                  aps[nm][kc * 128:(kc + 1) * 128, :], waits=g0 if (i == 0 and kc == 0) else (),
                  inc="sGW")
    for kc in range(NDC):
        w_ = list(g0) if kc == 0 else []
        if kc >= 4:
            w_.append(("sV", p.meta.get("gla_wrs", 0)))
        lwr = p.dma("sync", b.XC[:, (kc % 4) * 1024:(kc % 4 + 1) * 1024],
                    aps["Wr"][kc * 128:(kc + 1) * 128, :], waits=w_, inc=f"sWRS{kc % 4}")
        hi = p.emit("scalar", lambda e, kc=kc: e.activation(
            b.BIGA[:, WR + kc * 1024:WR + (kc + 1) * 1024],
            b.XC[:, (kc % 4) * 1024:(kc % 4 + 1) * 1024], AF.Identity),
            waits=[(f"sWRS{kc % 4}", lwr)], inc="sA")
        lo = p.emit("vector", lambda e, kc=kc: e.tensor_sub(
            b.BIGA[:, WRLO + kc * 1024:WRLO + (kc + 1) * 1024],
            b.XC[:, (kc % 4) * 1024:(kc % 4 + 1) * 1024],
            b.BIGA[:, WR + kc * 1024:WR + (kc + 1) * 1024]),
            waits=[("sA", hi)], inc="sV")
        p.meta["gla_wrs"] = lo
    p.dma("gpsimd", b.BIGB[:, GNW:GNW + 1024], aps["gnw_b"], waits=g0, inc="sGW")
    lw = p.dma("gpsimd", b.WS[:, 0:64].rearrange("p (k c) -> p k c", k=8),
               aps["Wa"].rearrange("(k p) c -> p k c", p=128), inc="sGW")
    lc = p.dma("sync", C[:, 0:1], aps["qnw_c"], waits=g0, inc="sGC")
    lc = p.dma("sync", C[:, 1:2], aps["knw_c"], inc="sGC")
    lc = p.dma("sync", b.ROWS[0:1, 0:8], aps["alog_r"], inc="sGC")
    lc = p.dma("sync", b.ROWS[0:1, 8:16], aps["dtb_r"], inc="sGC")
    wrdy = [("sGW", p.now("sGW")), ("sGC", lc)]
    bc1 = p.emit("tensor", lambda e: e.matmul(b.pb[4][:, 0:8],
                 b.ONEF[0:1, 0:1].broadcast_to([1, 128]), b.ROWS[0:1, 0:8],
                 start=True, stop=True), waits=wrdy, inc="sP")
    e1 = p.emit("scalar", lambda e: e.activation(C[:, 16:24], b.pb[4][:, 0:8], AF.Exp),
                waits=[("sP", bc1)], inc="sA")
    p.emit("vector", lambda e: e.tensor_scalar_mul(C[:, 16:24], C[:, 16:24], -1.0),
           waits=[("sA", e1)])
    bc2 = p.emit("tensor", lambda e: e.matmul(b.pb[4][:, 8:16],
                 b.ONEF[0:1, 0:1].broadcast_to([1, 128]), b.ROWS[0:1, 8:16],
                 start=True, stop=True), waits=[("sA", e1)], inc="sP")
    p.emit("vector", lambda e: e.tensor_copy(C[:, 24:32], b.pb[4][:, 8:16]),
           waits=[("sP", bc2)])
    p.emit("vector", lambda e: e.memset(b.S2[:, 0:1024], 0.0))
    p.emit("vector", lambda e: e.memset(b.BIGB[:, S0B:S0B + 1024], 0.0), inc="sV")
    prdy = [("sV", p.now("sV"))]

    sti = [0]

    def slot():
        s = 1536 + (sti[0] % 20) * 128
        sti[0] += 1
        return slice(s, s + 128)

    for tc in range(NTC):
        xg = load_xc(p, b, aps["xin"], tc,
                     gate=([("sV", p.now("sV")), ("sA", p.now("sA"))] +
                           (p.meta.get("pbc_free", []) + prdy if tc == 0 else [])))
        xw = emit_ln(p, b, b.EPSC[0:1, 0:1],
                     gate=xg + ([("sP", p.now("sP"))] if tc > 0 else []),
                     lo_dst=[b.BIGB[:, XLO + k * TC:XLO + (k + 1) * TC] for k in range(NDC)])
        for kc in range(NDC):
            w_ = ([("sP", p.now("sP"))] + (wrdy if tc == 0 else [])) if kc == 0 else []
            lwv = p.dma("gpsimd", b.WQK[:, kc * 1024:(kc + 1) * 1024],
                        aps["Wv"][kc * 128:(kc + 1) * 128, :], waits=w_, inc="sWV")
        wvrdy = [("sWV", lwv)]
        for which, wbase, dsth, ncol in (("q", WQ, QH, 0), ("k", WK, KH, 1)):
            for h in range(8):
                bank = b.pb[h % 2]
                for kc in range(NDC):
                    w_ = []
                    if kc == 0:
                        w_ = list(xw)
                        if which == "q" and h == 0:
                            w_ += [(f"sXo{j}", p.now(f"sXo{j}")) for j in range(2)]
                        if p.meta.get(f"gla_pj{h % 2}") is not None:
                            w_.append(p.meta[f"gla_pj{h % 2}"])
                        if p.meta.get("gla_fmfree") is not None:
                            w_.append(("sV", p.meta["gla_fmfree"]))
                    mm = p.emit("tensor", lambda e, wbase=wbase, h=h, kc=kc, bank=bank: e.matmul(
                        bank[:, 0:512],
                        b.BIGA[:, wbase + kc * 1024 + h * 128:wbase + kc * 1024 + (h + 1) * 128],
                        b.XB[:, kc * TC:(kc + 1) * TC], start=(kc == 0), stop=(kc == NDC - 1)),
                        waits=w_, inc="sP" if kc == NDC - 1 else None)
                sil = p.emit("scalar", lambda e, bank=bank: e.activation(
                    b.FM[:, 0:512], bank[:, 0:512], AF.Silu), waits=[("sP", mm)], inc="sA")
                p.meta[f"gla_pj{h % 2}"] = ("sA", sil)
                sq = p.emit("scalar", lambda e: e.activation(
                    b.STB[:, 1024:1536], b.FM[:, 0:512], AF.Square), inc="sA")
                w_ = [("sA", sq)]
                if p.meta.get("gla_pb4row") is not None:
                    w_.append(("sV", p.meta["gla_pb4row"]))
                smm = p.emit("tensor", lambda e: e.matmul(
                    b.pb[4][0:1, 0:512], b.ONEB[:, 0:1], b.STB[:, 1024:1536],
                    start=True, stop=True), waits=w_, inc="sP")
                sv1 = p.emit("vector", lambda e: e.tensor_scalar(
                    b.ROWS[0:1, 512:1024], b.pb[4][0:1, 0:512], 1.0 / 128.0, None, ALU.mult),
                    waits=[("sP", smm)], inc="sV")
                p.meta["gla_pb4row"] = sv1
                sq2 = p.emit("scalar", lambda e: e.activation(
                    b.ROWS[0:1, 512:1024], b.ROWS[0:1, 512:1024], AF.Sqrt, bias=b.EPSC[0:1, 0:1]),
                    waits=[("sV", sv1)], inc="sA")
                rc = p.emit("vector", lambda e: e.reciprocal(b.ROWS[0:1, 512:1024], b.ROWS[0:1, 512:1024]),
                            waits=[("sA", sq2)], inc="sV")
                bcm = p.emit("tensor", lambda e: e.matmul(
                    b.pbc[0][:], b.ONEF[0:1, 0:1].broadcast_to([1, 128]), b.ROWS[0:1, 512:1024],
                    start=True, stop=True), waits=[("sV", rc)] + p.meta.get("pbc_free", []), inc="sP")
                dstap = b.BIGB[:, dsth + h * 128:dsth + h * 128 + 3 * 1024 + 128].rearrange(
                    "p (q c) -> p q c", q=4, c=896)[:, :, 0:128] if False else None
                ev = p.emit("vector", lambda e, dsth=dsth, h=h, ncol=ncol: e.scalar_tensor_tensor(
                    b.BIGB[:, dsth:dsth + 4096].rearrange("p (q c) -> p q c", q=4)[:, :, h * 128:(h + 1) * 128],
                    b.FM[:, 0:512].rearrange("p (q c) -> p q c", q=4),
                    C[:, ncol:ncol + 1], b.pbc[0][:].rearrange("p (q c) -> p q c", q=4),
                    ALU.mult, ALU.mult),
                    waits=[("sP", bcm)], inc="sV")
                p.meta["pbc_free"] = [("sV", ev)]
                p.meta["gla_fmfree"] = ev
        for tsq in range(4):
            for dn in range(2):
                bank = b.pb[2 + dn]
                for kc in range(NDC):
                    w_ = []
                    if kc == 0:
                        w_ = list(xw)
                        if p.meta.get(f"gla_vb{dn}") is not None:
                            w_.append(("sA", p.meta[f"gla_vb{dn}"]))
                    if kc == 0:
                        w_ = w_ + wvrdy
                    mm = p.emit("tensor", lambda e, tsq=tsq, dn=dn, kc=kc, bank=bank: e.matmul(
                        bank[:], b.XB[:, kc * TC + tsq * 128:kc * TC + tsq * 128 + 128],
                        b.WQK[:, kc * 1024 + dn * 512:kc * 1024 + (dn + 1) * 512],
                        start=(kc == 0), stop=(kc == NDC - 1)),
                        waits=w_, inc="sP" if kc == NDC - 1 else None)
                vv = p.emit("scalar", lambda e, tsq=tsq, dn=dn, bank=bank: e.activation(
                    b.BIGB[:, tsq * 1024 + dn * 512:tsq * 1024 + (dn + 1) * 512], bank[:], AF.Silu),
                    waits=[("sP", mm)], inc="sA")
                p.meta[f"gla_vb{dn}"] = vv
        vrdy = p.now("sA")
        for kc in range(NDC):
            w_ = [("sP", p.now("sP"))] if kc == 0 else []
            lwo = p.dma("gpsimd", b.WQK[:, kc * 1024:(kc + 1) * 1024],
                        aps["Wo"][kc * 128:(kc + 1) * 128, :], waits=w_, inc="sWO")
        wordy = [("sWO", lwo)]

        for tsq in range(4):
            if GLA_TRUNC and (tc > 0 or tsq > 0):
                break
            for dn in range(2):
                bank = b.pb[2 + dn]
                for kc in range(NDC):
                    w_ = []
                    if kc == 0:
                        w_ = [("sA", vrdy), ("sV", p.meta.get("gla_sc_used", 0))]
                        if tsq == 0:
                            w_.append(("sV", p.now("sV")))
                    for term in range(3):
                        lhs_base = WR if term != 1 else WRLO
                        xsrc = (b.XB[:, kc * TC + tsq * 128:kc * TC + tsq * 128 + 128]
                                if term != 2 else
                                b.BIGB[:, XLO + kc * TC + tsq * 128:XLO + kc * TC + tsq * 128 + 128])
                        mm = p.emit("tensor", lambda e, lhs_base=lhs_base, xsrc=xsrc, dn=dn, kc=kc,
                                    term=term, bank=bank: e.matmul(
                            xsrc, b.BIGA[:, lhs_base + kc * 1024 + dn * 512:lhs_base + kc * 1024 + (dn + 1) * 512],
                            bank[:], start=False, stop=False)
                            if False else e.matmul(
                            bank[:], xsrc,
                            b.BIGA[:, lhs_base + kc * 1024 + dn * 512:lhs_base + kc * 1024 + (dn + 1) * 512],
                            start=(kc == 0 and term == 0), stop=(kc == NDC - 1 and term == 2)),
                            waits=w_ if term == 0 else (), inc="sP" if (kc == NDC - 1 and term == 2) else None)
                        w_ = []
                sg = p.emit("scalar", lambda e, dn=dn, bank=bank: e.activation(
                    b.FM[:, dn * 512:(dn + 1) * 512], bank[:], AF.Sigmoid),
                    waits=[("sP", mm)], inc="sA")
            sgr = sg
            for kc in range(NDC):
                w_ = [("sV", p.now("sV"))] if kc == 0 else []
                zm = p.emit("tensor", lambda e, tsq=tsq, kc=kc: e.matmul(
                    b.pb[4][:, 0:8], b.XB[:, kc * TC + tsq * 128:kc * TC + tsq * 128 + 128],
                    b.WS[:, kc * 8:(kc + 1) * 8], start=(kc == 0), stop=(kc == NDC - 1)),
                    waits=w_, inc="sP" if kc == NDC - 1 else None)
            S = b.S1
            p.emit("vector", lambda e: e.tensor_add(S[:, 800:808], b.pb[4][:, 0:8], C[:, 24:32]),
                   waits=[("sP", zm)])
            vz = p.emit("vector", lambda e: e.tensor_scalar_max(S[:, 808:816], S[:, 800:808], 0.0),
                        inc="sV")
            az = p.emit("scalar", lambda e: e.activation(S[:, 816:824], S[:, 800:808], AF.Abs),
                        waits=[("sV", vz)], inc="sA")
            ez = p.emit("scalar", lambda e: e.activation(S[:, 824:832], S[:, 816:824], AF.Exp,
                                                         scale=-1.0), inc="sA")
            lz = p.emit("scalar", lambda e: e.activation(S[:, 816:824], S[:, 824:832], AF.Ln,
                                                         bias=b.ONEF[:, 0:1]), inc="sA")
            p.emit("vector", lambda e: e.tensor_add(S[:, 808:816], S[:, 808:816], S[:, 816:824]),
                   waits=[("sA", lz)])
            fh = p.emit("vector", lambda e: e.tensor_mul(S[:, 788:796], S[:, 808:816], C[:, 16:24]),
                        inc="sV")
            S = b.S1
            for hg in range(2):
                if GLA_ONEGROUP and hg == 1:
                    break
                if GLA_G1STOP == 'T0' and hg == 1:
                    continue
                hs = [hg * 4 + j for j in range(4)]
                SL = lambda h, j: slice(1024 + j * 512 + (h % 4) * 128,
                                        1024 + j * 512 + (h % 4) * 128 + 128)
                FMB = lambda h, hg=hg: slice(hg * 512 + (h % 4) * 128, hg * 512 + (h % 4) * 128 + 128)

                def sv3(j):
                    base = 1024 + j * 512
                    return b.STB[:, base:base + 512].rearrange("p (h c) -> p h c", h=4)

                def qk3(base, hg=hg, tsq=tsq):
                    s0 = base + tsq * 1024 + hg * 512
                    return b.BIGB[:, s0:s0 + 512].rearrange("p (h c) -> p h c", h=4)
                FM4 = b.FM[:, hg * 512:hg * 512 + 512]
                FM43 = FM4.rearrange("p (h c) -> p h c", h=4)
                first = True
                for h in hs:
                    fsl = slice(h * 128, (h + 1) * 128)
                    cur = b.FM[:, fsl]
                    dsts = [S[:, 512:640], S[:, 640:768], S[:, 512:640], S[:, (h % 4) * 128:(h % 4) * 128 + 128]]
                    for r in range(4):
                        w_ = []
                        if first:
                            w_ = [("sA", sgr), ("sV", fh), ("sP", p.meta.get("gla_t1p", 0))]
                            first = False
                        p.emit("vector", lambda e, cur=cur: e.max(out=S[:, 768:776], in_=cur), waits=w_)
                        p.emit("vector", lambda e, cur=cur, dst=dsts[r]: e.match_replace(
                            out=dst, in_to_replace=S[:, 768:776], in_values=cur, imm_value=0.0))
                        cur = dsts[r]
                if GLA_G1STOP == 'T1b' and hg == 1:
                    continue
                sc3 = b.FM[:, hg * 4 * 128:hg * 4 * 128 + 512].rearrange("p (h c) -> p h c", h=4)
                rem3 = S[:, 0:512].rearrange("p (h c) -> p h c", h=4)
                if aps.get("dbg") is not None and tc == 0 and tsq == 0 and hg == 0:
                    p.dma("gpsimd", aps["dbg2"][:, 260:388], S[:, 0:128], waits=[("sV", p.now("sV"))], inc="sDB")
                    p.dma("gpsimd", aps["dbg2"][:, 388:516], b.FM[:, 0:128], waits=[("sV", p.now("sV"))], inc="sDB")
                dw = [("sDB", p.now("sDB"))] if aps.get("dbg") is not None else []
                p.emit("vector", lambda e, sc3=sc3: e.tensor_sub(rem3, sc3, rem3), waits=dw)
                if GLA_G1STOP == 'T1c' and hg == 1:
                    continue
                p.emit("vector", lambda e: e.reduce_sum(
                    S[:, 776:780].rearrange("p (h c) -> p h c", h=4), rem3, AX.X))
                p.emit("vector", lambda e: e.tensor_scalar_add(S[:, 776:780], S[:, 776:780], b.EPSC[:, 2:3]))
                p.emit("vector", lambda e: e.reciprocal(S[:, 780:784], S[:, 776:780]))
                p.emit("vector", lambda e, hg=hg: e.tensor_mul(S[:, 784:788], S[:, 780:784],
                                                               S[:, 788 + hg * 4:792 + hg * 4]))
                if GLA_G1STOP == 'T1d' and hg == 1:
                    continue
                tv = p.emit("vector", lambda e: e.tensor_mul(
                    sv3(0), rem3,
                    S[:, 784:788].rearrange("p (h c) -> p h c", h=4).broadcast_to([128, 4, 128])),
                    inc="sV")
                if aps.get("dbg") is not None and tc == 0 and tsq == 0 and hg == 0:
                    p.dma("gpsimd", aps["dbg2"][:, 0:128], S[:, 0:128], waits=[("sV", tv)], inc="sDB")
                    p.dma("gpsimd", aps["dbg2"][:, 128:256], b.STB[:, SL(0, 0)], waits=[("sV", tv)], inc="sDB")
                    p.dma("gpsimd", aps["dbg2"][:, 256:260], S[:, 784:788], waits=[("sV", tv)], inc="sDB")
                if GLA_TSTOP == 'T1' or (GLA_G1STOP == 'T1' and hg == 1):
                    continue
                w_ = [("sV", tv), ("sV", p.meta.get("ptr_reads", 0))]
                for h in hs:
                    tr = p.emit("tensor", lambda e, h=h, SL=SL: e.transpose(
                        b.ptr[h % 4], b.STB[:, SL(h, 0)], b.IDN[:]),
                        waits=w_ if h == hs[0] else (), inc="sP" if h == hs[-1] else None)
                p.emit("vector", lambda e: e.tensor_copy(
                    sv3(0), b.PTR[:, 0:512].rearrange("p (h c) -> p h c", h=4)),
                    waits=[("sP", tr)])
                for h in hs:
                    sc_ = p.emit("vector", lambda e, h=h, SL=SL, FMB=FMB: e.tensor_tensor_scan(
                        b.FM[:, FMB(h)], b.ONEF[:, 0:1].broadcast_to([128, 128]), b.STB[:, SL(h, 0)],
                        0.0, ALU.mult, ALU.add), inc="sV" if h == hs[-1] else None)
                p.meta["ptr_reads"] = p.now("sV")
                if GLA_TSTOP == 'T3' or (GLA_G1STOP == 'T3' and hg == 1):
                    continue
                if aps.get("dbg") is not None and tc == 0 and tsq == 0 and hg == 0:
                    p.dma("gpsimd", aps["dbg"][:, 0:128], b.FM[:, FMB(0)], waits=[("sV", sc_)], inc="sDB")
                    p.dma("gpsimd", aps["dbg"][:, 128:256], b.STB[:, SL(0, 0)], waits=[("sV", sc_)], inc="sDB")
                eb3 = sv3(1)
                enb3 = sv3(2)
                w_ = [("sV", sc_)]
                if aps.get("dbg") is not None:
                    w_.append(("sDB", p.now("sDB")))
                p.emit("scalar", lambda e, eb3=eb3, FM43=FM43: e.activation(eb3, FM43, AF.Exp), waits=w_)
                p.emit("scalar", lambda e, enb3=enb3, FM43=FM43: e.activation(enb3, FM43, AF.Exp, scale=-1.0))
                p.emit("scalar", lambda e, FM43=FM43: e.activation(
                    S[:, 840:844].rearrange("p (h c) -> p h c", h=4),
                    FM43[:, :, 127:128], AF.Exp))
                ea = p.emit("scalar", lambda e, FM43=FM43: e.activation(FM43, sv3(0), AF.Exp), inc="sA")
                p.emit("vector", lambda e, FM4=FM4: e.tensor_scalar(FM4, FM4, -1.0, 1.0, ALU.mult, ALU.add),
                       waits=[("sA", ea)])
                p.emit("vector", lambda e, eb3=eb3, qv=qk3(QH): e.tensor_mul(sv3(3), qv, eb3))
                p.emit("vector", lambda e, FM43=FM43, kv=qk3(KH): e.tensor_mul(sv3(4), kv, FM43))
                p.emit("vector", lambda e, enb3=enb3: e.tensor_mul(sv3(4), sv3(4), enb3))
                t5 = p.emit("vector", lambda e: e.tensor_mul(
                    sv3(5), sv3(4),
                    S[:, 840:844].rearrange("p (h c) -> p h c", h=4).broadcast_to([128, 4, 128])),
                    inc="sV")
                if aps.get("dbg") is not None and tc == 0 and tsq == 0 and hg == 0:
                    p.dma("gpsimd", aps["dbg"][:, 256:384], b.STB[:, SL(0, 3)], waits=[("sV", t5)], inc="sDB")
                    p.dma("gpsimd", aps["dbg"][:, 384:512], b.STB[:, SL(0, 4)], waits=[("sV", t5)], inc="sDB")
                    p.dma("gpsimd", aps["dbg"][:, 512:640], b.STB[:, SL(0, 5)], waits=[("sV", t5)], inc="sDB")
                    p.dma("gpsimd", aps["dbg"][:, 640:644], S[:, 840:844], waits=[("sV", t5)], inc="sDB")
                if GLA_TSTOP == 'T5' or (GLA_G1STOP == 'T5' and hg == 1):
                    continue
                for h in hs:
                    w_ = []
                    if h == hs[0]:
                        w_ = [("sV", t5), ("sV", p.meta.get("gla_maskph", 0))]
                        if tsq == 0 and hg == 0:
                            w_.append(("sA", vrdy))
                    am = p.emit("tensor", lambda e, h=h, SL=SL: e.matmul(
                        b.pbc[1][:, (h % 4) * 128:(h % 4) * 128 + 128],
                        b.STB[:, SL(h, 4)], b.STB[:, SL(h, 3)], start=True, stop=True,
                        skip_group_check=True),
                        waits=w_, inc="sP" if h == hs[-1] else None)
                t7 = p.emit("vector", lambda e: e.tensor_mul(
                    sv3(4), b.pbc[1][:, 0:512].rearrange("p (h c) -> p h c", h=4),
                    b.TRIL[:].rearrange("(o p) c -> o p c", o=1).broadcast_to([128, 4, 128])
                    if False else b.TRIL[:].rearrange("p (o c) -> p o c", o=1).broadcast_to([128, 4, 128])),
                    waits=[("sP", am)], inc="sV")
                p.meta["gla_maskph"] = t7
                if aps.get("dbg") is not None and tc == 0 and tsq == 0 and hg == 0:
                    p.dma("gpsimd", aps["dbg"][:, 644:772], b.STB[:, SL(0, 4)], waits=[("sV", t7)], inc="sDB")
                for h in hs:
                    w_ = []
                    if h == hs[0]:
                        w_ = [("sV", t7), ("sV", p.meta.get("gla_oevph", 0))]
                        if tsq == 0 and hg == 0:
                            w_.append(("sA", vrdy))
                    p.emit("tensor", lambda e, h=h, SL=SL: e.matmul(
                        b.pb[0][:, (h % 4) * 128:(h % 4) * 128 + 128], b.STB[:, SL(h, 3)],
                        b.BIGB[:, S0B + h * 128:S0B + (h + 1) * 128],
                        start=True, stop=False, skip_group_check=True), waits=w_)
                    p.emit("tensor", lambda e, h=h, SL=SL, tsq=tsq: e.matmul(
                        b.pb[0][:, (h % 4) * 128:(h % 4) * 128 + 128], b.STB[:, SL(h, 4)],
                        b.BIGB[:, tsq * 1024 + h * 128:tsq * 1024 + (h + 1) * 128],
                        start=False, stop=True, skip_group_check=True))
                for h in hs:
                    w_ = [("sV", p.meta.get("ptr_reads", 0))] if h == hs[0] else []
                    tr2 = p.emit("tensor", lambda e, h=h, SL=SL: e.transpose(
                        b.ptr[h % 4], b.STB[:, SL(h, 5)], b.IDN[:]),
                        waits=w_, inc="sP" if h == hs[-1] else None)
                if GLA_G1STOP == 'T7' and hg == 1:
                    continue
                t9 = p.emit("vector", lambda e: e.tensor_copy(
                    sv3(5), b.PTR[:, 0:512].rearrange("p (h c) -> p h c", h=4)),
                    waits=[("sP", tr2)], inc="sV")
                p.meta["ptr_reads"] = t9
                for h in hs:
                    w_ = []
                    if h == hs[0]:
                        w_ = [("sV", t9), ("sV", p.meta.get("gla_stph", 0))]
                        if tsq == 0 and hg == 0:
                            w_.append(("sA", vrdy))
                    sm = p.emit("tensor", lambda e, h=h, SL=SL, tsq=tsq: e.matmul(
                        b.pb[1][:, (h % 4) * 128:(h % 4) * 128 + 128], b.STB[:, SL(h, 5)],
                        b.BIGB[:, tsq * 1024 + h * 128:tsq * 1024 + (h + 1) * 128],
                        start=True, stop=True, skip_group_check=True),
                        waits=w_, inc="sP" if h == hs[-1] else None)
                if GLA_G1STOP == 'T9' and hg == 1:
                    continue
                for h in hs:
                    p.emit("vector", lambda e, h=h, hg=hg: e.scalar_tensor_tensor(
                        b.S2[:, h * 128:(h + 1) * 128], b.S2[:, h * 128:(h + 1) * 128],
                        S[:, 840 + (h % 4):841 + (h % 4)], b.pb[1][:, (h % 4) * 128:(h % 4) * 128 + 128],
                        ALU.mult, ALU.add),
                        waits=[("sP", sm)] if h == hs[0] else ())
                p.emit("vector", lambda e, hg=hg: e.tensor_copy(
                    b.BIGB[:, S0B + hg * 512:S0B + hg * 512 + 512], b.S2[:, hg * 512:hg * 512 + 512]))
                t11 = p.emit("vector", lambda e, hg=hg: e.tensor_copy(
                    b.FM[:, 1024 + hg * 512:1024 + hg * 512 + 512], b.pb[0][:, 0:512]), inc="sV")
                p.meta["gla_stph"] = t11
                p.meta["gla_oevph"] = t11
                p.meta["gla_t1p"] = p.now("sP")
            p.meta["gla_sc_used"] = p.now("sV")
            if GLA_TSTOP is not None or GLA_ONEGROUP or GLA_G1STOP is not None:
                break
            o = b.FM[:, 1024:2048]
            S = b.S1
            p.emit("vector", lambda e: e.tensor_mul(S[:, 0:1024], o, o),
                   waits=([("sDB", p.now("sDB"))] if aps.get("dbg") is not None else ()))
            p.emit("vector", lambda e: e.reduce_sum(S[:, 264:265], S[:, 0:1024], AX.X))
            gv = p.emit("vector", lambda e: e.tensor_scalar_mul(S[:, 264:265], S[:, 264:265],
                                                                1.0 / 1024.0), inc="sV")
            gs = p.emit("scalar", lambda e: e.activation(S[:, 265:266], S[:, 264:265], AF.Sqrt,
                                                         bias=b.EPSC[:, 0:1]),
                        waits=[("sV", gv)], inc="sA")
            gr = p.emit("vector", lambda e: e.reciprocal(S[:, 264:265], S[:, 265:266]),
                        waits=[("sA", gs)], inc="sV")
            go = p.emit("vector", lambda e, tsq=tsq: e.scalar_tensor_tensor(
                b.BIGB[:, 4096 + tsq * 1024:4096 + (tsq + 1) * 1024], o, S[:, 264:265],
                b.BIGB[:, GNW:GNW + 1024], ALU.mult, ALU.mult), waits=[("sV", gr)], inc="sV")
            for oc in range(8):
                pslot = b.ptr[4 + oc % 4]
                w_ = [("sV", go)]
                if p.meta.get("ptr_reads") is not None:
                    w_.append(("sV", p.meta["ptr_reads"]))
                tr3 = p.emit("tensor", lambda e, tsq=tsq, oc=oc, pslot=pslot: e.transpose(
                    pslot, b.BIGB[:, 4096 + tsq * 1024 + oc * 128:4096 + tsq * 1024 + (oc + 1) * 128],
                    b.IDN[:]), waits=w_, inc="sP")
                of = p.emit("vector", lambda e, tsq=tsq, oc=oc, pslot=pslot: e.tensor_copy(
                    b.BIGB[:, 8192 + oc * 512 + tsq * 128:8192 + oc * 512 + (tsq + 1) * 128], pslot),
                    waits=[("sP", tr3)], inc="sV")
                p.meta["ptr_reads"] = of
        ofrdy = p.now("sV")

        for dm in range(NDC if not GLA_TRUNC else 0):
            for kc in range(NDC):
                w_ = []
                if kc == 0:
                    w_ = [("sV", ofrdy)]
                    if p.meta.get("gla_wo") is not None:
                        w_.append(("sV", p.meta["gla_wo"]))
                wm = p.emit("tensor", lambda e, dm=dm, kc=kc: e.matmul(
                    b.pb[4][:], b.WQK[:, kc * 1024 + dm * 128:kc * 1024 + (dm + 1) * 128],
                    b.BIGB[:, 8192 + kc * 512:8192 + (kc + 1) * 512],
                    start=(kc == 0), stop=(kc == NDC - 1)),
                    waits=(w_ + (wordy if (dm == 0 and kc == 0) else [])),
                    inc="sP" if kc == NDC - 1 else None)
            w_ = [("sP", wm), ("sLX", p.now("sLX"))]
            if p.meta.get(f"gla_xo{dm % 2}") is not None:
                w_.append((f"sXo{dm % 2}", p.meta[f"gla_xo{dm % 2}"]))
            ev = p.emit("vector", lambda e, dm=dm: e.tensor_add(
                b.FM[:, dm % 2 * 512:dm % 2 * 512 + 512], b.pb[4][:],
                b.XC[:, dm * TC:(dm + 1) * TC]), waits=w_, inc="sV")
            p.meta["gla_wo"] = ev
            st = p.dma("gpsimd", aps["xout"][dm * 128:(dm + 1) * 128, tc * TC:(tc + 1) * TC],
                       b.FM[:, dm % 2 * 512:dm % 2 * 512 + 512], waits=[("sV", ev)],
                       inc=f"sXo{dm % 2}")
            p.meta[f"gla_xo{dm % 2}"] = st
        p.meta["pbc_free"] = [("sV", p.now("sV"))]
    return [(f"sXo{j}", p.now(f"sXo{j}")) for j in range(2)]


import ml_dtypes
from concourse.bass_utils import run_bass_kernel_spmd

BFNP = ml_dtypes.bfloat16
T_FULL = 4096


def pack_cols(v):
    return np.ascontiguousarray(v.reshape(-1, 128).T)


def make_rope_tables(T):
    dk = 64
    pos = np.arange(T, dtype=np.float64)
    kidx = np.arange(dk // 2, dtype=np.float64)
    ang = pos[:, None] / 10000.0 ** (2.0 * kidx / dk)
    cosT = np.zeros((128, T), np.float32)
    sinT = np.zeros((128, T), np.float32)
    for r in range(128):
        i = (r % 64) // 2
        cosT[r] = np.cos(ang[:, i])
        sinT[r] = (-1 if r % 2 == 0 else 1) * np.sin(ang[:, i])
    return cosT, sinT


def make_pswap():
    P = np.zeros((128, 128), np.float32)
    for i in range(64):
        P[2 * i, 2 * i + 1] = 1.0
        P[2 * i + 1, 2 * i] = 1.0
    return P.astype(BFNP)


def host_consts():
    return {
        "ident": np.eye(128, dtype=BFNP),
        "identf": np.eye(128, dtype=np.float32),
        "tril": np.tril(np.ones((128, 128), np.float32)).T.astype(BFNP),
    }


def host_gla(inp):
    g = lambda n: inp["L0_attn_a_" + n]
    return {
        "Wq": np.ascontiguousarray(g("Wq")), "Wk": np.ascontiguousarray(g("Wk")),
        "Wv": np.ascontiguousarray(g("Wv")), "Wr": np.ascontiguousarray(g("Wr")),
        "Wa": np.ascontiguousarray(g("Wa")), "Wo": np.ascontiguousarray(g("Wo")),
        "alog_r": np.ascontiguousarray(g("A_log").reshape(1, 8)),
        "dtb_r": np.ascontiguousarray(g("dt_bias").reshape(1, 8)),
        "qnw_c": np.ascontiguousarray(g("q_norm_w").reshape(128, 1)),
        "knw_c": np.ascontiguousarray(g("k_norm_w").reshape(128, 1)),
        "gnw_b": np.ascontiguousarray(np.broadcast_to(g("g_norm_w"), (128, 1024))),
    }


def host_ffn(inp):
    w1 = inp["L0_ffn_w1"]
    w2 = inp["L0_ffn_w2"]
    return {
        "w1T": np.ascontiguousarray(w1.T), "w1": np.ascontiguousarray(w1),
        "w2T": np.ascontiguousarray(w2.T), "w2": np.ascontiguousarray(w2),
        "b1c": pack_cols(inp["L0_ffn_b1"]), "b2c": pack_cols(inp["L0_ffn_b2"]),
    }


def host_attnb(inp):
    cols = np.zeros((128, 144), np.float32)
    cols[:, 0:8] = pack_cols(inp["L0_attn_b_a"])
    cols[:, 8:16] = pack_cols(inp["L0_attn_b_b"])
    cols[:, 16:24] = pack_cols(inp["L0_attn_b_conv_b"])
    cw = inp["L0_attn_b_conv_w"].reshape(1024, 3)
    for j in range(3):
        cols[:, 24 + j * 8:24 + (j + 1) * 8] = pack_cols(np.ascontiguousarray(cw[:, j]))
    cols[:, 48:56] = pack_cols(inp["L0_attn_b_norm_w"])
    cols[:, 56:64] = pack_cols(inp["L0_attn_b_mlp_norm_w"])
    cols[:, 64:96] = pack_cols(inp["L0_attn_b_b_gate"])
    cols[:, 96:128] = pack_cols(inp["L0_attn_b_b_up"])
    cols[:, 128:136] = pack_cols(inp["L0_attn_b_b_down"])
    return {
        "cols": cols,
        "w_gate": np.ascontiguousarray(inp["L0_attn_b_w_gate"]),
        "w_up": np.ascontiguousarray(inp["L0_attn_b_w_up"]),
        "w_down": np.ascontiguousarray(inp["L0_attn_b_w_down"]),
    }


def host_attna(inp):
    cosT, sinT = make_rope_tables(T_FULL)
    return {
        "wqT": np.ascontiguousarray(inp["L1_attn_a_Wq"].T),
        "wkT": np.ascontiguousarray(inp["L1_attn_a_Wk"].T),
        "wvT": np.ascontiguousarray(inp["L1_attn_a_Wv"].T),
        "woT": np.ascontiguousarray(inp["L1_attn_a_Wo"].T),
        "bqc": pack_cols(inp["L1_attn_a_bq"]),
        "bkc": pack_cols(inp["L1_attn_a_bk"]),
        "boc": pack_cols(inp["L1_attn_a_bo"]),
        "bvb": np.ascontiguousarray(np.broadcast_to(inp["L1_attn_a_bv"], (128, 1024))),
        "cosT": cosT, "sinT": sinT, "pswap": make_pswap(),
    }


def host_attnb2(inp):
    return {
        "wqkvT": np.ascontiguousarray(inp["L1_attn_b_w_qkv"].T),
        "woT": np.ascontiguousarray(inp["L1_attn_b_w_proj"].T),
    }


def host_moe(inp):
    return {
        "gate_w": np.ascontiguousarray(inp["L1_ffn_gate_w"]),
        "w1": np.ascontiguousarray(inp["L1_ffn_w1"]),
        "w2": np.ascontiguousarray(inp["L1_ffn_w2"]),
    }



def _declare(nc, host, T):
    aps = {}
    for name, arr in host.items():
        dt = BF16 if arr.dtype == BFNP else F32
        aps[name] = nc.dram_tensor(name, list(arr.shape), dt, kind="ExternalInput").ap()
    aps["xin"] = nc.dram_tensor("xin", [1024, T], F32, kind="ExternalInput").ap()
    aps["xout"] = nc.dram_tensor("xout", [1024, T], F32, kind="ExternalOutput").ap()
    return aps


def build_sub(kind, host, T):
    nc = bass.Bass("TRN2", target_bir_lowering=False)
    stack = ExitStack()
    stack.__enter__()
    p = Prog(nc, stack, T=T)
    b = Bufs(nc, stack)
    aps = _declare(nc, host, T)
    if kind == "gla":
        pass
    if kind == "ffn":
        aps["hd"] = nc.dram_tensor("hd", [4096, T], BF16, kind="Internal").ap()
    if kind == "attnb":
        aps["ud"] = nc.dram_tensor("ud", [1024, T], BF16, kind="Internal").ap()
        aps["hd"] = nc.dram_tensor("hd", [4096, T], BF16, kind="Internal").ap()
    if kind in ("attna", "attnb2"):
        aps["qd"] = nc.dram_tensor("qd", [1024, T], BF16, kind="Internal").ap()
        aps["xhd"] = nc.dram_tensor("xhd", [1024, T], BF16, kind="Internal").ap()
        if kind == "attnb2":
            aps["wqT"] = aps["wqkvT"][:, 0:1024]
            aps["wkT"] = aps["wqkvT"][:, 1024:2048]
            aps["wvT"] = aps["wqkvT"][:, 2048:3072]
    if kind == "moe":
        aps["accd"] = nc.dram_tensor("accd", [T, 1024], F32, kind="Internal").ap()
    emit_consts(p, b, aps)
    gate = p.meta["consts"]
    if kind == "gla":
        done = emit_gla(p, b, aps, gate=gate)
    elif kind == "ffn":
        done = emit_ffn_bin(p, b, aps, gate=gate)
    elif kind == "attnb":
        done = emit_attnb(p, b, aps, gate=gate)
    elif kind == "attna":
        done = emit_attn(p, b, aps, True, gate=gate)
    elif kind == "attnb2":
        done = emit_attn(p, b, aps, False, gate=gate)
    elif kind == "moe":
        done = emit_moe(p, b, aps, gate=gate)
    p.emit("sync", lambda e: e.nop(), waits=done)
    block = stack.enter_context(nc.Block())
    p.finalize(block)
    stack.__exit__(None, None, None)
    return nc



CHAIN = [
    ("gla", host_gla), ("ffn", host_ffn), ("attnb", host_attnb), ("ffn", host_ffn),
    ("attna", host_attna), ("moe", host_moe), ("attnb2", host_attnb2), ("moe", host_moe),
]


def build_fused(hosts, T, self_edges=False):
    nc = bass.Bass("TRN2", target_bir_lowering=False)
    stack = ExitStack()
    stack.__enter__()
    p = Prog(nc, stack, T=T)
    b = Bufs(nc, stack)
    aps_all = {}
    for kind, h in hosts.items():
        for name, arr in h.items():
            key = f"{kind}_{name}"
            dt = BF16 if arr.dtype == BFNP else F32
            aps_all[key] = nc.dram_tensor(key, list(arr.shape), dt, kind="ExternalInput").ap()
    cn = {"ident": np.eye(128), "identf": 0, "tril": 0}
    capz = {
        "ident": nc.dram_tensor("ident", [128, 128], BF16, kind="ExternalInput").ap(),
        "identf": nc.dram_tensor("identf", [128, 128], F32, kind="ExternalInput").ap(),
        "tril": nc.dram_tensor("tril", [128, 128], BF16, kind="ExternalInput").ap(),
    }
    xin = nc.dram_tensor("xin", [1024, T], F32, kind="ExternalInput").ap()
    xout = nc.dram_tensor("xout", [1024, T], F32, kind="ExternalOutput").ap()
    xd = [nc.dram_tensor(f"xd{i}", [1024, T], F32, kind="Internal").ap() for i in range(2)]
    scr = {
        "hd": nc.dram_tensor("hd", [4096, T], BF16, kind="Internal").ap(),
        "ud": nc.dram_tensor("ud", [1024, T], BF16, kind="Internal").ap(),
        "qd": nc.dram_tensor("qd", [1024, T], BF16, kind="Internal").ap(),
        "xhd": nc.dram_tensor("xhd", [1024, T], BF16, kind="Internal").ap(),
        "accd": nc.dram_tensor("accd", [T, 1024], F32, kind="Internal").ap(),
    }
    emit_consts(p, b, capz)
    done = p.meta["consts"]
    nsub = len(CHAIN)
    for i, (kind, _) in enumerate(CHAIN):
        aps = {name: aps_all[f"{kind}_{name}"] for name in hosts[kind]}
        aps.update(scr)
        if kind == "attnb2":
            aps["wqT"] = aps["wqkvT"][:, 0:1024]
            aps["wkT"] = aps["wqkvT"][:, 1024:2048]
            aps["wvT"] = aps["wqkvT"][:, 2048:3072]
        aps["xin"] = xin if i == 0 else xd[(i + 1) % 2]
        aps["xout"] = xout if i == nsub - 1 else xd[i % 2]
        p.barrier()
        if kind == "gla":
            done = emit_gla(p, b, aps)
        elif kind == "ffn":
            done = emit_ffn_bin(p, b, aps)
        elif kind == "attnb":
            done = emit_attnb(p, b, aps)
        elif kind == "attna":
            done = emit_attn(p, b, aps, True)
        elif kind == "attnb2":
            done = emit_attn(p, b, aps, False)
        elif kind == "moe":
            done = emit_moe(p, b, aps)
    p.emit("sync", lambda e: e.nop(), waits=done)
    block = stack.enter_context(nc.Block())
    p.finalize(block, self_edges=self_edges)
    stack.__exit__(None, None, None)
    return nc


def kernel(**inputs):
    T = T_FULL
    x = np.ascontiguousarray(
        np.asarray(inputs["L0_attn_a_hidden_states"], dtype=np.float32).transpose(0, 2, 1))
    consts = host_consts()
    hosts = {}
    for kind, hostf in CHAIN:
        if kind not in hosts:
            hosts[kind] = dict(hostf(inputs))
    nc = build_fused(hosts, T)
    flat = dict(consts)
    for kind, h in hosts.items():
        for name, arr in h.items():
            flat[f"{kind}_{name}"] = arr
    in_maps = [dict(flat, xin=np.ascontiguousarray(x[c])) for c in range(8)]
    res = run_bass_kernel_spmd(nc, in_maps, core_ids=list(range(8)))
    out = np.stack([np.asarray(res.results[c]["xout"], dtype=np.float32) for c in range(8)])
    return np.ascontiguousarray(out.transpose(0, 2, 1))


# version 06 (final)
# speedup vs baseline: 1.0516x; 1.0042x over previous
import numpy as np
from contextlib import ExitStack

import concourse.bass as bass
import concourse.mybir as mybir

F32 = mybir.dt.float32
BF16 = mybir.dt.bfloat16
AF = mybir.ActivationFunctionType
ALU = mybir.AluOpType
AX = mybir.AxisListType

ENGINES = ("sync", "vector", "scalar", "tensor", "gpsimd")

D = 1024
NDC = D // 128
TC = 512
GLA_TRUNC = False
GLA_TSTOP = None
GLA_ONEGROUP = False
GLA_G1STOP = None


class Prog:

    def __init__(self, nc, stack, T=4096):
        self.nc = nc
        self.stack = stack
        self.T = T
        self.NTC = T // TC
        self.ops = {e: [] for e in ENGINES}
        self._sems = {}
        self.counts = {}
        self.meta = {}

    def sem(self, name):
        if name not in self._sems:
            self._sems[name] = self.stack.enter_context(self.nc.semaphore(name))
            self.counts[name] = 0
        return name

    def emit(self, eng, fn, waits=(), inc=None, incn=1):
        at = None
        if inc is not None:
            self.sem(inc)
            self.counts[inc] += incn
            at = self.counts[inc]
        ws = []
        for s, v in waits:
            self.sem(s)
            if v and v > 0:
                ws.append((s, v))
        self.ops[eng].append([fn, tuple(ws), inc, incn, at])
        return at

    def dma(self, eng, out, in_, waits=(), inc=None, **kw):
        return self.emit(eng, lambda e: e.dma_start(out=out, in_=in_, **kw),
                         waits=waits, inc=inc, incn=16)

    def now(self, sem):
        self.sem(sem)
        return self.counts[sem]

    def barrier_counts(self):
        return [(s, c) for s, c in self.counts.items() if c > 0]

    def barrier(self):
        counts = self.barrier_counts()
        for eng in ENGINES:
            self.emit(eng, lambda e: e.nop(), waits=counts)

    def finalize(self, block, self_edges=False, edge_engines=None):
        if edge_engines is None:
            edge_engines = ENGINES if self_edges else ()
        edge_engines = tuple(set(edge_engines) | {"vector"})
        for e in edge_engines:
            self.sem(f"_se_{e}")
        sems = self._sems
        for eng_name in ENGINES:
            ops = self.ops[eng_name]

            def body(engine, ops=ops, eng_name=eng_name):
                se = eng_name in edge_engines
                last = None
                secount = 0
                for fn, waits, inc, incn, at in ops:
                    if se and last is not None:
                        engine.wait_ge(sems[last[0]], last[1])
                    for s, v in waits:
                        engine.wait_ge(sems[s], v)
                    ins = fn(engine)
                    if inc is not None:
                        ins.then_inc(sems[inc], incn)
                        last = (inc, at)
                    elif se:
                        secount += 1
                        ins.then_inc(sems[f"_se_{eng_name}"], 1)
                        last = (f"_se_{eng_name}", secount)
            getattr(block, eng_name)(body)


class Bufs:
    def __init__(self, nc, stack):
        en = stack.enter_context
        sb = lambda name, shape, dt: en(nc.sbuf_tensor(name, shape, dt))
        ps = lambda name, shape, dt: en(nc.psum_tensor(name, shape, dt))
        self.BIGA = sb("BIGA", [128, 32768], BF16)
        self.BIGB = sb("BIGB", [128, 33280], BF16)
        self.XC = sb("XC", [128, 4096], F32)
        self.XB = sb("XB", [128, 4096], BF16)
        self.STB = sb("STB", [128, 4096], BF16)
        self.FM = sb("FM", [128, 2048], F32)
        self.WS = sb("WS", [128, 4096], BF16)
        self.S1 = sb("S1", [128, 1024], F32)
        self.S2 = sb("S2", [128, 1024], F32)
        self.WQK = sb("WQK", [128, 8192], BF16)
        self.CT = self.WS[:, 512:1024]
        self.ST = self.WS[:, 1024:1536]
        self.COLS = sb("COLS", [128, 256], F32)
        self.ROWS = sb("ROWS", [1, 1024], F32)
        self.ONEB = sb("ONEB", [128, 8], BF16)
        self.ONEF = sb("ONEF", [128, 8], F32)
        self.EPSC = sb("EPSC", [128, 4], F32)
        self.IDN = sb("IDN", [128, 128], BF16)
        self.IDF = sb("IDF", [128, 128], F32)
        self.TRIL = sb("TRIL", [128, 128], BF16)
        self.pb = [ps(f"pb{i}", [128, 512], F32) for i in range(5)]
        self.pbc = [ps(f"pbc{i}", [128, 512], F32) for i in range(2)]
        self.PTR = ps("PTR", [128, 1024], BF16)
        self.ptr = [self.PTR[:, i * 128:(i + 1) * 128] for i in range(8)]


def emit_consts(p: Prog, b: Bufs, aps):
    p.emit("vector", lambda e: e.memset(b.ONEB[:], 1.0))
    p.emit("vector", lambda e: e.memset(b.ONEF[:], 1.0))
    p.emit("vector", lambda e: e.memset(b.EPSC[:, 0:1], 1e-5))
    p.emit("vector", lambda e: e.memset(b.EPSC[:, 1:2], 1e-6))
    p.emit("vector", lambda e: e.memset(b.EPSC[:, 2:3], 1e-9))
    p.emit("vector", lambda e: e.memset(b.EPSC[:, 3:4], 1e-8), inc="sV")
    p.dma("sync", b.IDN[:], aps["ident"], inc="sLC")
    p.dma("sync", b.IDF[:], aps["identf"], inc="sLC")
    p.dma("sync", b.TRIL[:], aps["tril"], inc="sLC")
    p.meta["consts"] = [("sV", p.now("sV")), ("sLC", p.now("sLC"))]


def emit_ln(p: Prog, b: Bufs, eps_slice, gate=(), dst=None, lo_dst=None):
    pbcw = p.meta.get("pbc_free", [])
    for k in range(NDC):
        sl = slice(k * TC, (k + 1) * TC)
        vW = p.emit("vector", lambda e, sl=sl: e.tensor_copy(b.XB[:, sl], b.XC[:, sl]),
                    waits=gate if k == 0 else (), inc="sV" if k == NDC - 1 else None)
        aW = p.emit("scalar", lambda e, sl=sl: e.activation(b.STB[:, sl], b.XC[:, sl], AF.Square),
                    waits=gate if k == 0 else (), inc="sA" if k == NDC - 1 else None)
    for k in range(NDC):
        sl = slice(k * TC, (k + 1) * TC)
        w_ = ([("sV", vW)] + pbcw) if k == 0 else ()
        p.emit("tensor", lambda e, sl=sl, k=k: e.matmul(
            b.pbc[0][0:1, :], b.ONEB[:, 0:1], b.XB[:, sl], start=(k == 0), stop=(k == NDC - 1)),
            waits=w_)
    for k in range(NDC):
        sl = slice(k * TC, (k + 1) * TC)
        w_ = [("sA", aW)] if k == 0 else ()
        mmW = p.emit("tensor", lambda e, sl=sl, k=k: e.matmul(
            b.pbc[1][0:1, :], b.ONEB[:, 0:1], b.STB[:, sl], start=(k == 0), stop=(k == NDC - 1)),
            waits=w_, inc="sP" if k == NDC - 1 else None)
    R = b.ROWS
    Sx = b.S1[0:1, 0:512]
    p.emit("vector", lambda e: e.tensor_scalar_mul(R[:, 0:512], b.pbc[0][0:1, :], 1.0 / D),
           waits=[("sP", mmW)])
    p.emit("vector", lambda e: e.tensor_scalar_mul(R[:, 512:1024], b.pbc[1][0:1, :], 1.0 / D))
    p.emit("vector", lambda e: e.tensor_mul(Sx, R[:, 0:512], R[:, 0:512]))
    rv = p.emit("vector", lambda e: e.tensor_sub(Sx, R[:, 512:1024], Sx), inc="sV")
    sq = p.emit("scalar", lambda e: e.activation(R[:, 512:1024], Sx, AF.Sqrt,
                                                 bias=eps_slice), waits=[("sV", rv)], inc="sA")
    rc = p.emit("vector", lambda e: e.reciprocal(Sx, R[:, 512:1024]),
                waits=[("sA", sq)], inc="sV")
    p.emit("tensor", lambda e: e.matmul(b.pbc[0][:], b.ONEF[0:1, 0:1].broadcast_to([1, 128]),
                                        R[:, 0:512], start=True, stop=True),
           waits=[("sV", rc)])
    bcW = p.emit("tensor", lambda e: e.matmul(b.pbc[1][:], b.ONEF[0:1, 0:1].broadcast_to([1, 128]),
                                              b.S1[0:1, 0:512], start=True, stop=True), inc="sP")
    for k in range(NDC):
        sl = slice(k * TC, (k + 1) * TC)
        out_ap = b.XB[:, sl] if dst is None else dst[k]
        p.emit("vector", lambda e, sl=sl: e.tensor_sub(b.S1[:, 0:512], b.XC[:, sl], b.pbc[0][:]),
               waits=[("sP", bcW)] if k == 0 else ())
        w = p.emit("vector", lambda e, out_ap=out_ap: e.tensor_mul(out_ap, b.S1[:, 0:512], b.pbc[1][:]),
                   inc="sV")
        if lo_dst is not None:
            p.emit("vector", lambda e: e.tensor_mul(b.S1[:, 512:1024], b.S1[:, 0:512], b.pbc[1][:]))
            w = p.emit("vector", lambda e, out_ap=out_ap, k=k: e.tensor_sub(
                lo_dst[k], b.S1[:, 512:1024], out_ap), inc="sV")
    p.meta["ln_done"] = [("sV", w)]
    p.meta["pbc_free"] = [("sV", w)]
    return p.meta["ln_done"]


def load_xc(p: Prog, b: Bufs, xin, tc, gate=()):
    for k in range(NDC):
        ldx = p.dma("sync", b.XC[:, k * TC:(k + 1) * TC],
                    xin[k * 128:(k + 1) * 128, tc * TC:(tc + 1) * TC],
                    waits=gate if k == 0 else (), inc="sLX")
    return [("sLX", ldx)]


def emit_binarize(p: Prog, b: Bufs, dst, dst_cols_per_k, wT_ap, w_ap, K, M, alpha_col0, gate=()):
    nK = K // 128
    nM = M // 128
    colchunk = 2048
    ncc = (M + colchunk - 1) // colchunk
    slot_ready = [None, None]
    si = 0
    for k in range(nK):
        for c in range(ncc):
            cs = slice(c * colchunk, min(M, (c + 1) * colchunk))
            ncols = cs.stop - cs.start
            xsl = slice(si * 2048, si * 2048 + ncols)
            waits = list(gate) if (k == 0 and c == 0) else []
            if slot_ready[si] is not None:
                waits.append(("sA", slot_ready[si]))
            ldw = p.dma("sync", b.XC[:, xsl], wT_ap[k * 128:(k + 1) * 128, cs],
                        waits=waits, inc=f"bsg{si}")
            dsl = slice(k * dst_cols_per_k + cs.start, k * dst_cols_per_k + cs.stop)
            slot_ready[si] = p.emit(
                "scalar", lambda e, dsl=dsl, xsl=xsl: e.activation(dst[:, dsl], b.XC[:, xsl], AF.Sign),
                waits=[(f"bsg{si}", ldw)], inc="sA")
            si ^= 1
    fslot = [None, None]
    fi = 0
    nc2 = max(1, K // 1024)
    for m in range(nM):
        for c2 in range(nc2):
            fsl = slice(fi * 1024, (fi + 1) * 1024)
            waits = list(gate) if (m == 0 and c2 == 0) else []
            if fslot[fi] is not None:
                waits.append(("sV", fslot[fi]))
            ldn = p.dma("sync", b.FM[:, fsl], w_ap[m * 128:(m + 1) * 128, c2 * 1024:(c2 + 1) * 1024],
                        waits=waits, inc=f"bna{fi}")
            col = slice(alpha_col0 + m, alpha_col0 + m + 1)
            if c2 == 0:
                fslot[fi] = p.emit("vector", lambda e, fsl=fsl, col=col: e.tensor_reduce(
                    b.COLS[:, col], b.FM[:, fsl], AX.X, ALU.add, apply_absolute_value=True),
                    waits=[(f"bna{fi}", ldn)], inc="sV")
            else:
                p.emit("vector", lambda e, fsl=fsl: e.tensor_reduce(
                    b.S2[:, 0:1], b.FM[:, fsl], AX.X, ALU.add, apply_absolute_value=True),
                    waits=[(f"bna{fi}", ldn)])
                fslot[fi] = p.emit("vector", lambda e, col=col: e.tensor_add(
                    b.COLS[:, col], b.COLS[:, col], b.S2[:, 0:1]), inc="sV")
            fi ^= 1
    w = p.emit("vector", lambda e: e.tensor_scalar_mul(
        b.COLS[:, alpha_col0:alpha_col0 + nM], b.COLS[:, alpha_col0:alpha_col0 + nM], 1.0 / K),
        inc="sV")
    return [("sV", w), ("sA", p.now("sA"))]


def emit_ffn_bin(p: Prog, b: Bufs, aps, gate=()):
    NTC = p.NTC
    g1 = emit_binarize(p, b, b.BIGA, 4096, aps["w1T"], aps["w1"], 1024, 4096, 0, gate=gate)
    g2 = emit_binarize(p, b, b.BIGB, 1024, aps["w2T"], aps["w2"], 4096, 1024, 64,
                       gate=[("sA", p.now("sA")), ("sV", p.now("sV"))])
    ld1 = p.dma("sync", b.COLS[:, 96:128], aps["b1c"], inc="sLB")
    ld2 = p.dma("sync", b.COLS[:, 88:96], aps["b2c"], inc="sLB")
    wgt = g1 + g2 + [("sLB", ld2)]

    bank_free = [None, None]
    stb_free = [None] * 8
    last_mm = None
    for tc in range(NTC):
        xg = load_xc(p, b, aps["xin"], tc,
                     gate=(wgt if tc == 0 else [("sV", p.now("sV")), ("sA", p.now("sA"))]))
        lng = xg + ([("sP", last_mm)] if last_mm is not None else [])
        lng += [(f"sHo{s}", p.now(f"sHo{s}")) for s in range(8)]
        xw = emit_ln(p, b, b.EPSC[0:1, 0:1], gate=lng)
        for m in range(32):
            bank = b.pb[m % 2]
            for k in range(NDC):
                w_ = []
                if k == 0:
                    w_ = list(xw)
                    if bank_free[m % 2] is not None:
                        w_.append(("sA", bank_free[m % 2]))
                last_mm = p.emit("tensor", lambda e, m=m, k=k, bank=bank: e.matmul(
                    bank[:], b.BIGA[:, k * 4096 + m * 128:k * 4096 + (m + 1) * 128],
                    b.XB[:, k * TC:(k + 1) * TC], start=(k == 0), stop=(k == NDC - 1)),
                    waits=w_, inc="sP" if k == NDC - 1 else None)
            sl = slice((m % 8) * TC, (m % 8 + 1) * TC)
            ew = [("sP", last_mm)]
            if stb_free[m % 8] is not None:
                ew.append((f"sHo{m % 8}", stb_free[m % 8]))
            ev = p.emit("scalar", lambda e, sl=sl, m=m, bank=bank: e.activation(
                b.STB[:, sl], bank[:], AF.Gelu,
                bias=b.COLS[:, 96 + m:97 + m], scale=b.COLS[:, m:m + 1]),
                waits=ew, inc="sA")
            bank_free[m % 2] = ev
            st = p.dma("gpsimd", aps["hd"][m * 128:(m + 1) * 128, tc * TC:(tc + 1) * TC],
                       b.STB[:, sl], waits=[("sA", ev)], inc=f"sHo{m % 8}")
            stb_free[m % 8] = st

    all_h = [(f"sHo{s}", p.now(f"sHo{s}")) for s in range(8)]
    fm_free = [None] * 4
    stbD_free = [None] * 8
    first = True
    for tc in range(NTC):
        xg = load_xc(p, b, aps["xin"], tc,
                     gate=[("sV", p.now("sV")), ("sA", p.now("sA"))])
        for grp in range(2):
            kcW = None
            for kc in range(32):
                sl = slice((kc % 8) * TC, (kc % 8 + 1) * TC)
                w_ = list(all_h) if (tc == 0 and grp == 0 and kc < 8) else []
                first = False
                if stbD_free[kc % 8] is not None:
                    w_.append(("sP", stbD_free[kc % 8]))
                ldh = p.dma("sync", b.STB[:, sl],
                            aps["hd"][kc * 128:(kc + 1) * 128, tc * TC:(tc + 1) * TC],
                            waits=w_, inc=f"sLH{kc % 8}")
                for j in range(4):
                    dm = grp * 4 + j
                    ww = []
                    if j == 0:
                        ww.append((f"sLH{kc % 8}", ldh))
                    if kc == 0 and j == 0:
                        if fm_free[0] is not None:
                            ww.append(("sV", p.meta.get("ffnD_evac", 0)))
                        else:
                            ww.append(("sA", p.now("sA")))
                    kcW = p.emit("tensor", lambda e, kc=kc, dm=dm, j=j, sl=sl: e.matmul(
                        b.pb[j][:], b.BIGB[:, kc * 1024 + dm * 128:kc * 1024 + (dm + 1) * 128],
                        b.STB[:, sl], start=(kc == 0), stop=(kc == 31)),
                        waits=ww, inc="sP" if j == 3 else None)
                stbD_free[kc % 8] = kcW
            for j in range(4):
                dm = grp * 4 + j
                fs = slice(j * TC, (j + 1) * TC)
                w_ = [("sP", kcW)] + xg
                if fm_free[j] is not None:
                    w_.append((f"sXo{j}", fm_free[j]))
                p.emit("vector", lambda e, dm=dm, j=j, fs=fs: e.tensor_scalar(
                    b.FM[:, fs], b.pb[j][:], b.COLS[:, 64 + dm:65 + dm],
                    b.COLS[:, 88 + dm:89 + dm], ALU.mult, ALU.add), waits=w_)
                ev = p.emit("vector", lambda e, dm=dm, fs=fs: e.tensor_add(
                    b.FM[:, fs], b.FM[:, fs], b.XC[:, dm * TC:(dm + 1) * TC]), inc="sV")
                p.meta["ffnD_evac"] = ev
                st = p.dma("gpsimd", aps["xout"][dm * 128:(dm + 1) * 128, tc * TC:(tc + 1) * TC],
                           b.FM[:, fs], waits=[("sV", ev)], inc=f"sXo{j}")
                fm_free[j] = st
    return [(f"sXo{j}", p.now(f"sXo{j}")) for j in range(4)]


def emit_moe(p: Prog, b: Bufs, aps, gate=()):
    NTC = p.NTC
    T = p.T
    ntsub = T // 128

    for tc in range(NTC):
        xg = load_xc(p, b, aps["xin"], tc,
                     gate=(list(gate) + p.meta.get("pbc_free", []) if tc == 0
                           else [("sV", p.now("sV")), ("sA", p.now("sA"))]))
        dst = [b.BIGA[:, k * T + tc * TC:k * T + (tc + 1) * TC] for k in range(NDC)]
        xw = emit_ln(p, b, b.EPSC[0:1, 0:1], gate=xg, dst=dst)

    ldg = p.dma("gpsimd", b.WS[:, 0:64].rearrange("p (k c) -> p k c", k=8),
                aps["gate_w"].rearrange("(k p) c -> p k c", p=128), inc="sLG")
    for ts in range(ntsub):
        w_ = (list(xw) + [("sLG", ldg)]) if ts == 0 else [("sV", p.meta.get("moe_top2", 0))]
        for k in range(NDC):
            mm = p.emit("tensor", lambda e, k=k, ts=ts: e.matmul(
                b.pb[4][:, 0:8], b.BIGA[:, k * T + ts * 128:k * T + (ts + 1) * 128],
                b.WS[:, k * 8:(k + 1) * 8], start=(k == 0), stop=(k == NDC - 1)),
                waits=w_ if k == 0 else (), inc="sP" if k == NDC - 1 else None)
        S = b.S1
        p.emit("vector", lambda e: e.reduce_max(S[:, 0:1], b.pb[4][:, 0:8], AX.X),
               waits=[("sP", mm)])
        p.emit("vector", lambda e: e.tensor_single_scalar(S[:, 8:16], b.pb[4][:, 0:8], S[:, 0:1], ALU.is_ge))
        p.emit("vector", lambda e: e.scalar_tensor_tensor(S[:, 16:24], S[:, 8:16], -1e30, b.pb[4][:, 0:8], ALU.mult, ALU.add))
        p.emit("vector", lambda e: e.reduce_max(S[:, 1:2], S[:, 16:24], AX.X))
        sv = p.emit("vector", lambda e: e.tensor_sub(S[:, 2:3], S[:, 0:1], S[:, 1:2]), inc="sV")
        sg = p.emit("scalar", lambda e: e.activation(S[:, 3:4], S[:, 2:3], AF.Sigmoid),
                    waits=[("sV", sv)], inc="sA")
        p.emit("vector", lambda e: e.tensor_scalar(S[:, 4:5], S[:, 3:4], -1.0, 1.0, ALU.mult, ALU.add),
               waits=[("sA", sg)])
        p.emit("vector", lambda e: e.tensor_single_scalar(S[:, 24:32], S[:, 16:24], S[:, 1:2], ALU.is_ge))
        p.emit("vector", lambda e: e.tensor_single_scalar(S[:, 8:16], S[:, 8:16], S[:, 3:4], ALU.mult))
        p.emit("vector", lambda e: e.tensor_single_scalar(S[:, 24:32], S[:, 24:32], S[:, 4:5], ALU.mult))
        tv = p.emit("vector", lambda e, ts=ts: e.tensor_add(b.S2[:, ts * 8:(ts + 1) * 8], S[:, 8:16], S[:, 24:32]),
                    inc="sV")
        p.meta["moe_top2"] = tv

    W1C = 16384
    fm_free = [None] * 4
    for e in range(8):
        wld = [("sP", p.now("sP"))] if e > 0 else list(gate)
        for k in range(NDC):
            p.dma("gpsimd", b.BIGB[:, k * 2048:(k + 1) * 2048],
                  aps["w1"][e, k * 128:(k + 1) * 128, :], waits=wld if k == 0 else (),
                  inc="sLE")
        for hm in range(16):
            l2 = p.dma("gpsimd", b.BIGB[:, W1C + hm * 1024:W1C + (hm + 1) * 1024],
                       aps["w2"][e, hm * 128:(hm + 1) * 128, :], inc="sLE")
        ewg = [("sLE", l2)]
        prev_acc = [(f"sAcc{j}", p.now(f"sAcc{j}")) for j in range(4)]
        first_store = True
        for tc in range(NTC):
            for hm in range(16):
                bank = b.pb[hm % 2]
                for k in range(NDC):
                    w_ = []
                    if k == 0:
                        if tc == 0 and hm == 0:
                            w_ += list(ewg)
                            if e == 0:
                                w_ += list(xw) + [("sV", p.meta["moe_top2"])]
                        if p.meta.get(f"moe_bank{hm % 2}") is not None:
                            w_.append(("sA", p.meta[f"moe_bank{hm % 2}"]))
                    mm = p.emit("tensor", lambda e_, hm=hm, k=k, tc=tc, bank=bank: e_.matmul(
                        bank[:], b.BIGB[:, k * 2048 + hm * 128:k * 2048 + (hm + 1) * 128],
                        b.BIGA[:, k * T + tc * TC:k * T + (tc + 1) * TC],
                        start=(k == 0), stop=(k == NDC - 1)),
                        waits=w_, inc="sP" if k == NDC - 1 else None)
                dstap = (b.STB[:, (hm % 8) * TC:(hm % 8 + 1) * TC] if hm < 8 else
                         b.XB[:, (hm - 8) * TC:(hm - 8 + 1) * TC])
                ev = p.emit("scalar", lambda e_, dstap=dstap, bank=bank: e_.activation(
                    dstap, bank[:], AF.Silu), waits=[("sP", mm)], inc="sA")
                p.meta[f"moe_bank{hm % 2}"] = ev
            hready = p.now("sA")
            for tsq in range(4):
                ts_global = tc * 4 + tsq
                for dn in range(2):
                    bank = b.pb[2 + dn]
                    for hm in range(16):
                        w_ = []
                        if hm == 0:
                            w_ = [("sA", hready)]
                            if p.meta.get(f"moe_ybank{dn}") is not None:
                                w_.append(("sV", p.meta[f"moe_ybank{dn}"]))
                        lhs = (b.STB[:, (hm % 8) * TC + tsq * 128:(hm % 8) * TC + tsq * 128 + 128]
                               if hm < 8 else
                               b.XB[:, (hm - 8) * TC + tsq * 128:(hm - 8) * TC + tsq * 128 + 128])
                        ymm = p.emit("tensor", lambda e_, lhs=lhs, hm=hm, dn=dn, bank=bank: e_.matmul(
                            bank[:], lhs,
                            b.BIGB[:, W1C + hm * 1024 + dn * 512:W1C + hm * 1024 + (dn + 1) * 512],
                            start=(hm == 0), stop=(hm == 15)),
                            waits=w_, inc="sP" if hm == 15 else None)
                    j = (tsq * 2 + dn) % 4
                    fs = slice(j * TC, (j + 1) * TC)
                    cidx = ts_global * 8 + e
                    w_ = [("sP", ymm)]
                    if fm_free[j] is not None:
                        w_.append((f"sAcc{j}", fm_free[j]))
                    yev = p.emit("vector", lambda e_, fs=fs, cidx=cidx, bank=bank: e_.tensor_single_scalar(
                        b.FM[:, fs], bank[:], b.S2[:, cidx:cidx + 1], ALU.mult),
                        waits=w_, inc="sV")
                    p.meta[f"moe_ybank{dn}"] = yev
                    stw = [("sV", yev)]
                    if first_store and e > 0:
                        stw += prev_acc
                        first_store = False
                    st = p.dma("gpsimd",
                               aps["accd"][ts_global * 128:(ts_global + 1) * 128, dn * 512:(dn + 1) * 512],
                               b.FM[:, fs], waits=stw,
                               inc=f"sAcc{j}", accum_op=(ALU.bypass if e == 0 else ALU.add))
                    fm_free[j] = st

    allacc = [(f"sAcc{j}", p.now(f"sAcc{j}")) for j in range(4)]
    engines_done = [("sV", p.now("sV")), ("sA", p.now("sA"))]
    s1_free = [None] * 8
    xcf_free = [None] * 8
    fmF_free = [None] * 4
    pbF_free = [None] * 4
    ti = 0
    for dc in range(NDC):
        for tt in range(T // 128):
            s1s = ti % 8
            xcs = ti % 8
            j = ti % 4
            w_ = (list(allacc) + list(engines_done)) if ti == 0 else []
            if s1_free[s1s] is not None:
                w_.append(("sP", s1_free[s1s]))
            lda = p.dma("sync", b.S1[:, s1s * 128:(s1s + 1) * 128],
                        aps["accd"][tt * 128:(tt + 1) * 128, dc * 128:(dc + 1) * 128],
                        waits=w_, inc=f"sFA{s1s}")
            w_ = []
            if xcf_free[xcs] is not None:
                w_.append(("sV", xcf_free[xcs]))
            ldx = p.dma("sync", b.XC[:, xcs * 128:(xcs + 1) * 128],
                        aps["xin"][dc * 128:(dc + 1) * 128, tt * 128:(tt + 1) * 128],
                        waits=w_, inc=f"sFX{xcs}")
            w_ = [(f"sFA{s1s}", lda)]
            if ti < 4:
                w_ += engines_done
            if pbF_free[j] is not None:
                w_.append(("sV", pbF_free[j]))
            tr = p.emit("tensor", lambda e_, s1s=s1s, j=j: e_.transpose(
                b.pb[j][:, 0:128], b.S1[:, s1s * 128:(s1s + 1) * 128], b.IDF[:]),
                waits=w_, inc="sP")
            s1_free[s1s] = tr
            w_ = [("sP", tr), (f"sFX{xcs}", ldx)]
            if fmF_free[j] is not None:
                w_.append((f"sFO{j}", fmF_free[j]))
            ad = p.emit("vector", lambda e_, j=j, xcs=xcs: e_.tensor_add(
                b.FM[:, j * 128:(j + 1) * 128], b.pb[j][:, 0:128],
                b.XC[:, xcs * 128:(xcs + 1) * 128]), waits=w_, inc="sV")
            pbF_free[j] = ad
            xcf_free[xcs] = ad
            st = p.dma("gpsimd", aps["xout"][dc * 128:(dc + 1) * 128, tt * 128:(tt + 1) * 128],
                       b.FM[:, j * 128:(j + 1) * 128], waits=[("sV", ad)], inc=f"sFO{j}")
            fmF_free[j] = st
            ti += 1
    return [(f"sFO{j}", p.now(f"sFO{j}")) for j in range(4)]


def emit_rms(p: Prog, b: Bufs, src, dst, wcol, eps_slice, gate=()):
    pbcw = p.meta.get("pbc_free", [])
    for k in range(NDC):
        sl = slice(k * TC, (k + 1) * TC)
        aW = p.emit("scalar", lambda e, sl=sl, k=k: e.activation(b.STB[:, sl], src(k), AF.Square),
                    waits=gate if k == 0 else (), inc="sA" if k == NDC - 1 else None)
    for k in range(NDC):
        sl = slice(k * TC, (k + 1) * TC)
        w_ = ([("sA", aW)] + pbcw) if k == 0 else ()
        mmW = p.emit("tensor", lambda e, sl=sl, k=k: e.matmul(
            b.pbc[0][0:1, :], b.ONEB[:, 0:1], b.STB[:, sl], start=(k == 0), stop=(k == NDC - 1)),
            waits=w_, inc="sP" if k == NDC - 1 else None)
    R = b.ROWS
    sv = p.emit("vector", lambda e: e.tensor_scalar_mul(R[:, 0:512], b.pbc[0][0:1, :], 1.0 / D),
                waits=[("sP", mmW)], inc="sV")
    sq = p.emit("scalar", lambda e: e.activation(R[:, 512:1024], R[:, 0:512], AF.Sqrt,
                                                 bias=eps_slice), waits=[("sV", sv)], inc="sA")
    rc = p.emit("vector", lambda e: e.reciprocal(R[:, 0:512], R[:, 512:1024]),
                waits=[("sA", sq)], inc="sV")
    bcW = p.emit("tensor", lambda e: e.matmul(b.pbc[0][:], b.ONEF[0:1, 0:1].broadcast_to([1, 128]),
                                              R[:, 0:512], start=True, stop=True),
                 waits=[("sV", rc)], inc="sP")
    for k in range(NDC):
        w = p.emit("vector", lambda e, k=k: e.scalar_tensor_tensor(
            dst(k), src(k), wcol(k), b.pbc[0][:], ALU.mult, ALU.mult),
            waits=[("sP", bcW)] if k == 0 else (), inc="sV")
    p.meta["pbc_free"] = [("sV", w)]
    return [("sV", w)]


def emit_attnb(p: Prog, b: Bufs, aps, gate=()):
    NTC = p.NTC
    T = p.T
    C = b.COLS
    ldc = p.dma("sync", C[:, 0:144], aps["cols"], waits=list(gate), inc="sLB")
    th = p.emit("scalar", lambda e: e.activation(C[:, 0:8], C[:, 0:8], AF.Tanh),
                waits=[("sLB", ldc)], inc="sA")

    for tc in range(NTC):
        xg = load_xc(p, b, aps["xin"], tc,
                     gate=(list(gate) + p.meta.get("pbc_free", []) if tc == 0
                           else [("sV", p.now("sV")), ("sA", p.now("sA"))]))
        dstb = [b.BIGB[:, k * T + tc * TC:k * T + (tc + 1) * TC] for k in range(NDC)]
        xw = emit_ln(p, b, b.EPSC[0:1, 0:1], gate=xg, dst=dstb)
        yw = emit_rms(p, b,
                      src=lambda k, tc=tc: b.BIGB[:, k * T + tc * TC:k * T + (tc + 1) * TC],
                      dst=lambda k, tc=tc: b.BIGA[:, k * T + tc * TC:k * T + (tc + 1) * TC],
                      wcol=lambda k: C[:, 48 + k:49 + k],
                      eps_slice=b.EPSC[0:1, 1:2], gate=xw)

    ugate = [("sA", th)] + yw
    for k in range(NDC):
        ys = lambda lo, hi, k=k: b.BIGA[:, k * T + lo:k * T + hi]
        bw = p.emit("vector", lambda e, k=k, ys=ys: e.tensor_single_scalar(
            b.STB[:, 0:T], ys(0, T), C[:, 8 + k:9 + k], ALU.mult),
            waits=ugate if k == 0 else (), inc="sV")
        sc = p.emit("vector", lambda e, k=k: e.tensor_tensor_scan(
            b.XB[:, 0:T], C[:, 0 + k:1 + k].broadcast_to([128, T]), b.STB[:, 0:T],
            0.0, ALU.mult, ALU.add), inc="sV")
        for tc in range(NTC):
            lo, hi = tc * TC, (tc + 1) * TC
            fs = slice((tc % 2) * TC, (tc % 2) * TC + TC)
            fw = []
            if p.meta.get(f"ab_fm{tc % 2}") is not None:
                fw.append((f"sUo{tc % 2}", p.meta[f"ab_fm{tc % 2}"]))
            if tc == 0:
                p.emit("vector", lambda e, fs=fs: e.memset(b.FM[:, fs.start:fs.start + 1], 0.0),
                       waits=fw)
                p.emit("vector", lambda e, k=k, fs=fs, lo=lo, hi=hi, ys=ys: e.tensor_single_scalar(
                    b.FM[:, fs.start + 1:fs.stop], ys(lo, hi - 1), C[:, 24 + k:25 + k], ALU.mult))
            else:
                p.emit("vector", lambda e, k=k, fs=fs, lo=lo, hi=hi, ys=ys: e.tensor_single_scalar(
                    b.FM[:, fs], ys(lo - 1, hi - 1), C[:, 24 + k:25 + k], ALU.mult),
                    waits=fw)
            p.emit("vector", lambda e, k=k, fs=fs, lo=lo, hi=hi, ys=ys: e.scalar_tensor_tensor(
                b.FM[:, fs], ys(lo, hi), C[:, 32 + k:33 + k], b.FM[:, fs], ALU.mult, ALU.add))
            if tc == NTC - 1:
                p.emit("vector", lambda e, k=k, fs=fs, lo=lo, hi=hi, ys=ys: e.scalar_tensor_tensor(
                    b.FM[:, fs.start:fs.stop - 1], ys(lo + 1, hi), C[:, 40 + k:41 + k],
                    b.FM[:, fs.start:fs.stop - 1], ALU.mult, ALU.add))
            else:
                p.emit("vector", lambda e, k=k, fs=fs, lo=lo, hi=hi, ys=ys: e.scalar_tensor_tensor(
                    b.FM[:, fs], ys(lo + 1, hi + 1), C[:, 40 + k:41 + k], b.FM[:, fs],
                    ALU.mult, ALU.add))
            p.emit("vector", lambda e, k=k, fs=fs: e.tensor_scalar_add(
                b.FM[:, fs], b.FM[:, fs], C[:, 16 + k:17 + k]))
            p.emit("vector", lambda e, k=k, fs=fs, lo=lo, hi=hi: e.tensor_add(
                b.FM[:, fs], b.FM[:, fs], b.BIGB[:, k * T + lo:k * T + hi]))
            p.emit("vector", lambda e, fs=fs, lo=lo, hi=hi: e.tensor_add(
                b.FM[:, fs], b.FM[:, fs], b.XB[:, lo:hi]))
            uv = p.emit("vector", lambda e, k=k, fs=fs, lo=lo, hi=hi: e.tensor_copy(
                b.BIGB[:, k * T + lo:k * T + hi], b.FM[:, fs]), inc="sV")
            st = p.dma("gpsimd", aps["ud"][k * 128:(k + 1) * 128, lo:hi],
                       b.BIGB[:, k * T + lo:k * T + hi], waits=[("sV", uv)], inc=f"sUo{tc % 2}")
            p.meta[f"ab_fm{tc % 2}"] = st
    uall = [(f"sUo{s}", p.now(f"sUo{s}")) for s in range(2)]

    wg0 = [("sV", p.now("sV"))]
    for k in range(NDC):
        lg = p.dma("gpsimd", b.BIGA[:, k * 4096:(k + 1) * 4096],
                   aps["w_gate"][k * 128:(k + 1) * 128, :], waits=wg0 if k == 0 else (),
                   inc="sLWG")
    wu0 = uall
    for k in range(NDC):
        lu = p.dma("gpsimd", b.BIGB[:, k * 4096:(k + 1) * 4096],
                   aps["w_up"][k * 128:(k + 1) * 128, :], waits=wu0 if k == 0 else (),
                   inc="sLWU")
    wboth = [("sLWG", lg), ("sLWU", lu)]
    bank_free = {}
    stb_free = [None] * 8
    last_mm = None
    for tc in range(NTC):
        ulw = [("sV", p.now("sV")), ("sA", p.now("sA"))]
        ulw += [(f"sHo{s}", p.now(f"sHo{s}")) for s in range(8)]
        if tc == 0:
            ulw += uall
        for k in range(NDC):
            ldu = p.dma("sync", b.STB[:, k * TC:(k + 1) * TC],
                        aps["ud"][k * 128:(k + 1) * 128, tc * TC:(tc + 1) * TC],
                        waits=ulw if k == 0 else (), inc="sLU")
        pbcw = p.meta.get("pbc_free", [])
        for k in range(NDC):
            sl = slice(k * TC, (k + 1) * TC)
            aW = p.emit("scalar", lambda e, sl=sl: e.activation(b.XB[:, sl], b.STB[:, sl], AF.Square),
                        waits=([("sLU", ldu), ("sP", p.now("sP"))] if k == 0 else ()),
                        inc="sA" if k == NDC - 1 else None)
        for k in range(NDC):
            sl = slice(k * TC, (k + 1) * TC)
            w_ = ([("sA", aW)] + pbcw) if k == 0 else ()
            mmW = p.emit("tensor", lambda e, sl=sl, k=k: e.matmul(
                b.pbc[0][0:1, :], b.ONEB[:, 0:1], b.XB[:, sl], start=(k == 0), stop=(k == NDC - 1)),
                waits=w_, inc="sP" if k == NDC - 1 else None)
        R = b.ROWS
        sv = p.emit("vector", lambda e: e.tensor_scalar_mul(R[:, 0:512], b.pbc[0][0:1, :], 1.0 / D),
                    waits=[("sP", mmW)], inc="sV")
        sq = p.emit("scalar", lambda e: e.activation(R[:, 512:1024], R[:, 0:512], AF.Sqrt,
                                                     bias=b.EPSC[0:1, 1:2]), waits=[("sV", sv)], inc="sA")
        rc = p.emit("vector", lambda e: e.reciprocal(R[:, 0:512], R[:, 512:1024]),
                    waits=[("sA", sq)], inc="sV")
        bcW = p.emit("tensor", lambda e: e.matmul(b.pbc[0][:], b.ONEF[0:1, 0:1].broadcast_to([1, 128]),
                                                  R[:, 0:512], start=True, stop=True),
                     waits=[("sV", rc)], inc="sP")
        for k in range(NDC):
            sl = slice(k * TC, (k + 1) * TC)
            zw = p.emit("vector", lambda e, sl=sl, k=k: e.scalar_tensor_tensor(
                b.XB[:, sl], b.STB[:, sl], C[:, 56 + k:57 + k], b.pbc[0][:], ALU.mult, ALU.mult),
                waits=[("sP", bcW)] if k == 0 else (), inc="sV")
        p.meta["pbc_free"] = [("sV", zw)]
        for m in range(32):
            bg, bu = b.pb[(m % 2) * 2], b.pb[(m % 2) * 2 + 1]
            for k in range(NDC):
                w_ = []
                if k == 0:
                    w_ = [("sV", zw)] + (wboth if (tc == 0 and m == 0) else [])
                    if bank_free.get((m % 2, "g")) is not None:
                        w_.append(("sA", bank_free[(m % 2, "g")]))
                p.emit("tensor", lambda e, m=m, k=k, bg=bg: e.matmul(
                    bg[:], b.BIGA[:, k * 4096 + m * 128:k * 4096 + (m + 1) * 128],
                    b.XB[:, k * TC:(k + 1) * TC], start=(k == 0), stop=(k == NDC - 1)),
                    waits=w_)
                w_ = []
                if k == 0 and bank_free.get((m % 2, "u")) is not None:
                    w_ = [("sV", bank_free[(m % 2, "u")])]
                last_mm = p.emit("tensor", lambda e, m=m, k=k, bu=bu: e.matmul(
                    bu[:], b.BIGB[:, k * 4096 + m * 128:k * 4096 + (m + 1) * 128],
                    b.XB[:, k * TC:(k + 1) * TC], start=(k == 0), stop=(k == NDC - 1)),
                    waits=w_, inc="sP" if k == NDC - 1 else None)
            w_ = [("sP", last_mm)]
            if p.meta.get("ab_ev2") is not None:
                w_.append(("sV", p.meta["ab_ev2"]))
            ev1 = p.emit("scalar", lambda e, m=m, bg=bg: e.activation(
                b.S1[:, 0:512], bg[:], AF.Silu, bias=C[:, 64 + m:65 + m]),
                waits=w_, inc="sA")
            bank_free[(m % 2, "g")] = ev1
            sl = slice((m % 8) * TC, (m % 8 + 1) * TC)
            w_ = [("sA", ev1)]
            if stb_free[m % 8] is not None:
                w_.append((f"sHo{m % 8}", stb_free[m % 8]))
            ev2 = p.emit("vector", lambda e, m=m, sl=sl, bu=bu: e.scalar_tensor_tensor(
                b.STB[:, sl], bu[:], C[:, 96 + m:97 + m], b.S1[:, 0:512], ALU.add, ALU.mult),
                waits=w_, inc="sV")
            p.meta["ab_ev2"] = ev2
            bank_free[(m % 2, "u")] = ev2
            st = p.dma("gpsimd", aps["hd"][m * 128:(m + 1) * 128, tc * TC:(tc + 1) * TC],
                       b.STB[:, sl], waits=[("sV", ev2)], inc=f"sHo{m % 8}")
            stb_free[m % 8] = st

    wd0 = [("sP", p.now("sP"))]
    for k in range(32):
        ldw = p.dma("gpsimd", b.BIGA[:, k * 1024:(k + 1) * 1024],
                    aps["w_down"][k * 128:(k + 1) * 128, :], waits=wd0 if k == 0 else (),
                    inc="sLWD")
    all_h = [(f"sHo{s}", p.now(f"sHo{s}")) for s in range(8)] + [("sLWD", ldw)]
    fm_free = [None] * 2
    stbD_free = [None] * 8
    uD_free = [None] * 2
    for tc in range(NTC):
        xg = load_xc(p, b, aps["xin"], tc,
                     gate=[("sV", p.now("sV")), ("sA", p.now("sA"))])
        if tc == 0:
            w_ = [("sP", p.now("sP"))]
        else:
            w_ = [("sV", p.meta["abD_evac"])]
        for k in range(NDC):
            ldu = p.dma("sync", b.XB[:, k * TC:(k + 1) * TC],
                        aps["ud"][k * 128:(k + 1) * 128, tc * TC:(tc + 1) * TC],
                        waits=w_ if k == 0 else (), inc="sLU2")
        for grp in range(2):
            kcW = None
            for kc in range(32):
                sl = slice((kc % 8) * TC, (kc % 8 + 1) * TC)
                w_ = list(all_h) if (tc == 0 and grp == 0 and kc < 8) else []
                if stbD_free[kc % 8] is not None:
                    w_.append(("sP", stbD_free[kc % 8]))
                ldh = p.dma("sync", b.STB[:, sl],
                            aps["hd"][kc * 128:(kc + 1) * 128, tc * TC:(tc + 1) * TC],
                            waits=w_, inc=f"sLH{kc % 8}")
                for j in range(4):
                    dm = grp * 4 + j
                    ww = []
                    if j == 0:
                        ww.append((f"sLH{kc % 8}", ldh))
                    if kc == 0 and j == 0:
                        if fm_free[0] is not None:
                            ww.append(("sV", p.meta.get("abD_evac", 0)))
                        else:
                            ww.append(("sA", p.now("sA")))
                            ww.append(("sV", p.now("sV")))
                    kcW = p.emit("tensor", lambda e, kc=kc, dm=dm, j=j, sl=sl: e.matmul(
                        b.pb[j][:], b.BIGA[:, kc * 1024 + dm * 128:kc * 1024 + (dm + 1) * 128],
                        b.STB[:, sl], start=(kc == 0), stop=(kc == 31)),
                        waits=ww, inc="sP" if j == 3 else None)
                stbD_free[kc % 8] = kcW
            for j in range(4):
                dm = grp * 4 + j
                fs = slice((j % 2) * TC, (j % 2) * TC + TC)
                w_ = [("sP", kcW), ("sLX", p.now("sLX")), ("sLU2", ldu)]
                if fm_free[j % 2] is not None:
                    w_.append((f"sXo{j % 2}", fm_free[j % 2]))
                p.emit("vector", lambda e, dm=dm, j=j, fs=fs: e.tensor_scalar(
                    b.FM[:, fs], b.pb[j][:], C[:, 128 + dm:129 + dm], None, ALU.add),
                    waits=w_)
                p.emit("vector", lambda e, dm=dm, fs=fs: e.tensor_add(
                    b.FM[:, fs], b.FM[:, fs], b.XB[:, dm * TC:(dm + 1) * TC]))
                ev = p.emit("vector", lambda e, dm=dm, fs=fs: e.tensor_add(
                    b.FM[:, fs], b.FM[:, fs], b.XC[:, dm * TC:(dm + 1) * TC]), inc="sV")
                p.meta["abD_evac"] = ev
                uD_free[tc % 2] = ev
                st = p.dma("gpsimd", aps["xout"][dm * 128:(dm + 1) * 128, tc * TC:(tc + 1) * TC],
                           b.FM[:, fs], waits=[("sV", ev)], inc=f"sXo{j % 2}")
                fm_free[j % 2] = st
    return [(f"sXo{j}", p.now(f"sXo{j}")) for j in range(2)]


def emit_attn(p: Prog, b: Bufs, aps, rope: bool, gate=()):
    NTC = p.NTC
    T = p.T
    NSC = T // 128
    V65 = 1040

    if rope:
        p.dma("sync", b.COLS[:, 0:8], aps["bqc"], waits=list(gate), inc="sLB")
        p.dma("sync", b.COLS[:, 8:16], aps["bkc"], inc="sLB")
        p.dma("sync", b.COLS[:, 16:24], aps["boc"], inc="sLB")
        p.dma("sync", b.S2[:, 0:1024], aps["bvb"], inc="sLB")
        p.dma("gpsimd", b.WS[:, 0:128], aps["pswap"], waits=list(gate), inc="sLB2")
        bg = [("sLB", p.now("sLB")), ("sLB2", p.now("sLB2"))]
    else:
        bg = list(gate)

    def load_w(ap, wait):
        for kc in range(NDC):
            lw = p.dma("gpsimd", b.WQK[:, kc * 1024:(kc + 1) * 1024],
                       ap[kc * 128:(kc + 1) * 128, :], waits=wait if kc == 0 else (), inc="sLWS")
        return [("sLWS", lw)]

    w1r = load_w(aps["wqT"], list(gate))

    vt3 = b.BIGB[:, 0:NSC * V65].rearrange("p (s h c) -> p s h c", h=16, c=65)
    mo = p.emit("vector", lambda e: e.memset(vt3[:, :, :, 64:65], 1.0),
                waits=[("sV", p.now("sV")), ("sA", p.now("sA")), ("sP", p.now("sP"))] + list(gate),
                inc="sV")

    def rope_tables(tc):
        w_ = [("sV", p.meta["attn_ct_used"])] if p.meta.get("attn_ct_used") else []
        p.dma("gpsimd", b.CT, aps["cosT"][:, tc * TC:(tc + 1) * TC], waits=w_, inc="sLR")
        lst = p.dma("gpsimd", b.ST, aps["sinT"][:, tc * TC:(tc + 1) * TC], inc="sLR")
        return [("sLR", lst)]

    def proj_pass(which, tc, wrdy):
        for hp in range(NDC):
            bank = b.pb[hp % 2]
            for kc in range(NDC):
                w_ = []
                if kc == 0:
                    w_ = list(p.meta["attn_xbrdy"])
                    if tc == 0 and hp < 2:
                        w_ += wrdy
                    bkey = f"attn_bank{hp % 2}"
                    if p.meta.get(bkey) is not None:
                        w_.append(p.meta[bkey])
                mm = p.emit("tensor", lambda e, hp=hp, kc=kc, bank=bank: e.matmul(
                    bank[:], b.WQK[:, kc * 1024 + hp * 128:kc * 1024 + (hp + 1) * 128],
                    b.XB[:, kc * TC:(kc + 1) * TC], start=(kc == 0), stop=(kc == NDC - 1)),
                    waits=w_, inc="sP" if kc == NDC - 1 else None)
            if rope:
                rg = p.meta["attn_rg"]
                bcol = b.COLS[:, (0 if which == "q" else 8) + hp:(0 if which == "q" else 8) + hp + 1]
                sl = slice((hp % 4) * TC, (hp % 4 + 1) * TC)
                w_ = [("sP", mm)]
                if p.meta.get(f"attn_stb{hp % 4}") is not None:
                    w_.append(("sP", p.meta[f"attn_stb{hp % 4}"]))
                ev = p.emit("scalar", lambda e, sl=sl, bank=bank, bcol=bcol: e.activation(
                    b.STB[:, sl], bank[:], AF.Identity, bias=bcol),
                    waits=w_, inc="sA")
                p.meta[f"attn_bank{hp % 2}"] = ("sA", ev)
                sw = p.emit("tensor", lambda e, sl=sl: e.matmul(
                    b.pb[4][:], b.WS[:, 0:128], b.STB[:, sl], start=True, stop=True),
                    waits=[("sA", ev), ("sV", p.meta.get("attn_sw_used", 0))] + rg,
                    inc="sP")
                p.meta[f"attn_stb{hp % 4}"] = sw
                p.emit("vector", lambda e, sl=sl: e.tensor_mul(b.FM[:, 0:512], b.STB[:, sl], b.CT),
                       waits=[("sP", sw)])
                sm = p.emit("vector", lambda e: e.tensor_mul(b.S1[:, 0:512], b.pb[4][:], b.ST),
                            inc="sV")
                p.meta["attn_sw_used"] = sm
                if which == "q":
                    qs = slice((hp % 4) * TC + 2048, (hp % 4 + 1) * TC + 2048)
                    w_ = []
                    if p.meta.get(f"attn_qst{hp % 4}") is not None:
                        w_.append((f"sQo{hp % 4}", p.meta[f"attn_qst{hp % 4}"]))
                    rv = p.emit("vector", lambda e, qs=qs: e.tensor_add(
                        b.STB[:, qs], b.FM[:, 0:512], b.S1[:, 0:512]), waits=w_, inc="sV")
                    st = p.dma("gpsimd", aps["qd"][hp * 128:(hp + 1) * 128, tc * TC:(tc + 1) * TC],
                               b.STB[:, qs], waits=[("sV", rv)], inc=f"sQo{hp % 4}")
                    p.meta[f"attn_qst{hp % 4}"] = st
                else:
                    rv = p.emit("vector", lambda e, hp=hp, tc=tc: e.tensor_add(
                        b.BIGA[:, hp * T + tc * TC:hp * T + (tc + 1) * TC],
                        b.FM[:, 0:512], b.S1[:, 0:512]), inc="sV")
                p.meta["attn_ct_used"] = rv
            else:
                if which == "q":
                    sl = slice((hp % 4) * TC, (hp % 4 + 1) * TC)
                    w_ = [("sP", mm)]
                    if p.meta.get(f"attn_qst{hp % 4}") is not None:
                        w_.append((f"sQo{hp % 4}", p.meta[f"attn_qst{hp % 4}"]))
                    ev = p.emit("scalar", lambda e, sl=sl, bank=bank: e.activation(
                        b.STB[:, sl], bank[:], AF.Identity), waits=w_, inc="sA")
                    p.meta[f"attn_bank{hp % 2}"] = ("sA", ev)
                    st = p.dma("gpsimd", aps["qd"][hp * 128:(hp + 1) * 128, tc * TC:(tc + 1) * TC],
                               b.STB[:, sl], waits=[("sA", ev)], inc=f"sQo{hp % 4}")
                    p.meta[f"attn_qst{hp % 4}"] = st
                else:
                    ev = p.emit("scalar", lambda e, hp=hp, tc=tc, bank=bank: e.activation(
                        b.BIGA[:, hp * T + tc * TC:hp * T + (tc + 1) * TC], bank[:], AF.Identity),
                        waits=[("sP", mm)], inc="sA")
                    p.meta[f"attn_bank{hp % 2}"] = ("sA", ev)

    for tc in range(NTC):
        xg = load_xc(p, b, aps["xin"], tc,
                     gate=([("sV", p.now("sV")), ("sA", p.now("sA"))] +
                           (p.meta.get("pbc_free", []) + bg if tc == 0 else [])))
        lng = xg + ([("sP", p.now("sP"))] if tc > 0 else [])
        lng += [(f"sQo{j}", p.now(f"sQo{j}")) for j in range(4)]
        lng += [("sXh", p.now("sXh"))]
        xw = emit_ln(p, b, b.EPSC[0:1, 0:1], gate=lng)
        for k in range(NDC):
            p.dma("gpsimd", aps["xhd"][k * 128:(k + 1) * 128, tc * TC:(tc + 1) * TC],
                  b.XB[:, k * TC:(k + 1) * TC], waits=list(xw) if k == 0 else (), inc="sXh")
        if rope:
            p.meta["attn_rg"] = rope_tables(tc)
        p.meta["attn_xbrdy"] = xw
        proj_pass("q", tc, w1r)
    allxh = [("sXh", p.now("sXh"))]
    wkr = load_w(aps["wkT"], [("sP", p.now("sP"))])
    p.meta["attn_ct_used"] = None
    for tc in range(NTC):
        w_ = ([("sP", p.now("sP")), ("sA", p.now("sA")), ("sV", p.now("sV"))] + allxh) if tc == 0 \
            else [("sP", p.now("sP"))]
        for k in range(NDC):
            ldxh = p.dma("sync", b.XB[:, k * TC:(k + 1) * TC],
                         aps["xhd"][k * 128:(k + 1) * 128, tc * TC:(tc + 1) * TC],
                         waits=w_ if k == 0 else (), inc="sXh2")
        if rope:
            p.meta["attn_rg"] = rope_tables(tc)
        p.meta["attn_xbrdy"] = [("sXh2", ldxh)]
        proj_pass("k", tc, wkr)

    wvr = load_w(aps["wvT"], [("sP", p.now("sP"))])
    for tc in range(NTC):
        w_ = [("sP", p.now("sP"))] if tc > 0 else [("sP", p.now("sP")), ("sA", p.now("sA"))]
        for k in range(NDC):
            ldxh = p.dma("sync", b.XB[:, k * TC:(k + 1) * TC],
                         aps["xhd"][k * 128:(k + 1) * 128, tc * TC:(tc + 1) * TC],
                         waits=w_ if k == 0 else (), inc="sXh2")
        xh2 = [("sXh2", ldxh)]
        for tsq in range(4):
            sc = tc * 4 + tsq
            for dn in range(2):
                bank = b.pb[2 + dn]
                for kc in range(NDC):
                    w_ = []
                    if kc == 0:
                        w_ = [("sV", mo)] + xh2
                        if tc == 0 and tsq == 0:
                            w_ += wvr
                        bkey = f"attn_vbank{dn}"
                        if p.meta.get(bkey) is not None:
                            w_.append(("sV", p.meta[bkey]))
                    mm = p.emit("tensor", lambda e, tsq=tsq, dn=dn, kc=kc, bank=bank: e.matmul(
                        bank[:], b.XB[:, kc * TC + tsq * 128:kc * TC + tsq * 128 + 128],
                        b.WQK[:, kc * 1024 + dn * 512:kc * 1024 + (dn + 1) * 512],
                        start=(kc == 0), stop=(kc == NDC - 1)),
                        waits=w_, inc="sP" if kc == NDC - 1 else None)
                out3 = b.BIGB[:, sc * V65 + dn * 8 * 65:sc * V65 + (dn * 8 + 8) * 65].rearrange(
                    "p (h c) -> p h c", c=65)[:, :, 0:64]
                in3 = bank[:].rearrange("p (h c) -> p h c", c=64)
                if rope:
                    vv = p.emit("vector", lambda e, out3=out3, in3=in3, dn=dn: e.tensor_add(
                        out3, in3, b.S2[:, dn * 512:(dn + 1) * 512].rearrange("p (h c) -> p h c", c=64)),
                        waits=[("sP", mm)], inc="sV")
                else:
                    vv = p.emit("vector", lambda e, out3=out3, in3=in3: e.tensor_copy(out3, in3),
                                waits=[("sP", mm)], inc="sV")
                p.meta[f"attn_vbank{dn}"] = vv
    wor = load_w(aps["woT"], [("sP", p.now("sP"))])
    kvready = [("sV", p.now("sV")), ("sA", p.now("sA")), ("sP", p.now("sP"))] + wor
    allq = [(f"sQo{j}", p.now(f"sQo{j}")) for j in range(4)]

    scale = 0.125
    stb_free = [None] * 8
    sti = 0
    for tq in range(NTC):
        w_ = (kvready + allq) if tq == 0 else [("sP", p.meta["attn_lastwo"])]
        for hp in range(NDC):
            ldq = p.dma("sync", b.WS[:, hp * TC:(hp + 1) * TC],
                        aps["qd"][hp * 128:(hp + 1) * 128, tq * TC:(tq + 1) * TC],
                        waits=w_ if hp == 0 else (), inc="sLQ")
        xg = load_xc(p, b, aps["xin"], tq, gate=[("sV", p.now("sV")), ("sA", p.now("sA"))])
        qrdy = [("sLQ", ldq)]
        for hp in range(8):
            nsc = tq * 4 + 4
            for sc in range(nsc):
                delta = max(0, sc * 128 - tq * TC)
                sbank = [b.pb[0], b.pb[1]] if (sc % 2 == 0) else [b.pbc[0], b.pbc[1]]
                for h2 in range(2):
                    lhs = b.BIGA[h2 * 64:h2 * 64 + 64, hp * T + sc * 128:hp * T + (sc + 1) * 128]
                    rhs = b.WS[h2 * 64:h2 * 64 + 64, hp * TC + delta:(hp + 1) * TC]
                    w_ = [("sA", p.meta.get(f"attn_sexp{sc % 2}_{h2}", 0))]
                    if sc < 2:
                        w_ += qrdy
                        if tq == 0 and hp == 0:
                            w_ += kvready
                    smm = p.emit("tensor", lambda e, lhs=lhs, rhs=rhs, delta=delta, bank=sbank[h2]: e.matmul(
                        bank[:, delta:512], lhs, rhs, start=True, stop=True),
                        waits=w_, inc="sP")
                    ssl = slice((sti % 8) * TC + delta, (sti % 8) * TC + TC)
                    w_ = [("sP", smm)]
                    if stb_free[sti % 8] is not None:
                        w_.append(("sP", stb_free[sti % 8]))
                    elif sti < 8:
                        w_ += allq
                    ex = p.emit("scalar", lambda e, ssl=ssl, delta=delta, bank=sbank[h2]: e.activation(
                        b.STB[:, ssl], bank[:, delta:512], AF.Exp, scale=scale),
                        waits=w_, inc="sA")
                    p.meta[f"attn_sexp{sc % 2}_{h2}"] = ex
                    pvw = [("sA", ex)]
                    if delta > 0 or sc * 128 == tq * TC:
                        dsl = slice((sti % 8) * TC + delta, (sti % 8) * TC + delta + 128)
                        ex2 = p.emit("vector", lambda e, dsl=dsl: e.tensor_mul(
                            b.STB[:, dsl], b.STB[:, dsl], b.TRIL[:]),
                            waits=[("sA", ex)], inc="sV")
                        pvw = [("sV", ex2)]
                    if delta > 0:
                        zsl = slice((sti % 8) * TC, (sti % 8) * TC + delta)
                        mz = p.emit("vector", lambda e, zsl=zsl: e.memset(b.STB[:, zsl], 0.0),
                                    waits=list(w_), inc="sV")
                        pvw.append(("sV", mz))
                    hglob = hp * 2 + h2
                    if sc == 0:
                        pvw = pvw + [("sV", p.meta.get(f"attn_opsum{h2}", 0))]
                    pvm = p.emit("tensor", lambda e, sc=sc, hglob=hglob, sti=sti, h2=h2, nsc=nsc: e.matmul(
                        b.pb[2 + h2][0:65, 0:512],
                        b.BIGB[:, sc * V65 + hglob * 65:sc * V65 + (hglob + 1) * 65],
                        b.STB[:, (sti % 8) * TC:(sti % 8) * TC + TC],
                        start=(sc == 0), stop=(sc == nsc - 1)),
                        waits=pvw, inc="sP")
                    stb_free[sti % 8] = pvm
                    sti += 1
            for h2 in range(2):
                R = b.ROWS
                w_ = [("sP", p.now("sP"))]
                if rope:
                    p.emit("vector", lambda e, h2=h2: e.tensor_scalar_add(
                        R[:, 0:512], b.pb[2 + h2][64:65, :], b.EPSC[0:1, 3:4]), waits=w_)
                else:
                    p.emit("vector", lambda e, h2=h2: e.tensor_copy(R[:, 0:512], b.pb[2 + h2][64:65, :]),
                           waits=w_)
                rc = p.emit("vector", lambda e: e.reciprocal(R[:, 512:1024], R[:, 0:512]), inc="sV")
                bc = p.emit("tensor", lambda e: e.matmul(
                    b.pb[4][:], b.ONEF[0:1, 0:1].broadcast_to([1, 128]), R[:, 512:1024],
                    start=True, stop=True), waits=[("sV", rc)], inc="sP")
                cp = p.emit("vector", lambda e: e.tensor_copy(b.S1[:, 0:512], b.pb[4][:, 0:512]),
                            waits=[("sP", bc)], inc="sV")
                ev = p.emit("vector", lambda e, hp=hp, h2=h2: e.tensor_mul(
                    b.XB[h2 * 64:h2 * 64 + 64, hp * TC:(hp + 1) * TC],
                    b.pb[2 + h2][0:64, :], b.S1[0:64, 0:512]), inc="sV")
                p.meta["attn_ctx"] = ev
                p.meta[f"attn_opsum{h2}"] = ev
        for dm in range(NDC):
            for kc in range(NDC):
                w_ = []
                if kc == 0:
                    w_ = [("sV", p.meta["attn_ctx"])]
                    if p.meta.get("attn_wob") is not None:
                        w_.append(("sV", p.meta["attn_wob"]))
                wm = p.emit("tensor", lambda e, dm=dm, kc=kc: e.matmul(
                    b.pb[4][:], b.WQK[:, kc * 1024 + dm * 128:kc * 1024 + (dm + 1) * 128],
                    b.XB[:, kc * TC:(kc + 1) * TC], start=(kc == 0), stop=(kc == NDC - 1)),
                    waits=w_, inc="sP" if kc == NDC - 1 else None)
            fs = slice((dm % 2) * TC, (dm % 2) * TC + TC)
            w_ = [("sP", wm)] + (xg if dm == 0 else [])
            if p.meta.get(f"attn_xo{dm % 2}") is not None:
                w_.append((f"sXo{dm % 2}", p.meta[f"attn_xo{dm % 2}"]))
            if rope:
                p.emit("vector", lambda e, dm=dm, fs=fs: e.tensor_scalar_add(
                    b.FM[:, fs], b.pb[4][:], b.COLS[:, 16 + dm:17 + dm]), waits=w_)
                ev = p.emit("vector", lambda e, dm=dm, fs=fs: e.tensor_add(
                    b.FM[:, fs], b.FM[:, fs], b.XC[:, dm * TC:(dm + 1) * TC]), inc="sV")
            else:
                ev = p.emit("vector", lambda e, dm=dm, fs=fs: e.tensor_add(
                    b.FM[:, fs], b.pb[4][:], b.XC[:, dm * TC:(dm + 1) * TC]), waits=w_, inc="sV")
            p.meta["attn_wob"] = ev
            st = p.dma("gpsimd", aps["xout"][dm * 128:(dm + 1) * 128, tq * TC:(tq + 1) * TC],
                       b.FM[:, fs], waits=[("sV", ev)], inc=f"sXo{dm % 2}")
            p.meta[f"attn_xo{dm % 2}"] = st
        p.meta["attn_lastwo"] = p.now("sP")
    return [(f"sXo{j}", p.now(f"sXo{j}")) for j in range(2)]


def emit_gla(p: Prog, b: Bufs, aps, gate=()):
    NTC = p.NTC
    T = p.T
    C = b.COLS
    WQ, WK, WRLO, WR = 0, 8192, 16384, 24576
    QH, KH, GNW, XLO = 12288, 16384, 20480, 21504
    S0B = 25600

    g0 = list(gate)
    for i, nm in enumerate(("Wq", "Wk")):
        for kc in range(NDC):
            p.dma("gpsimd", b.BIGA[:, i * 8192 + kc * 1024:i * 8192 + (kc + 1) * 1024],
                  aps[nm][kc * 128:(kc + 1) * 128, :], waits=g0 if (i == 0 and kc == 0) else (),
                  inc="sGW")
    for kc in range(NDC):
        w_ = list(g0) if kc == 0 else []
        if kc >= 4:
            w_.append(("sV", p.meta.get("gla_wrs", 0)))
        lwr = p.dma("sync", b.XC[:, (kc % 4) * 1024:(kc % 4 + 1) * 1024],
                    aps["Wr"][kc * 128:(kc + 1) * 128, :], waits=w_, inc=f"sWRS{kc % 4}")
        hi = p.emit("scalar", lambda e, kc=kc: e.activation(
            b.BIGA[:, WR + kc * 1024:WR + (kc + 1) * 1024],
            b.XC[:, (kc % 4) * 1024:(kc % 4 + 1) * 1024], AF.Identity),
            waits=[(f"sWRS{kc % 4}", lwr)], inc="sA")
        lo = p.emit("vector", lambda e, kc=kc: e.tensor_sub(
            b.BIGA[:, WRLO + kc * 1024:WRLO + (kc + 1) * 1024],
            b.XC[:, (kc % 4) * 1024:(kc % 4 + 1) * 1024],
            b.BIGA[:, WR + kc * 1024:WR + (kc + 1) * 1024]),
            waits=[("sA", hi)], inc="sV")
        p.meta["gla_wrs"] = lo
    p.dma("gpsimd", b.BIGB[:, GNW:GNW + 1024], aps["gnw_b"], waits=g0, inc="sGW")
    lw = p.dma("gpsimd", b.WS[:, 0:64].rearrange("p (k c) -> p k c", k=8),
               aps["Wa"].rearrange("(k p) c -> p k c", p=128), inc="sGW")
    lc = p.dma("sync", C[:, 0:1], aps["qnw_c"], waits=g0, inc="sGC")
    lc = p.dma("sync", C[:, 1:2], aps["knw_c"], inc="sGC")
    lc = p.dma("sync", b.ROWS[0:1, 0:8], aps["alog_r"], inc="sGC")
    lc = p.dma("sync", b.ROWS[0:1, 8:16], aps["dtb_r"], inc="sGC")
    wrdy = [("sGW", p.now("sGW")), ("sGC", lc)]
    bc1 = p.emit("tensor", lambda e: e.matmul(b.pb[4][:, 0:8],
                 b.ONEF[0:1, 0:1].broadcast_to([1, 128]), b.ROWS[0:1, 0:8],
                 start=True, stop=True), waits=wrdy, inc="sP")
    e1 = p.emit("scalar", lambda e: e.activation(C[:, 16:24], b.pb[4][:, 0:8], AF.Exp),
                waits=[("sP", bc1)], inc="sA")
    p.emit("vector", lambda e: e.tensor_scalar_mul(C[:, 16:24], C[:, 16:24], -1.0),
           waits=[("sA", e1)])
    bc2 = p.emit("tensor", lambda e: e.matmul(b.pb[4][:, 8:16],
                 b.ONEF[0:1, 0:1].broadcast_to([1, 128]), b.ROWS[0:1, 8:16],
                 start=True, stop=True), waits=[("sA", e1)], inc="sP")
    p.emit("vector", lambda e: e.tensor_copy(C[:, 24:32], b.pb[4][:, 8:16]),
           waits=[("sP", bc2)])
    p.emit("vector", lambda e: e.memset(b.S2[:, 0:1024], 0.0))
    p.emit("vector", lambda e: e.memset(b.BIGB[:, S0B:S0B + 1024], 0.0), inc="sV")
    prdy = [("sV", p.now("sV"))]

    sti = [0]

    def slot():
        s = 1536 + (sti[0] % 20) * 128
        sti[0] += 1
        return slice(s, s + 128)

    for tc in range(NTC):
        xg = load_xc(p, b, aps["xin"], tc,
                     gate=([("sV", p.now("sV")), ("sA", p.now("sA"))] +
                           (p.meta.get("pbc_free", []) + prdy if tc == 0 else [])))
        xw = emit_ln(p, b, b.EPSC[0:1, 0:1],
                     gate=xg + ([("sP", p.now("sP"))] if tc > 0 else []),
                     lo_dst=[b.BIGB[:, XLO + k * TC:XLO + (k + 1) * TC] for k in range(NDC)])
        for kc in range(NDC):
            w_ = ([("sP", p.now("sP"))] + (wrdy if tc == 0 else [])) if kc == 0 else []
            lwv = p.dma("gpsimd", b.WQK[:, kc * 1024:(kc + 1) * 1024],
                        aps["Wv"][kc * 128:(kc + 1) * 128, :], waits=w_, inc="sWV")
        wvrdy = [("sWV", lwv)]
        for which, wbase, dsth, ncol in (("q", WQ, QH, 0), ("k", WK, KH, 1)):
            for h in range(8):
                bank = b.pb[h % 2]
                for kc in range(NDC):
                    w_ = []
                    if kc == 0:
                        w_ = list(xw)
                        if which == "q" and h == 0:
                            w_ += [(f"sXo{j}", p.now(f"sXo{j}")) for j in range(2)]
                        if p.meta.get(f"gla_pj{h % 2}") is not None:
                            w_.append(p.meta[f"gla_pj{h % 2}"])
                        if p.meta.get("gla_fmfree") is not None:
                            w_.append(("sV", p.meta["gla_fmfree"]))
                    mm = p.emit("tensor", lambda e, wbase=wbase, h=h, kc=kc, bank=bank: e.matmul(
                        bank[:, 0:512],
                        b.BIGA[:, wbase + kc * 1024 + h * 128:wbase + kc * 1024 + (h + 1) * 128],
                        b.XB[:, kc * TC:(kc + 1) * TC], start=(kc == 0), stop=(kc == NDC - 1)),
                        waits=w_, inc="sP" if kc == NDC - 1 else None)
                sil = p.emit("scalar", lambda e, bank=bank: e.activation(
                    b.FM[:, 0:512], bank[:, 0:512], AF.Silu), waits=[("sP", mm)], inc="sA")
                p.meta[f"gla_pj{h % 2}"] = ("sA", sil)
                sq = p.emit("scalar", lambda e: e.activation(
                    b.STB[:, 1024:1536], b.FM[:, 0:512], AF.Square), inc="sA")
                w_ = [("sA", sq)]
                if p.meta.get("gla_pb4row") is not None:
                    w_.append(("sV", p.meta["gla_pb4row"]))
                smm = p.emit("tensor", lambda e: e.matmul(
                    b.pb[4][0:1, 0:512], b.ONEB[:, 0:1], b.STB[:, 1024:1536],
                    start=True, stop=True), waits=w_, inc="sP")
                sv1 = p.emit("vector", lambda e: e.tensor_scalar(
                    b.ROWS[0:1, 512:1024], b.pb[4][0:1, 0:512], 1.0 / 128.0, None, ALU.mult),
                    waits=[("sP", smm)], inc="sV")
                p.meta["gla_pb4row"] = sv1
                sq2 = p.emit("scalar", lambda e: e.activation(
                    b.ROWS[0:1, 512:1024], b.ROWS[0:1, 512:1024], AF.Sqrt, bias=b.EPSC[0:1, 0:1]),
                    waits=[("sV", sv1)], inc="sA")
                rc = p.emit("vector", lambda e: e.reciprocal(b.ROWS[0:1, 512:1024], b.ROWS[0:1, 512:1024]),
                            waits=[("sA", sq2)], inc="sV")
                bcm = p.emit("tensor", lambda e: e.matmul(
                    b.pbc[0][:], b.ONEF[0:1, 0:1].broadcast_to([1, 128]), b.ROWS[0:1, 512:1024],
                    start=True, stop=True), waits=[("sV", rc)] + p.meta.get("pbc_free", []), inc="sP")
                dstap = b.BIGB[:, dsth + h * 128:dsth + h * 128 + 3 * 1024 + 128].rearrange(
                    "p (q c) -> p q c", q=4, c=896)[:, :, 0:128] if False else None
                ev = p.emit("vector", lambda e, dsth=dsth, h=h, ncol=ncol: e.scalar_tensor_tensor(
                    b.BIGB[:, dsth:dsth + 4096].rearrange("p (q c) -> p q c", q=4)[:, :, h * 128:(h + 1) * 128],
                    b.FM[:, 0:512].rearrange("p (q c) -> p q c", q=4),
                    C[:, ncol:ncol + 1], b.pbc[0][:].rearrange("p (q c) -> p q c", q=4),
                    ALU.mult, ALU.mult),
                    waits=[("sP", bcm)], inc="sV")
                p.meta["pbc_free"] = [("sV", ev)]
                p.meta["gla_fmfree"] = ev
        for tsq in range(4):
            for dn in range(2):
                bank = b.pb[2 + dn]
                for kc in range(NDC):
                    w_ = []
                    if kc == 0:
                        w_ = list(xw)
                        if p.meta.get(f"gla_vb{dn}") is not None:
                            w_.append(("sA", p.meta[f"gla_vb{dn}"]))
                    if kc == 0:
                        w_ = w_ + wvrdy
                    mm = p.emit("tensor", lambda e, tsq=tsq, dn=dn, kc=kc, bank=bank: e.matmul(
                        bank[:], b.XB[:, kc * TC + tsq * 128:kc * TC + tsq * 128 + 128],
                        b.WQK[:, kc * 1024 + dn * 512:kc * 1024 + (dn + 1) * 512],
                        start=(kc == 0), stop=(kc == NDC - 1)),
                        waits=w_, inc="sP" if kc == NDC - 1 else None)
                vv = p.emit("scalar", lambda e, tsq=tsq, dn=dn, bank=bank: e.activation(
                    b.BIGB[:, tsq * 1024 + dn * 512:tsq * 1024 + (dn + 1) * 512], bank[:], AF.Silu),
                    waits=[("sP", mm)], inc="sA")
                p.meta[f"gla_vb{dn}"] = vv
        vrdy = p.now("sA")
        for kc in range(NDC):
            w_ = [("sP", p.now("sP"))] if kc == 0 else []
            lwo = p.dma("gpsimd", b.WQK[:, kc * 1024:(kc + 1) * 1024],
                        aps["Wo"][kc * 128:(kc + 1) * 128, :], waits=w_, inc="sWO")
        wordy = [("sWO", lwo)]

        for tsq in range(4):
            if GLA_TRUNC and (tc > 0 or tsq > 0):
                break
            sgd = {}
            for dn in range(2):
                bank = b.pb[2]
                for kc in range(NDC):
                    w_ = []
                    if kc == 0:
                        w_ = [("sA", vrdy if dn == 0 else sgd[0]), ("sV", p.meta.get("gla_t7b", 0))]
                        if tsq == 0 and dn == 0:
                            w_.append(("sV", p.now("sV")))
                    for term in range(3):
                        lhs_base = WR if term != 1 else WRLO
                        xsrc = (b.XB[:, kc * TC + tsq * 128:kc * TC + tsq * 128 + 128]
                                if term != 2 else
                                b.BIGB[:, XLO + kc * TC + tsq * 128:XLO + kc * TC + tsq * 128 + 128])
                        mm = p.emit("tensor", lambda e, lhs_base=lhs_base, xsrc=xsrc, dn=dn, kc=kc,
                                    term=term, bank=bank: e.matmul(
                            xsrc, b.BIGA[:, lhs_base + kc * 1024 + dn * 512:lhs_base + kc * 1024 + (dn + 1) * 512],
                            bank[:], start=False, stop=False)
                            if False else e.matmul(
                            bank[:], xsrc,
                            b.BIGA[:, lhs_base + kc * 1024 + dn * 512:lhs_base + kc * 1024 + (dn + 1) * 512],
                            start=(kc == 0 and term == 0), stop=(kc == NDC - 1 and term == 2)),
                            waits=w_ if term == 0 else (), inc="sP" if (kc == NDC - 1 and term == 2) else None)
                        w_ = []
                sg = p.emit("scalar", lambda e, dn=dn, bank=bank: e.activation(
                    b.FM[:, dn * 512:(dn + 1) * 512], bank[:], AF.Sigmoid),
                    waits=[("sP", mm)], inc="sA")
                sgd[dn] = sg
            sgr = sg
            for kc in range(NDC):
                w_ = [("sV", p.now("sV"))] if kc == 0 else []
                zm = p.emit("tensor", lambda e, tsq=tsq, kc=kc: e.matmul(
                    b.pb[4][:, 0:8], b.XB[:, kc * TC + tsq * 128:kc * TC + tsq * 128 + 128],
                    b.WS[:, kc * 8:(kc + 1) * 8], start=(kc == 0), stop=(kc == NDC - 1)),
                    waits=w_, inc="sP" if kc == NDC - 1 else None)
            S = b.S1
            p.emit("vector", lambda e: e.tensor_add(S[:, 800:808], b.pb[4][:, 0:8], C[:, 24:32]),
                   waits=[("sP", zm)])
            vz = p.emit("vector", lambda e: e.tensor_scalar_max(S[:, 808:816], S[:, 800:808], 0.0),
                        inc="sV")
            az = p.emit("scalar", lambda e: e.activation(S[:, 816:824], S[:, 800:808], AF.Abs),
                        waits=[("sV", vz)], inc="sA")
            ez = p.emit("scalar", lambda e: e.activation(S[:, 824:832], S[:, 816:824], AF.Exp,
                                                         scale=-1.0), inc="sA")
            lz = p.emit("scalar", lambda e: e.activation(S[:, 816:824], S[:, 824:832], AF.Ln,
                                                         bias=b.ONEF[:, 0:1]), inc="sA")
            p.emit("vector", lambda e: e.tensor_add(S[:, 808:816], S[:, 808:816], S[:, 816:824]),
                   waits=[("sA", lz)])
            fh = p.emit("vector", lambda e: e.tensor_mul(S[:, 788:796], S[:, 808:816], C[:, 16:24]),
                        inc="sV")
            S = b.S1
            gmeta = p.meta.setdefault("gla_g", {})
            gst = {0: {}, 1: {}}
            PSLB = {0: 0, 1: 4}
            OPB = {0: b.pb[0], 1: b.pb[3]}
            SPB = {0: b.pb[1], 1: b.pbc[0]}
            APB = {0: b.pbc[1], 1: b.pb[2]}

            def slot_t(hg):
                return (b.STB, 1024) if hg == 0 else (b.BIGB, 26624)

            def SLh(hg, h, j):
                st, sb = slot_t(hg)
                o = sb + j * 512 + (h % 4) * 128
                return st[:, o:o + 128]

            def sv3g(hg, j):
                st, sb = slot_t(hg)
                return st[:, sb + j * 512:sb + j * 512 + 512].rearrange("p (h c) -> p h c", h=4)

            def qk3g(hg, base, tsq=tsq):
                s0 = base + tsq * 1024 + hg * 512
                return b.BIGB[:, s0:s0 + 512].rearrange("p (h c) -> p h c", h=4)

            def EXB3(hg):
                return S[:, 840 + hg * 4:844 + hg * 4].rearrange("p (h c) -> p h c", h=4)

            def FM43g(hg):
                return b.FM[:, hg * 512:hg * 512 + 512].rearrange("p (h c) -> p h c", h=4)

            def PTR4(hg):
                return b.PTR[:, PSLB[hg] * 128:PSLB[hg] * 128 + 512].rearrange("p (h c) -> p h c", h=4)

            def T1(hg, sgr=None, fh=fh):
                sgr = sgd[hg]
                g = gst[hg]
                first = True
                for h in range(hg * 4, hg * 4 + 4):
                    cur = b.FM[:, h * 128:(h + 1) * 128]
                    dsts = [S[:, 512:640], S[:, 640:768], S[:, 512:640],
                            S[:, (h % 4) * 128:(h % 4) * 128 + 128]]
                    for r in range(4):
                        w_ = []
                        if first:
                            w_ = [("sA", sgr), ("sV", fh)]
                            first = False
                        p.emit("vector", lambda e, cur=cur: e.max(out=S[:, 768:776], in_=cur), waits=w_)
                        p.emit("vector", lambda e, cur=cur, dst=dsts[r]: e.match_replace(
                            out=dst, in_to_replace=S[:, 768:776], in_values=cur, imm_value=0.0))
                        cur = dsts[r]
                sc3 = b.FM[:, hg * 512:hg * 512 + 512].rearrange("p (h c) -> p h c", h=4)
                rem3 = S[:, 0:512].rearrange("p (h c) -> p h c", h=4)
                dw = []
                if aps.get("dbg") is not None and tc == 0 and tsq == 0 and hg == 0:
                    p.dma("gpsimd", aps["dbg2"][:, 260:388], S[:, 0:128], waits=[("sV", p.now("sV"))], inc="sDB")
                    p.dma("gpsimd", aps["dbg2"][:, 388:516], b.FM[:, 0:128], waits=[("sV", p.now("sV"))], inc="sDB")
                    dw = [("sDB", p.now("sDB"))]
                p.emit("vector", lambda e, sc3=sc3, rem3=rem3: e.tensor_sub(rem3, sc3, rem3), waits=dw)
                p.emit("vector", lambda e, rem3=rem3: e.reduce_sum(
                    S[:, 776:780].rearrange("p (h c) -> p h c", h=4), rem3, AX.X))
                p.emit("vector", lambda e: e.tensor_scalar_add(S[:, 776:780], S[:, 776:780], b.EPSC[:, 2:3]))
                p.emit("vector", lambda e: e.reciprocal(S[:, 780:784], S[:, 776:780]))
                p.emit("vector", lambda e, hg=hg: e.tensor_mul(S[:, 784:788], S[:, 780:784],
                                                               S[:, 788 + hg * 4:792 + hg * 4]))
                g["tv"] = p.emit("vector", lambda e, dst=sv3g(hg, 0), rem3=rem3: e.tensor_mul(
                    dst, rem3,
                    S[:, 784:788].rearrange("p (h c) -> p h c", h=4).broadcast_to([128, 4, 128])),
                    inc="sV")
                if aps.get("dbg") is not None and tc == 0 and tsq == 0 and hg == 0:
                    p.dma("gpsimd", aps["dbg2"][:, 0:128], S[:, 0:128], waits=[("sV", g["tv"])], inc="sDB")
                    p.dma("gpsimd", aps["dbg2"][:, 128:256], b.STB[:, 1024:1152], waits=[("sV", g["tv"])], inc="sDB")
                    p.dma("gpsimd", aps["dbg2"][:, 256:260], S[:, 784:788], waits=[("sV", g["tv"])], inc="sDB")

            def T2(hg):
                g = gst[hg]
                w_ = [("sV", g["tv"])]
                if hg == 0:
                    w_.append(("sV", p.meta.get("ptr_reads", 0)))
                for h in range(hg * 4, hg * 4 + 4):
                    g["t2"] = p.emit("tensor", lambda e, dst=b.ptr[PSLB[hg] + h % 4],
                                     src=SLh(hg, h, 0): e.transpose(dst, src, b.IDN[:]),
                                     waits=w_ if h == hg * 4 else (),
                                     inc="sP" if h == hg * 4 + 3 else None)

            def T3(hg, wt2):
                g = gst[hg]
                w_ = [("sP", wt2)] if hg == 0 else []
                p.emit("vector", lambda e, dst=sv3g(hg, 0), src=PTR4(hg): e.tensor_copy(dst, src), waits=w_)
                for h in range(hg * 4, hg * 4 + 4):
                    g["sc"] = p.emit("vector", lambda e,
                                     dst=b.FM[:, hg * 512 + (h % 4) * 128:hg * 512 + (h % 4) * 128 + 128],
                                     src=SLh(hg, h, 0): e.tensor_tensor_scan(
                        dst, b.ONEF[:, 0:1].broadcast_to([128, 128]), src, 0.0, ALU.mult, ALU.add),
                        inc="sV" if h == hg * 4 + 3 else None)
                if aps.get("dbg") is not None and tc == 0 and tsq == 0 and hg == 0:
                    p.dma("gpsimd", aps["dbg"][:, 0:128], b.FM[:, 0:128], waits=[("sV", g["sc"])], inc="sDB")
                    p.dma("gpsimd", aps["dbg"][:, 128:256], b.STB[:, 1024:1152], waits=[("sV", g["sc"])], inc="sDB")

            def T4(hg):
                g = gst[hg]
                w_ = [("sV", g["sc"])]
                if aps.get("dbg") is not None:
                    w_.append(("sDB", p.now("sDB")))
                p.emit("scalar", lambda e, dst=sv3g(hg, 1), src=FM43g(hg): e.activation(dst, src, AF.Exp),
                       waits=w_)
                p.emit("scalar", lambda e, dst=sv3g(hg, 2), src=FM43g(hg): e.activation(
                    dst, src, AF.Exp, scale=-1.0))
                p.emit("scalar", lambda e, dst=EXB3(hg), src=FM43g(hg)[:, :, 127:128]: e.activation(
                    dst, src, AF.Exp))
                g["ea"] = p.emit("scalar", lambda e, dst=FM43g(hg), src=sv3g(hg, 0): e.activation(
                    dst, src, AF.Exp), inc="sA")

            def T5(hg):
                g = gst[hg]
                p.emit("vector", lambda e, a=b.FM[:, hg * 512:hg * 512 + 512]: e.tensor_scalar(
                    a, a, -1.0, 1.0, ALU.mult, ALU.add), waits=[("sA", g["ea"])])
                p.emit("vector", lambda e, dst=sv3g(hg, 3), q=qk3g(hg, QH), eb=sv3g(hg, 1): e.tensor_mul(dst, q, eb))
                p.emit("vector", lambda e, dst=sv3g(hg, 4), k=qk3g(hg, KH), s=FM43g(hg): e.tensor_mul(dst, k, s))
                p.emit("vector", lambda e, dst=sv3g(hg, 4), enb=sv3g(hg, 2): e.tensor_mul(dst, dst, enb))
                g["t5"] = p.emit("vector", lambda e, dst=sv3g(hg, 5), src=sv3g(hg, 4),
                                 ex=EXB3(hg).broadcast_to([128, 4, 128]): e.tensor_mul(dst, src, ex),
                                 inc="sV")
                if aps.get("dbg") is not None and tc == 0 and tsq == 0 and hg == 0:
                    p.dma("gpsimd", aps["dbg"][:, 256:384], b.STB[:, 2560:2688], waits=[("sV", g["t5"])], inc="sDB")
                    p.dma("gpsimd", aps["dbg"][:, 384:512], b.STB[:, 3072:3200], waits=[("sV", g["t5"])], inc="sDB")
                    p.dma("gpsimd", aps["dbg"][:, 512:640], b.STB[:, 3584:3712], waits=[("sV", g["t5"])], inc="sDB")
                    p.dma("gpsimd", aps["dbg"][:, 640:644], S[:, 840:844], waits=[("sV", g["t5"])], inc="sDB")

            def T6(hg):
                g = gst[hg]
                for h in range(hg * 4, hg * 4 + 4):
                    w_ = []
                    if h == hg * 4:
                        w_ = [("sV", g["t5"]), ("sV", gmeta.get(("mask", hg), 0))]
                    g["am"] = p.emit("tensor", lambda e, dst=APB[hg][:, (h % 4) * 128:(h % 4) * 128 + 128],
                                     kt=SLh(hg, h, 4), qt=SLh(hg, h, 3): e.matmul(
                        dst, kt, qt, start=True, stop=True, skip_group_check=True),
                        waits=w_, inc="sP" if h == hg * 4 + 3 else None)

            def T7(hg):
                g = gst[hg]
                g["t7"] = p.emit("vector", lambda e, dst=sv3g(hg, 4),
                                 src=APB[hg][:, 0:512].rearrange("p (h c) -> p h c", h=4): e.tensor_mul(
                    dst, src, b.TRIL[:].rearrange("p (o c) -> p o c", o=1).broadcast_to([128, 4, 128])),
                    waits=[("sP", g["am"])], inc="sV")
                gmeta[("mask", hg)] = g["t7"]
                if hg == 1 or GLA_ONEGROUP:
                    p.meta["gla_t7b"] = g["t7"]
                if aps.get("dbg") is not None and tc == 0 and tsq == 0 and hg == 0:
                    p.dma("gpsimd", aps["dbg"][:, 644:772], b.STB[:, 3072:3200], waits=[("sV", g["t7"])], inc="sDB")

            def T8(hg, wptr):
                g = gst[hg]
                for h in range(hg * 4, hg * 4 + 4):
                    w_ = []
                    if h == hg * 4:
                        w_ = [("sV", g["t7"]), ("sV", gmeta.get(("oev", hg), 0))]
                    p.emit("tensor", lambda e, dst=OPB[hg][:, (h % 4) * 128:(h % 4) * 128 + 128],
                           qt=SLh(hg, h, 3), s0=b.BIGB[:, S0B + h * 128:S0B + (h + 1) * 128]: e.matmul(
                        dst, qt, s0, start=True, stop=False, skip_group_check=True), waits=w_)
                    p.emit("tensor", lambda e, dst=OPB[hg][:, (h % 4) * 128:(h % 4) * 128 + 128],
                           at=SLh(hg, h, 4), v=b.BIGB[:, tsq * 1024 + h * 128:tsq * 1024 + (h + 1) * 128],
                           tsq=tsq: e.matmul(dst, at, v, start=False, stop=True, skip_group_check=True))
                for h in range(hg * 4, hg * 4 + 4):
                    w_ = [("sV", wptr)] if (h == hg * 4 and hg == 0) else []
                    g["t8p"] = p.emit("tensor", lambda e, dst=b.ptr[PSLB[hg] + h % 4],
                                      src=SLh(hg, h, 5): e.transpose(dst, src, b.IDN[:]),
                                      waits=w_, inc="sP" if h == hg * 4 + 3 else None)

            def T9(hg, wt8):
                g = gst[hg]
                g["t9"] = p.emit("vector", lambda e, dst=sv3g(hg, 5), src=PTR4(hg): e.tensor_copy(dst, src),
                                 waits=[("sP", wt8)] if hg == 0 else (), inc="sV")

            def T10(hg):
                g = gst[hg]
                for h in range(hg * 4, hg * 4 + 4):
                    w_ = []
                    if h == hg * 4:
                        w_ = [("sV", g["t9"]), ("sV", gmeta.get(("st", hg), 0))]
                    g["sm"] = p.emit("tensor", lambda e, dst=SPB[hg][:, (h % 4) * 128:(h % 4) * 128 + 128],
                                     wtt=SLh(hg, h, 5),
                                     v=b.BIGB[:, tsq * 1024 + h * 128:tsq * 1024 + (h + 1) * 128],
                                     tsq=tsq: e.matmul(dst, wtt, v, start=True, stop=True,
                                                       skip_group_check=True),
                                     waits=w_, inc="sP" if h == hg * 4 + 3 else None)

            def T11(hg):
                g = gst[hg]
                for h in range(hg * 4, hg * 4 + 4):
                    p.emit("vector", lambda e, h=h, hg=hg,
                           pbh=SPB[hg][:, (h % 4) * 128:(h % 4) * 128 + 128]: e.scalar_tensor_tensor(
                        b.S2[:, h * 128:(h + 1) * 128], b.S2[:, h * 128:(h + 1) * 128],
                        S[:, 840 + hg * 4 + (h % 4):841 + hg * 4 + (h % 4)], pbh,
                        ALU.mult, ALU.add),
                        waits=[("sP", g["sm"])] if h == hg * 4 else ())
                p.emit("vector", lambda e, hg=hg: e.tensor_copy(
                    b.BIGB[:, S0B + hg * 512:S0B + hg * 512 + 512], b.S2[:, hg * 512:hg * 512 + 512]))
                g["t11"] = p.emit("vector", lambda e, hg=hg, pb_=OPB[hg]: e.tensor_copy(
                    b.FM[:, 1024 + hg * 512:1024 + hg * 512 + 512], pb_[:, 0:512]), inc="sV")
                gmeta[("oev", hg)] = g["t11"]
                gmeta[("st", hg)] = g["t11"]

            grps = [0] if GLA_ONEGROUP else [0, 1]
            for hg in grps:
                T1(hg)
            if GLA_TSTOP != 'T1':
                for hg in grps:
                    T2(hg)
                wt2 = gst[grps[-1]]["t2"]
                for hg in grps:
                    T3(hg, wt2)
                p.meta["ptr_reads"] = p.now("sV")
                if GLA_TSTOP != 'T3':
                    for hg in grps:
                        T4(hg)
                    for hg in grps:
                        T5(hg)
                    if GLA_TSTOP != 'T5':
                        for hg in grps:
                            T6(hg)
                        for hg in grps:
                            T7(hg)
                        wptr = p.meta["ptr_reads"]
                        for hg in grps:
                            T8(hg, wptr)
                        wt8 = gst[grps[-1]]["t8p"]
                        for hg in grps:
                            T9(hg, wt8)
                        p.meta["ptr_reads"] = p.now("sV")
                        for hg in grps:
                            T10(hg)
                        for hg in grps:
                            T11(hg)
            p.meta["gla_sc_used"] = p.now("sV")
            if GLA_TSTOP is not None or GLA_ONEGROUP or GLA_G1STOP is not None:
                break
            o = b.FM[:, 1024:2048]
            S = b.S1
            p.emit("vector", lambda e: e.tensor_mul(S[:, 0:1024], o, o),
                   waits=([("sDB", p.now("sDB"))] if aps.get("dbg") is not None else ()))
            p.emit("vector", lambda e: e.reduce_sum(S[:, 264:265], S[:, 0:1024], AX.X))
            gv = p.emit("vector", lambda e: e.tensor_scalar_mul(S[:, 264:265], S[:, 264:265],
                                                                1.0 / 1024.0), inc="sV")
            gs = p.emit("scalar", lambda e: e.activation(S[:, 265:266], S[:, 264:265], AF.Sqrt,
                                                         bias=b.EPSC[:, 0:1]),
                        waits=[("sV", gv)], inc="sA")
            gr = p.emit("vector", lambda e: e.reciprocal(S[:, 264:265], S[:, 265:266]),
                        waits=[("sA", gs)], inc="sV")
            go = p.emit("vector", lambda e, tsq=tsq: e.scalar_tensor_tensor(
                b.BIGB[:, 4096 + tsq * 1024:4096 + (tsq + 1) * 1024], o, S[:, 264:265],
                b.BIGB[:, GNW:GNW + 1024], ALU.mult, ALU.mult), waits=[("sV", gr)], inc="sV")
            for oc in range(8):
                pslot = b.ptr[4 + oc % 4]
                w_ = [("sV", go)]
                if p.meta.get("ptr_reads") is not None:
                    w_.append(("sV", p.meta["ptr_reads"]))
                tr3 = p.emit("tensor", lambda e, tsq=tsq, oc=oc, pslot=pslot: e.transpose(
                    pslot, b.BIGB[:, 4096 + tsq * 1024 + oc * 128:4096 + tsq * 1024 + (oc + 1) * 128],
                    b.IDN[:]), waits=w_, inc="sP")
                of = p.emit("vector", lambda e, tsq=tsq, oc=oc, pslot=pslot: e.tensor_copy(
                    b.BIGB[:, 8192 + oc * 512 + tsq * 128:8192 + oc * 512 + (tsq + 1) * 128], pslot),
                    waits=[("sP", tr3)], inc="sV")
                p.meta["ptr_reads"] = of
        ofrdy = p.now("sV")

        for dm in range(NDC if not GLA_TRUNC else 0):
            for kc in range(NDC):
                w_ = []
                if kc == 0:
                    w_ = [("sV", ofrdy)]
                    if p.meta.get("gla_wo") is not None:
                        w_.append(("sV", p.meta["gla_wo"]))
                wm = p.emit("tensor", lambda e, dm=dm, kc=kc: e.matmul(
                    b.pb[4][:], b.WQK[:, kc * 1024 + dm * 128:kc * 1024 + (dm + 1) * 128],
                    b.BIGB[:, 8192 + kc * 512:8192 + (kc + 1) * 512],
                    start=(kc == 0), stop=(kc == NDC - 1)),
                    waits=(w_ + (wordy if (dm == 0 and kc == 0) else [])),
                    inc="sP" if kc == NDC - 1 else None)
            w_ = [("sP", wm), ("sLX", p.now("sLX"))]
            if p.meta.get(f"gla_xo{dm % 2}") is not None:
                w_.append((f"sXo{dm % 2}", p.meta[f"gla_xo{dm % 2}"]))
            ev = p.emit("vector", lambda e, dm=dm: e.tensor_add(
                b.FM[:, dm % 2 * 512:dm % 2 * 512 + 512], b.pb[4][:],
                b.XC[:, dm * TC:(dm + 1) * TC]), waits=w_, inc="sV")
            p.meta["gla_wo"] = ev
            st = p.dma("gpsimd", aps["xout"][dm * 128:(dm + 1) * 128, tc * TC:(tc + 1) * TC],
                       b.FM[:, dm % 2 * 512:dm % 2 * 512 + 512], waits=[("sV", ev)],
                       inc=f"sXo{dm % 2}")
            p.meta[f"gla_xo{dm % 2}"] = st
        p.meta["pbc_free"] = [("sV", p.now("sV"))]
    return [(f"sXo{j}", p.now(f"sXo{j}")) for j in range(2)]


import ml_dtypes
from concourse.bass_utils import run_bass_kernel_spmd

BFNP = ml_dtypes.bfloat16
T_FULL = 4096


def pack_cols(v):
    return np.ascontiguousarray(v.reshape(-1, 128).T)


def make_rope_tables(T):
    dk = 64
    pos = np.arange(T, dtype=np.float64)
    kidx = np.arange(dk // 2, dtype=np.float64)
    ang = pos[:, None] / 10000.0 ** (2.0 * kidx / dk)
    cosT = np.zeros((128, T), np.float32)
    sinT = np.zeros((128, T), np.float32)
    for r in range(128):
        i = (r % 64) // 2
        cosT[r] = np.cos(ang[:, i])
        sinT[r] = (-1 if r % 2 == 0 else 1) * np.sin(ang[:, i])
    return cosT, sinT


def make_pswap():
    P = np.zeros((128, 128), np.float32)
    for i in range(64):
        P[2 * i, 2 * i + 1] = 1.0
        P[2 * i + 1, 2 * i] = 1.0
    return P.astype(BFNP)


def host_consts():
    return {
        "ident": np.eye(128, dtype=BFNP),
        "identf": np.eye(128, dtype=np.float32),
        "tril": np.tril(np.ones((128, 128), np.float32)).T.astype(BFNP),
    }


def host_gla(inp):
    g = lambda n: inp["L0_attn_a_" + n]
    return {
        "Wq": np.ascontiguousarray(g("Wq")), "Wk": np.ascontiguousarray(g("Wk")),
        "Wv": np.ascontiguousarray(g("Wv")), "Wr": np.ascontiguousarray(g("Wr")),
        "Wa": np.ascontiguousarray(g("Wa")), "Wo": np.ascontiguousarray(g("Wo")),
        "alog_r": np.ascontiguousarray(g("A_log").reshape(1, 8)),
        "dtb_r": np.ascontiguousarray(g("dt_bias").reshape(1, 8)),
        "qnw_c": np.ascontiguousarray(g("q_norm_w").reshape(128, 1)),
        "knw_c": np.ascontiguousarray(g("k_norm_w").reshape(128, 1)),
        "gnw_b": np.ascontiguousarray(np.broadcast_to(g("g_norm_w"), (128, 1024))),
    }


def host_ffn(inp):
    w1 = inp["L0_ffn_w1"]
    w2 = inp["L0_ffn_w2"]
    return {
        "w1T": np.ascontiguousarray(w1.T), "w1": np.ascontiguousarray(w1),
        "w2T": np.ascontiguousarray(w2.T), "w2": np.ascontiguousarray(w2),
        "b1c": pack_cols(inp["L0_ffn_b1"]), "b2c": pack_cols(inp["L0_ffn_b2"]),
    }


def host_attnb(inp):
    cols = np.zeros((128, 144), np.float32)
    cols[:, 0:8] = pack_cols(inp["L0_attn_b_a"])
    cols[:, 8:16] = pack_cols(inp["L0_attn_b_b"])
    cols[:, 16:24] = pack_cols(inp["L0_attn_b_conv_b"])
    cw = inp["L0_attn_b_conv_w"].reshape(1024, 3)
    for j in range(3):
        cols[:, 24 + j * 8:24 + (j + 1) * 8] = pack_cols(np.ascontiguousarray(cw[:, j]))
    cols[:, 48:56] = pack_cols(inp["L0_attn_b_norm_w"])
    cols[:, 56:64] = pack_cols(inp["L0_attn_b_mlp_norm_w"])
    cols[:, 64:96] = pack_cols(inp["L0_attn_b_b_gate"])
    cols[:, 96:128] = pack_cols(inp["L0_attn_b_b_up"])
    cols[:, 128:136] = pack_cols(inp["L0_attn_b_b_down"])
    return {
        "cols": cols,
        "w_gate": np.ascontiguousarray(inp["L0_attn_b_w_gate"]),
        "w_up": np.ascontiguousarray(inp["L0_attn_b_w_up"]),
        "w_down": np.ascontiguousarray(inp["L0_attn_b_w_down"]),
    }


def host_attna(inp):
    cosT, sinT = make_rope_tables(T_FULL)
    return {
        "wqT": np.ascontiguousarray(inp["L1_attn_a_Wq"].T),
        "wkT": np.ascontiguousarray(inp["L1_attn_a_Wk"].T),
        "wvT": np.ascontiguousarray(inp["L1_attn_a_Wv"].T),
        "woT": np.ascontiguousarray(inp["L1_attn_a_Wo"].T),
        "bqc": pack_cols(inp["L1_attn_a_bq"]),
        "bkc": pack_cols(inp["L1_attn_a_bk"]),
        "boc": pack_cols(inp["L1_attn_a_bo"]),
        "bvb": np.ascontiguousarray(np.broadcast_to(inp["L1_attn_a_bv"], (128, 1024))),
        "cosT": cosT, "sinT": sinT, "pswap": make_pswap(),
    }


def host_attnb2(inp):
    return {
        "wqkvT": np.ascontiguousarray(inp["L1_attn_b_w_qkv"].T),
        "woT": np.ascontiguousarray(inp["L1_attn_b_w_proj"].T),
    }


def host_moe(inp):
    return {
        "gate_w": np.ascontiguousarray(inp["L1_ffn_gate_w"]),
        "w1": np.ascontiguousarray(inp["L1_ffn_w1"]),
        "w2": np.ascontiguousarray(inp["L1_ffn_w2"]),
    }



def _declare(nc, host, T):
    aps = {}
    for name, arr in host.items():
        dt = BF16 if arr.dtype == BFNP else F32
        aps[name] = nc.dram_tensor(name, list(arr.shape), dt, kind="ExternalInput").ap()
    aps["xin"] = nc.dram_tensor("xin", [1024, T], F32, kind="ExternalInput").ap()
    aps["xout"] = nc.dram_tensor("xout", [1024, T], F32, kind="ExternalOutput").ap()
    return aps


def build_sub(kind, host, T):
    nc = bass.Bass("TRN2", target_bir_lowering=False)
    stack = ExitStack()
    stack.__enter__()
    p = Prog(nc, stack, T=T)
    b = Bufs(nc, stack)
    aps = _declare(nc, host, T)
    if kind == "gla":
        pass
    if kind == "ffn":
        aps["hd"] = nc.dram_tensor("hd", [4096, T], BF16, kind="Internal").ap()
    if kind == "attnb":
        aps["ud"] = nc.dram_tensor("ud", [1024, T], BF16, kind="Internal").ap()
        aps["hd"] = nc.dram_tensor("hd", [4096, T], BF16, kind="Internal").ap()
    if kind in ("attna", "attnb2"):
        aps["qd"] = nc.dram_tensor("qd", [1024, T], BF16, kind="Internal").ap()
        aps["xhd"] = nc.dram_tensor("xhd", [1024, T], BF16, kind="Internal").ap()
        if kind == "attnb2":
            aps["wqT"] = aps["wqkvT"][:, 0:1024]
            aps["wkT"] = aps["wqkvT"][:, 1024:2048]
            aps["wvT"] = aps["wqkvT"][:, 2048:3072]
    if kind == "moe":
        aps["accd"] = nc.dram_tensor("accd", [T, 1024], F32, kind="Internal").ap()
    emit_consts(p, b, aps)
    gate = p.meta["consts"]
    if kind == "gla":
        done = emit_gla(p, b, aps, gate=gate)
    elif kind == "ffn":
        done = emit_ffn_bin(p, b, aps, gate=gate)
    elif kind == "attnb":
        done = emit_attnb(p, b, aps, gate=gate)
    elif kind == "attna":
        done = emit_attn(p, b, aps, True, gate=gate)
    elif kind == "attnb2":
        done = emit_attn(p, b, aps, False, gate=gate)
    elif kind == "moe":
        done = emit_moe(p, b, aps, gate=gate)
    p.emit("sync", lambda e: e.nop(), waits=done)
    block = stack.enter_context(nc.Block())
    p.finalize(block)
    stack.__exit__(None, None, None)
    return nc



CHAIN = [
    ("gla", host_gla), ("ffn", host_ffn), ("attnb", host_attnb), ("ffn", host_ffn),
    ("attna", host_attna), ("moe", host_moe), ("attnb2", host_attnb2), ("moe", host_moe),
]


def build_fused(hosts, T, self_edges=False):
    nc = bass.Bass("TRN2", target_bir_lowering=False)
    stack = ExitStack()
    stack.__enter__()
    p = Prog(nc, stack, T=T)
    b = Bufs(nc, stack)
    aps_all = {}
    for kind, h in hosts.items():
        for name, arr in h.items():
            key = f"{kind}_{name}"
            dt = BF16 if arr.dtype == BFNP else F32
            aps_all[key] = nc.dram_tensor(key, list(arr.shape), dt, kind="ExternalInput").ap()
    cn = {"ident": np.eye(128), "identf": 0, "tril": 0}
    capz = {
        "ident": nc.dram_tensor("ident", [128, 128], BF16, kind="ExternalInput").ap(),
        "identf": nc.dram_tensor("identf", [128, 128], F32, kind="ExternalInput").ap(),
        "tril": nc.dram_tensor("tril", [128, 128], BF16, kind="ExternalInput").ap(),
    }
    xin = nc.dram_tensor("xin", [1024, T], F32, kind="ExternalInput").ap()
    xout = nc.dram_tensor("xout", [1024, T], F32, kind="ExternalOutput").ap()
    xd = [nc.dram_tensor(f"xd{i}", [1024, T], F32, kind="Internal").ap() for i in range(2)]
    scr = {
        "hd": nc.dram_tensor("hd", [4096, T], BF16, kind="Internal").ap(),
        "ud": nc.dram_tensor("ud", [1024, T], BF16, kind="Internal").ap(),
        "qd": nc.dram_tensor("qd", [1024, T], BF16, kind="Internal").ap(),
        "xhd": nc.dram_tensor("xhd", [1024, T], BF16, kind="Internal").ap(),
        "accd": nc.dram_tensor("accd", [T, 1024], F32, kind="Internal").ap(),
    }
    emit_consts(p, b, capz)
    done = p.meta["consts"]
    nsub = len(CHAIN)
    for i, (kind, _) in enumerate(CHAIN):
        aps = {name: aps_all[f"{kind}_{name}"] for name in hosts[kind]}
        aps.update(scr)
        if kind == "attnb2":
            aps["wqT"] = aps["wqkvT"][:, 0:1024]
            aps["wkT"] = aps["wqkvT"][:, 1024:2048]
            aps["wvT"] = aps["wqkvT"][:, 2048:3072]
        aps["xin"] = xin if i == 0 else xd[(i + 1) % 2]
        aps["xout"] = xout if i == nsub - 1 else xd[i % 2]
        p.barrier()
        if kind == "gla":
            done = emit_gla(p, b, aps)
        elif kind == "ffn":
            done = emit_ffn_bin(p, b, aps)
        elif kind == "attnb":
            done = emit_attnb(p, b, aps)
        elif kind == "attna":
            done = emit_attn(p, b, aps, True)
        elif kind == "attnb2":
            done = emit_attn(p, b, aps, False)
        elif kind == "moe":
            done = emit_moe(p, b, aps)
    p.emit("sync", lambda e: e.nop(), waits=done)
    block = stack.enter_context(nc.Block())
    p.finalize(block, self_edges=self_edges)
    stack.__exit__(None, None, None)
    return nc


def kernel(**inputs):
    T = T_FULL
    x = np.ascontiguousarray(
        np.asarray(inputs["L0_attn_a_hidden_states"], dtype=np.float32).transpose(0, 2, 1))
    consts = host_consts()
    hosts = {}
    for kind, hostf in CHAIN:
        if kind not in hosts:
            hosts[kind] = dict(hostf(inputs))
    nc = build_fused(hosts, T)
    flat = dict(consts)
    for kind, h in hosts.items():
        for name, arr in h.items():
            flat[f"{kind}_{name}"] = arr
    in_maps = [dict(flat, xin=np.ascontiguousarray(x[c])) for c in range(8)]
    res = run_bass_kernel_spmd(nc, in_maps, core_ids=list(range(8)))
    out = np.stack([np.asarray(res.results[c]["xout"], dtype=np.float32) for c in range(8)])
    return np.ascontiguousarray(out.transpose(0, 2, 1))
